# version 14
# baseline (speedup 1.0000x reference)
import numpy as np
import ml_dtypes
import concourse.bass as bass
import concourse.mybir as mybir
from concourse.bass_utils import run_bass_kernel_spmd
from contextlib import ExitStack

F32 = mybir.dt.float32
BF16 = mybir.dt.bfloat16
AF = mybir.ActivationFunctionType
AX = mybir.AxisListType
ALU = mybir.AluOpType

NCORES = 8
S = 8192
D = 2048
HFF = 8192
NH = 16
HD = 128
TOK = 1024
WIN = 1536
SCALING = float(HD) ** -0.5
C_SOFT = 25.0
FWHT_SCALE = float(HFF) ** -0.5
BF = np.dtype(ml_dtypes.bfloat16)
OWN0 = 256


def _had(n):
    h = np.array([[1.0]], dtype=np.float32)
    while h.shape[0] < n:
        h = np.block([[h, h], [h, -h]])
    return h


def _masks_for_core(c):
    s = c * TOK
    ws = s - 256
    out = np.zeros((8, 5, 128, 128), dtype=np.float32)
    i = np.arange(128)
    for qt in range(8):
        qg = s + qt * 128 + i[None, :]
        for slot, kc in ((0, 0), (1, 1), (2, 3), (3, 4)):
            kg = ws + qt * 128 + kc * 128 + i[:, None]
            valid = (kg >= 0) & (kg < S)
            out[qt, slot] = (valid & ((np.abs(qg - kg) < 256) | (kg < 16))).astype(np.float32)
        kg = i[:16, None]
        in_window = (kg >= ws + qt * 128) & (kg < ws + qt * 128 + 640)
        out[qt, 4, :16, :] = (~in_window).astype(np.float32)
    return out.astype(BF)


def _prep_inputs(x, Wq, Wk, Wv, Wo, W1, W2, W3):
    x = np.asarray(x, dtype=np.float32).reshape(S, D)
    Wq, Wk, Wv, Wo = (np.asarray(w, np.float32) for w in (Wq, Wk, Wv, Wo))
    W1, W2, W3 = (np.asarray(w, np.float32) for w in (W1, W2, W3))

    def qk_layout(w):
        a = w.reshape(16, 128, 16, 128).transpose(2, 1, 0, 3)
        a = a.reshape(4, 4, 128, 16, 128).transpose(0, 2, 1, 3, 4)
        return np.ascontiguousarray(a.reshape(4, 128, 8192))

    wq_s = qk_layout(Wq)
    wk_s = qk_layout(Wk)
    a = Wv.reshape(16, 128, 4, 4, 128).transpose(2, 1, 0, 3, 4)
    wv_s = np.ascontiguousarray(a.reshape(4, 128, 8192))
    a = Wo.reshape(16, 128, 16, 128).transpose(2, 1, 0, 3)
    wo_s = np.ascontiguousarray(a.reshape(16, 128, 2048))

    def w13_layout(w):
        a = w.reshape(64, 128, 16, 128).transpose(0, 3, 2, 1)
        return np.ascontiguousarray(a.reshape(64, 128, 2048))

    w1_s = w13_layout(W1)
    w3_s = w13_layout(W3)
    a = W2.reshape(16, 128, 64, 128).transpose(0, 3, 2, 1)
    w2_s = np.ascontiguousarray(a.reshape(16, 128, 8192))

    common = {
        "wq_s": wq_s, "wk_s": wk_s, "wv_s": wv_s, "wo_s": wo_s,
        "w1_s": w1_s, "w3_s": w3_s, "w2_s": w2_s,
        "h128": _had(128).astype(BF),
        "id_b": np.eye(128, dtype=BF),
        "id_f": np.eye(128, dtype=np.float32),
        "ones_f": np.ones((128, 1), dtype=BF),
        "ones1_b": np.ones((1, 128), dtype=BF),
        "x_glob": np.ascontiguousarray(x[0:16]),
    }
    maps = []
    for c in range(NCORES):
        s = c * TOK
        xkv = np.zeros((WIN, D), np.float32)
        lo, hi = max(0, s - 256), min(S, s + TOK + 256)
        xkv[lo - (s - 256): hi - (s - 256)] = x[lo:hi]
        maps.append(dict(common, x_kv=xkv, masks=_masks_for_core(c)))
    return maps



class P:
    def __init__(self, role, eng, sems):
        self.role = role
        self.eng = eng
        self.sems = sems
        self.cnt = {n: 0 for n in sems}

    def do(self, role, fn, inc=None, inc_n=1):
        if inc is not None:
            self.cnt[inc] += inc_n
        if self.role == role:
            ins = fn(self.eng)
            if inc is not None:
                ins.then_inc(self.sems[inc], inc_n)
        return self.cnt[inc] if inc is not None else None

    def wait(self, role, sem, val):
        if val > 0 and self.role == role:
            self.eng.wait_ge(self.sems[sem], val)

    def val(self, sem):
        return self.cnt[sem]


BASE = 20480
LIMIT = 229376

def _mk_map2():
    m = {}
    off = BASE

    def add(name, nbytes):
        nonlocal off
        o = (off + 31) // 32 * 32
        m[name] = o
        off = o + nbytes
        return o

    for n, b in (("h128", 256), ("idb", 256), ("idf", 512), ("ones", 32),
                 ("ones1", 256), ("y_l", 1024), ("dlt", 1024), ("o_n", 512),
                 ("ln", 64), ("mg", 64), ("sg", 64), ("og", 1024),
                 ("outs", 1024), ("fsb", 8192), ("dltTc", 1024), ("fixc", 1024),
                 ("xnT", 32768)):
        add(n, b)
    ov = (off + 31) // 32 * 32
    m["hid"] = ov
    pf_tail = ov + 131072
    m["w1b"], m["w3b"] = pf_tail, pf_tail + 8192
    m["sil"], m["h0"] = pf_tail + 16384, pf_tail + 24576
    assert pf_tail + 28672 <= LIMIT, f"PF overflow: {pf_tail + 28672}"

    o = [ov]
    def pa(name, nbytes):
        oo = (o[0] + 31) // 32 * 32
        m[name] = oo
        o[0] = oo + nbytes
        assert o[0] <= ov + 131072, (name, o[0] - ov)
        return oo
    pa("attnT", 32768)
    pa("xnTh", 16896)
    pa("maskb", 10240)
    pa("k4t", 12416)
    pa("v4", 13312)
    pa("q4t", 8192)
    pa("qg4t", 128)
    pa("wbuf", 32768)
    pa("e_sb", 3072)
    pa("rb", 1024)
    pa("egt", 256)
    m["xt"] = m["k4t"]
    m["sq"] = m["k4t"] + 16384
    m["xn"] = m["k4t"] + 24576
    m["wo_sb"] = m["wbuf"]
    m["y_sb"] = m["wbuf"] + 8192
    m["mfl"] = m["k4t"]
    m["sfl"] = m["k4t"] + 8192
    m["oga"] = m["k4t"] + 16384
    m["tmg"] = m["k4t"] + 24576
    m["wb2"] = m["wbuf"] + 16384
    m["wbf"] = m["wbuf"] + 24576
    m["w2b"] = m["xnT"]
    m["wo2"] = m["w1b"]
    m["y2"] = m["w3b"]
    m["xrs"] = m["sil"]
    return m

MM = _mk_map2()


def build_program():
    nc = bass.Bass("TRN2", target_bir_lowering=False, detect_race_conditions=False)

    x_kv = nc.dram_tensor("x_kv", [WIN, D], F32, kind="ExternalInput").ap()
    x_glob = nc.dram_tensor("x_glob", [16, D], F32, kind="ExternalInput").ap()
    wq_d = nc.dram_tensor("wq_s", [4, 128, 8192], F32, kind="ExternalInput").ap()
    wk_d = nc.dram_tensor("wk_s", [4, 128, 8192], F32, kind="ExternalInput").ap()
    wv_d = nc.dram_tensor("wv_s", [4, 128, 8192], F32, kind="ExternalInput").ap()
    wo_d = nc.dram_tensor("wo_s", [16, 128, 2048], F32, kind="ExternalInput").ap()
    w1_d = nc.dram_tensor("w1_s", [64, 128, 2048], F32, kind="ExternalInput").ap()
    w3_d = nc.dram_tensor("w3_s", [64, 128, 2048], F32, kind="ExternalInput").ap()
    w2_d = nc.dram_tensor("w2_s", [16, 128, 8192], F32, kind="ExternalInput").ap()
    h128_d = nc.dram_tensor("h128", [128, 128], BF16, kind="ExternalInput").ap()
    idb_d = nc.dram_tensor("id_b", [128, 128], BF16, kind="ExternalInput").ap()
    idf_d = nc.dram_tensor("id_f", [128, 128], F32, kind="ExternalInput").ap()
    ones_d = nc.dram_tensor("ones_f", [128, 1], BF16, kind="ExternalInput").ap()
    ones1_d = nc.dram_tensor("ones1_b", [1, 128], BF16, kind="ExternalInput").ap()
    mask_d = nc.dram_tensor("masks", [8, 5, 128, 128], BF16, kind="ExternalInput").ap()

    o_main = nc.dram_tensor("o_main", [TOK, D], F32, kind="ExternalOutput").ap()
    o_fix = nc.dram_tensor("o_fix", [16, D], F32, kind="ExternalOutput").ap()

    y_scr = nc.dram_tensor("y_scr", [D, TOK], F32, kind="Internal").ap()
    w2bf_d = nc.dram_tensor("w2bf", [16, 128, 8192], BF16, kind="Internal").ap()
    mrow_d = nc.dram_tensor("mrow", [2048], F32, kind="Internal").ap()
    wobf_d = nc.dram_tensor("wobf", [16, 128, 2048], BF16, kind="Internal").ap()
    part_src = nc.dram_tensor("part_src", [130, 256], F32, kind="Internal")
    ag_out = nc.dram_tensor("ag_out", [1040, 256], F32, kind="Internal",
                            addr_space="Shared")

    at = nc.alloc_sbuf_tensor_at
    h128 = at("h128b", [128, 128], BF16, offset=MM["h128"])
    idb = at("idb", [128, 128], BF16, offset=MM["idb"])
    idf = at("idf", [128, 128], F32, offset=MM["idf"])
    ones = at("ones", [128, 1], BF16, offset=MM["ones"])
    ones1 = at("ones1", [1, 128], BF16, offset=MM["ones1"])
    y_l = at("y_l", [128, 256], F32, offset=MM["y_l"])
    dlt = at("dlt", [128, 256], F32, offset=MM["dlt"])
    o_n = at("o_n", [128, 256], BF16, offset=MM["o_n"])
    ln = at("ln", [128, 16], F32, offset=MM["ln"])
    mg = at("mg", [16, 16], F32, offset=MM["mg"])
    sg = at("sg", [16, 16], F32, offset=MM["sg"])
    og = at("og", [128, 256], F32, offset=MM["og"])
    outs = [at(f"outs{i}", [128, 128], F32, offset=MM["outs"] + 512 * i) for i in range(2)] + \
           [at(f"outs{i+2}", [128, 128], F32, offset=MM["og"] + 512 * i) for i in range(2)]
    fsb = [at(f"fsb{i}", [128, 1024], F32, offset=MM["fsb"] + 4096 * i) for i in range(2)]
    dltTc = [at(f"dltTc{i}", [16, 128], F32, offset=MM["dltTc"] + 512 * i) for i in range(2)]
    rtb = [at(f"rtb{i}", [1, 128], BF16, offset=MM["dltTc"] + 256 * i) for i in range(2)]
    fixc = [at(f"fixc{i}", [16, 128], F32, offset=MM["fixc"] + 512 * i) for i in range(2)]
    xnT = at("xnT", [128, 16 * 1024], BF16, offset=MM["xnT"])
    xnTh = at("xnTh", [128, 16 * 528], BF16, offset=MM["xnTh"])

    hid = at("hid", [128, 64 * 1024], BF16, offset=MM["hid"])
    w1b = [at(f"w1b{i}", [128, 2048], BF16, offset=MM["w1b"] + 4096 * i) for i in range(2)]
    bounce = [at(f"bounce{i}", [128, 4096], BF16,
                 offset=(MM["w1b"] if i == 0 else MM["sil"])) for i in range(2)]
    w3b = [at(f"w3b{i}", [128, 2048], BF16, offset=MM["w3b"] + 4096 * i) for i in range(2)]
    sil = [at(f"sil{i}", [128, 1024], F32, offset=MM["sil"] + 4096 * i) for i in range(2)]
    h0 = [at(f"h0{i}", [128, 1024], BF16, offset=MM["h0"] + 2048 * i) for i in range(2)]

    attnT = at("attnT", [128, NH * TOK], BF16, offset=MM["attnT"])
    maskb = at("maskb", [128, 8 * 5 * 128], BF16, offset=MM["maskb"])
    k4t = at("k4t", [128, 4 * 1552], BF16, offset=MM["k4t"])
    v4 = at("v4", [128, 13 * 512], BF16, offset=MM["v4"])
    q4t = at("q4t", [128, 4 * 1024], BF16, offset=MM["q4t"])
    qg4t = at("qg4t", [128, 64], BF16, offset=MM["qg4t"])
    wbuf = [at(f"wbuf{i}", [128, 8192], BF16, offset=MM["wbuf"] + 16384 * i) for i in range(2)]
    e_sb = [at(f"e{i}", [128, 768], BF16, offset=MM["e_sb"] + 1536 * i) for i in range(2)]
    rb_sb = [at(f"rb{i}", [128, 128], F32, offset=MM["rb"] + 512 * i) for i in range(2)]
    rs_sb = [ln[:, 13:14], ln[:, 14:15]]
    egt = at("egt", [128, 128], BF16, offset=MM["egt"])

    xt = [at(f"xt{i}", [128, 2048], F32, offset=MM["xt"] + 8192 * i) for i in range(2)]
    sq = at("sq", [128, 2048], F32, offset=MM["sq"])
    xn = [at(f"xn{i}", [128, 2048], BF16, offset=MM["xn"] + 4096 * i) for i in range(2)]

    wo_sb = [at(f"wo{i}", [128, 2048], BF16, offset=MM["wo_sb"] + 4096 * i) for i in range(2)]
    y_sb = [at(f"ysb{i}", [128, 1024], F32, offset=MM["y_sb"] + 4096 * i) for i in range(2)]

    mfl = at("mfl", [1, 2048], F32, offset=MM["mfl"])
    sfl = at("sfl", [1, 2048], F32, offset=MM["sfl"])
    oga = at("oga", [128, 2048], F32, offset=MM["oga"])
    tmg = at("tmg", [128, 2048], F32, offset=MM["tmg"])
    wb2 = at("wb2", [128, 2048], F32, offset=MM["wb2"])
    wbf = at("wbf", [1, 2048], BF16, offset=MM["wbf"])

    w2b = [at(f"w2b{i}", [128, 8192], BF16, offset=MM["w2b"] + 16384 * i) for i in range(2)]
    wo2 = [at(f"wo2{i}", [128, 2048], BF16, offset=MM["wo2"] + 4096 * i) for i in range(2)]
    y2 = [at(f"y2{i}", [128, 1024], F32, offset=MM["y2"] + 4096 * i) for i in range(2)]
    xrs = [at(f"xrs{i}", [128, 1024], F32, offset=MM["xrs"] + 4096 * i) for i in range(2)]

    pA = nc.alloc_psum_tensor("pA", [128, 1024], F32)
    pB = nc.alloc_psum_tensor("pB", [128, 1024], F32)
    pC = nc.alloc_psum_tensor("pC", [128, 1024], F32)
    pD = nc.alloc_psum_tensor("pD", [128, 1024], F32)
    PP = [(pA, pB), (pC, pD)]

    def xcol(dc, kl):
        if 256 <= kl < 1280:
            return xnT, dc * 1024 + (kl - 256)
        if kl < 256:
            return xnTh, dc * 528 + kl
        if kl < 1536:
            return xnTh, dc * 528 + 256 + (kl - 1280)
        return xnTh, dc * 528 + 512 + (kl - 1536)

    K_SPANS = [(0, 256), (256, 256), (512, 256), (768, 256), (1024, 256), (1280, 256), (1536, 16)]

    with ExitStack() as ctx:
        sem_names = ["LD", "WL", "PE", "SC", "DV", "ST", "CC", "YD", "PS", "PC", "S2"]
        sems = {n: ctx.enter_context(nc.semaphore(n)) for n in sem_names}
        block = ctx.enter_context(nc.Block())

        def build(p: P):
            for dst, src in ((h128, h128_d), (idb, idb_d), (idf, idf_d),
                             (ones, ones_d), (ones1, ones1_d)):
                p.do("sync", lambda e, d=dst, s_=src: e.dma_start(out=d[:, :], in_=s_),
                     inc="LD", inc_n=16)
            p.do("sync", lambda e: e.dma_start(
                out=maskb[:, :].rearrange("p (qt s f) -> p qt s f", qt=8, s=5),
                in_=mask_d.rearrange("qt s p f -> p qt s f")), inc="LD", inc_n=16)
            pre_ld = p.val("LD")
            p.do("vector", lambda e: e.memset(ln[:, 15:16], -C_SOFT), inc="DV")
            dv_cst = p.val("DV")

            ln_dv = {}
            evac_sc = {}
            tp_pe = {}
            for t in range(13):
                rows = 16 if t == 12 else 128
                buf = xt[t % 2]
                if t >= 2:
                    p.wait("sync", "DV", ln_dv[t - 2])
                src = x_glob if t == 12 else x_kv[128 * t:128 * t + 128, :]
                p.do("sync", lambda e, b=buf, s_=src, rws=rows:
                     e.dma_start(out=b[0:rws, :], in_=s_), inc="LD", inc_n=16)
                ld_now = p.val("LD")

                p.wait("vector", "LD", ld_now)
                p.do("vector", lambda e, b=buf, rws=rows: e.reduce_sum(
                    ln[0:rws, 0:1], b[0:rws, :], axis=AX.X, negate=True), inc="DV")
                p.wait("vector", "DV", p.val("DV"))
                p.wait("scalar", "LD", ld_now)
                p.do("scalar", lambda e, b=buf, rws=rows: e.activation(
                    sq[0:rws, :], b[0:rws, :], AF.Square, accum_out=ln[0:rws, 1:2]), inc="SC")
                sc_sq = p.val("SC")
                p.do("vector", lambda e, rws=rows: e.tensor_scalar_mul(
                    ln[0:rws, 2:3], ln[0:rws, 0:1], 1.0 / D), inc="DV")
                p.wait("vector", "DV", p.val("DV"))
                p.wait("vector", "SC", sc_sq)
                p.do("vector", lambda e, rws=rows: e.tensor_scalar(
                    ln[0:rws, 3:4], ln[0:rws, 1:2], 1.0 / D, 1e-5,
                    ALU.mult, ALU.add), inc="DV")
                p.wait("vector", "DV", p.val("DV"))
                p.do("vector", lambda e, rws=rows: e.tensor_mul(
                    ln[0:rws, 4:5], ln[0:rws, 2:3], ln[0:rws, 2:3]), inc="DV")
                p.wait("vector", "DV", p.val("DV"))
                p.do("vector", lambda e, rws=rows: e.tensor_sub(
                    ln[0:rws, 5:6], ln[0:rws, 3:4], ln[0:rws, 4:5]), inc="DV")
                dv_var = p.val("DV")
                p.wait("scalar", "DV", dv_var)
                p.do("scalar", lambda e, rws=rows: e.activation(
                    ln[0:rws, 6:7], ln[0:rws, 5:6], AF.Ln), inc="SC")
                p.wait("scalar", "SC", p.val("SC"))
                p.do("scalar", lambda e, rws=rows: e.activation(
                    ln[0:rws, 7:8], ln[0:rws, 6:7], AF.Exp, scale=-0.5), inc="SC")
                sc_rstd = p.val("SC")
                p.wait("vector", "SC", sc_rstd)
                p.do("vector", lambda e, rws=rows: e.tensor_scalar_mul(
                    ln[0:rws, 8:9], ln[0:rws, 2:3], ln[0:rws, 7:8]), inc="DV")
                p.wait("vector", "DV", p.val("DV"))
                if t >= 2:
                    p.wait("vector", "PE", tp_pe[t - 2])
                xnb = xn[t % 2]
                p.do("vector", lambda e, b=buf, xb=xnb, rws=rows: e.tensor_scalar(
                    xb[0:rws, :], b[0:rws, :], ln[0:rws, 7:8], ln[0:rws, 8:9],
                    ALU.mult, ALU.add), inc="DV")
                ln_dv[t] = p.val("DV")

                p.wait("tensor", "LD", pre_ld)
                p.wait("tensor", "DV", ln_dv[t])
                ps = PP[t % 2][0]
                for half in range(2):
                    if half == 1:
                        p.wait("tensor", "SC", evac_sc[(t, 0)])
                    elif t >= 2:
                        p.wait("tensor", "SC", evac_sc[(t - 2, 1)])
                    for j in range(8):
                        dc = half * 8 + j
                        dst = ps[:, 64 * j:64 * j + 64].bitcast(BF16)
                        p.do("tensor", lambda e, d=dst, xb=xnb, dcc=dc, rws=rows:
                             e.transpose(d[0:128, 0:rws],
                                         xb[0:rws, 128 * dcc:128 * dcc + 128],
                                         idb[0:rws, 0:rws]),
                             inc="PE" if j == 7 else None)
                    pe_now = p.val("PE")
                    if half == 1:
                        tp_pe[t] = pe_now
                    p.wait("scalar", "PE", pe_now)
                    src_ap = ps[:, 0:512].bitcast(BF16).rearrange("p (j k) -> p j k", j=8)
                    kl = 128 * t if t < 12 else 1536
                    ten0, col0 = xcol(0, kl)
                    stride = 1024 if ten0 is xnT else 528
                    dst_ap = ten0[:, :].rearrange("p (dc k) -> p dc k", dc=16)[
                        :, 8 * half:8 * half + 8, col0:col0 + rows]
                    p.do("scalar", lambda e, sp=src_ap, d=dst_ap, rws=rows:
                         e.copy(d, sp[:, :, 0:rws]), inc="SC")
                    evac_sc[(t, half)] = p.val("SC")
            p0_dv = p.val("DV")

            wl_done = {}
            sec_pe = {}
            nload = [0]

            def wload(src_ap, pe_release):
                i = nload[0]
                nload[0] += 1
                if i >= 2:
                    p.wait("gpsimd", "PE", sec_pe.get(i - 2, 0))
                elif pe_release:
                    p.wait("gpsimd", "PE", pe_release)
                p.do("gpsimd", lambda e, d=wbuf[i % 2], s_=src_ap:
                     e.dma_start(out=d[:, :], in_=s_), inc="WL", inc_n=16)
                wl_done[i] = p.val("WL")
                return i

            iter_dv = {}
            iter_sc = {}
            it = 0
            pieces = []
            pc_ready_wo = {}
            pc_ready_w2 = {}
            for i in range(8):
                pieces.append((wo_d[2 * i:2 * i + 2].rearrange("a p f -> p a f"),
                               wobf_d[2 * i:2 * i + 2].rearrange("a p f -> p a f"), True))
                pc_ready_wo[2 * i] = pc_ready_wo[2 * i + 1] = 16 * len(pieces)
                for dcp in (2 * i, 2 * i + 1):
                    for hp in range(2):
                        pieces.append((w2_d[dcp][:, 4096 * hp:4096 * hp + 4096],
                                       w2bf_d[dcp][:, 4096 * hp:4096 * hp + 4096], False))
                    pc_ready_w2[dcp] = 16 * len(pieces)
            pc_k = [0]

            def emit_precast(n):
                for _ in range(n):
                    k = pc_k[0]
                    if k >= len(pieces):
                        return
                    pc_k[0] += 1
                    srcap, dstap, three_d = pieces[k]
                    bb = bounce[k % 2]
                    bview = (bb[:, :].rearrange("p (a f) -> p a f", a=2)
                             if three_d else bb[:, :])
                    if k >= 2:
                        p.wait("gpsimd", "PC", 16 * (k - 1))
                    p.do("gpsimd", lambda e, d=bview, s_=srcap: e.dma_start(out=d, in_=s_),
                         inc="WL", inc_n=16)
                    wlv = p.val("WL")
                    p.wait("sync", "WL", wlv)
                    p.do("sync", lambda e, d=dstap, s_=bview: e.dma_start(out=d, in_=s_),
                         inc="PC", inc_n=16)

            for g in range(4):
                iq = wload(wq_d[g], 0)
                wq = wbuf[iq % 2]
                p.wait("tensor", "WL", wl_done[iq])
                p.wait("tensor", "SC", p.val("SC"))
                qev = {}
                for dqc in range(4):
                    lo, hi_ = PP[dqc % 2]
                    p.wait("tensor", "SC", qev.get(dqc - 2, 0))
                    for th in range(2):
                        for dc in range(16):
                            p.do("tensor", lambda e, o_=lo[:, 512 * th:512 * th + 512],
                                 l=wq[:, 2048 * dqc + 128 * dc:2048 * dqc + 128 * dc + 128],
                                 rr=xnT[:, 1024 * dc + 512 * th:1024 * dc + 512 * th + 512],
                                 s0=(dc == 0), s1=(dc == 15):
                                 e.matmul(o_, l, rr, start=s0, stop=s1),
                                 inc="PE" if dc == 15 else None)
                    for dc in range(16):
                        p.do("tensor", lambda e, o_=hi_[:, 896:912],
                             l=wq[:, 2048 * dqc + 128 * dc:2048 * dqc + 128 * dc + 128],
                             rr=xnTh[:, 528 * dc + 512:528 * dc + 528],
                             s0=(dc == 0), s1=(dc == 15):
                             e.matmul(o_, l, rr, start=s0, stop=s1),
                             inc="PE" if dc == 15 else None)
                    pe_q = p.val("PE")
                    p.wait("scalar", "PE", pe_q)
                    p.do("scalar", lambda e, lo_=lo, dq=dqc: e.copy(
                        q4t[:, 1024 * dq:1024 * dq + 1024], lo_[:, 0:1024]), inc="SC")
                    p.do("scalar", lambda e, hh=hi_, dq=dqc: e.copy(
                        qg4t[:, 16 * dq:16 * dq + 16], hh[:, 896:912]), inc="SC")
                    qev[dqc] = p.val("SC")
                sec_pe[iq] = p.val("PE")

                ik = wload(wk_d[g], 0)
                wk = wbuf[ik % 2]
                p.wait("tensor", "WL", wl_done[ik])
                p.wait("tensor", "SC", p.val("SC"))
                kev = {}
                for dkc in range(4):
                    lo, hi_ = PP[dkc % 2]
                    p.wait("tensor", "SC", kev.get(dkc - 2, 0))
                    for sp0, w in K_SPANS:
                        for dc in range(16):
                            ten, col = xcol(dc, sp0)
                            if sp0 < 1024:
                                o_ap = lo[:, sp0:sp0 + w]
                            elif sp0 < 1536:
                                o_ap = hi_[:, sp0 - 1024:sp0 - 1024 + w]
                            else:
                                o_ap = hi_[:, 896:912]
                            p.do("tensor", lambda e, o_=o_ap,
                                 l=wk[:, 2048 * dkc + 128 * dc:2048 * dkc + 128 * dc + 128],
                                 rr=ten[:, col:col + w],
                                 s0=(dc == 0), s1=(dc == 15):
                                 e.matmul(o_, l, rr, start=s0, stop=s1),
                                 inc="PE" if dc == 15 else None)
                    pe_k = p.val("PE")
                    p.wait("scalar", "PE", pe_k)
                    p.do("scalar", lambda e, lo_=lo, dk=dkc: e.copy(
                        k4t[:, 1552 * dk:1552 * dk + 1024], lo_[:, 0:1024]), inc="SC")
                    p.do("scalar", lambda e, hh=hi_, dk=dkc: e.copy(
                        k4t[:, 1552 * dk + 1024:1552 * dk + 1536], hh[:, 0:512]), inc="SC")
                    p.do("scalar", lambda e, hh=hi_, dk=dkc: e.copy(
                        k4t[:, 1552 * dk + 1536:1552 * dk + 1552], hh[:, 896:912]), inc="SC")
                    kev[dkc] = p.val("SC")
                sec_pe[ik] = p.val("PE")

                iv = wload(wv_d[g], 0)
                wv = wbuf[iv % 2]
                p.wait("tensor", "WL", wl_done[iv])
                p.wait("tensor", "SC", p.val("SC"))
                vev = {}
                for kt in range(13):
                    lo, _hi = PP[kt % 2]
                    p.wait("tensor", "SC", vev.get(kt - 2, 0))
                    rows = 16 if kt == 12 else 128
                    kl = 1536 if kt == 12 else 128 * kt
                    for dc in range(16):
                        ten, col = xcol(dc, kl)
                        p.do("tensor", lambda e, o_=lo[0:rows, 0:512],
                             l=ten[:, col:col + rows],
                             rr=wv[:, 512 * dc:512 * dc + 512],
                             s0=(dc == 0), s1=(dc == 15):
                             e.matmul(o_, l, rr, start=s0, stop=s1),
                             inc="PE" if dc == 15 else None)
                    pe_v = p.val("PE")
                    p.wait("scalar", "PE", pe_v)
                    p.do("scalar", lambda e, lo_=lo, k=kt, rws=rows: e.copy(
                        v4[0:rws, 512 * k:512 * k + 512], lo_[0:rws, 0:512]), inc="SC")
                    vev[kt] = p.val("SC")
                sec_pe[iv] = p.val("PE")
                sc_proj = p.val("SC")
                if g >= 1:
                    emit_precast(10 if g < 3 else 20)

                p.wait("tensor", "SC", sc_proj)
                p.wait("scalar", "DV", dv_cst)
                mview = maskb[:, :].rearrange("p (qt s f) -> p qt s f", qt=8, s=5)
                a_sc_exp = {}
                a_sc_exp1 = {}
                a_dv_mask = {}
                a_pe_st = {}
                a_pe_sum = {}
                a_pe_pv = {}
                a_dv_r = {}
                a_sc_rt = {}
                a_pe_rt = {}
                a_pe_rb = {}
                a_sc_rb = {}
                a_dv_mult = {}

                def st_s1(i):
                    hi2, qt = i // 8, i % 8
                    par = (it + i) % 2
                    sA, _sB = PP[par]
                    esb = e_sb[par]
                    p.wait("tensor", "SC", a_sc_exp.get(i - 2, 0))
                    for kc in range(5):
                        p.do("tensor", lambda e, o_=sA[:, 128 * kc:128 * kc + 128],
                             l=k4t[:, 1552 * hi2 + 128 * (qt + kc):1552 * hi2 + 128 * (qt + kc) + 128],
                             rr=q4t[:, 1024 * hi2 + 128 * qt:1024 * hi2 + 128 * qt + 128]:
                             e.matmul(o_, l, rr, start=True, stop=True))
                    p.do("tensor", lambda e, o_=sA[0:16, 640:768],
                         l=k4t[:, 1552 * hi2 + 1536:1552 * hi2 + 1552],
                         rr=q4t[:, 1024 * hi2 + 128 * qt:1024 * hi2 + 128 * qt + 128]:
                         e.matmul(o_, l, rr, start=True, stop=True), inc="PE")
                    a_pe_st[i] = p.val("PE")
                    p.wait("scalar", "PE", a_pe_st[i])
                    p.wait("scalar", "DV", a_dv_mask.get(i - 2, 0))
                    p.do("scalar", lambda e, s_=sA, d=esb: e.activation(
                        d[:, 128:512], s_[:, 128:512], AF.Exp, bias=ln[:, 15:16], scale=SCALING), inc="SC")
                    a_sc_exp1[i] = p.val("SC")
                    p.do("scalar", lambda e, s_=sA, d=esb: e.activation(
                        d[:, 0:128], s_[:, 0:128], AF.Exp, bias=ln[:, 15:16], scale=SCALING), inc="SC")
                    p.do("scalar", lambda e, s_=sA, d=esb: e.activation(
                        d[:, 512:640], s_[:, 512:640], AF.Exp, bias=ln[:, 15:16], scale=SCALING), inc="SC")
                    p.do("scalar", lambda e, s_=sA, d=esb: e.activation(
                        d[0:16, 640:768], s_[0:16, 640:768], AF.Exp, bias=ln[0:16, 15:16], scale=SCALING), inc="SC")
                    a_sc_exp[i] = p.val("SC")
                    p.wait("vector", "SC", a_sc_exp[i])
                    p.do("vector", lambda e, d=esb, q=qt: e.tensor_mul(
                        d[:, 0:128], d[:, 0:128], mview[:, q, 0, :]))
                    if qt == 0:
                        p.do("vector", lambda e, d=esb: e.tensor_mul(
                            d[:, 128:256], d[:, 128:256], mview[:, 0, 1, :]))
                    if qt == 7:
                        p.do("vector", lambda e, d=esb: e.tensor_mul(
                            d[:, 384:512], d[:, 384:512], mview[:, 7, 2, :]))
                    p.do("vector", lambda e, d=esb, q=qt: e.tensor_mul(
                        d[:, 512:640], d[:, 512:640], mview[:, q, 3, :]))
                    p.do("vector", lambda e, d=esb, q=qt: e.tensor_mul(
                        d[0:16, 640:768], d[0:16, 640:768], mview[0:16, q, 4, :]), inc="DV")
                    a_dv_mask[i] = p.val("DV")

                def st_s2(i):
                    hi2, qt = i // 8, i % 8
                    par = (it + i) % 2
                    _sA, sB = PP[par]
                    esb = e_sb[par]
                    free_kc = [kc for kc in (1, 2, 3)
                               if not (kc == 1 and qt == 0) and not (kc == 3 and qt == 7)]
                    mask_kc = [kc for kc in range(5) if kc not in free_kc]
                    p.wait("tensor", "SC", a_sc_exp1[i])
                    p.wait("tensor", "DV", a_dv_mult.get(i - 2, 0))
                    p.wait("tensor", "SC", a_sc_rb.get(i - 2, 0))
                    for n_, kc in enumerate(free_kc):
                        p.do("tensor", lambda e, o_=sB[:, 0:1],
                             l=esb[:, 128 * kc:128 * kc + 128], s0=(n_ == 0):
                             e.matmul(o_, l, ones[:, :], start=s0, stop=False))
                    p.wait("tensor", "DV", a_dv_mask[i])
                    for kc in mask_kc:
                        p.do("tensor", lambda e, o_=sB[:, 0:1],
                             l=esb[:, 128 * kc:128 * kc + 128]:
                             e.matmul(o_, l, ones[:, :], start=False, stop=False))
                    p.do("tensor", lambda e, o_=sB[:, 0:1], l=esb[0:16, 640:768]:
                         e.matmul(o_, l, ones[0:16, :], start=False, stop=True), inc="PE")
                    a_pe_sum[i] = p.val("PE")
                    for n_, kc in enumerate(free_kc + mask_kc):
                        p.do("tensor", lambda e, o_=sB[:, 512:640],
                             l=v4[:, 512 * (qt + kc) + 128 * hi2:512 * (qt + kc) + 128 * hi2 + 128],
                             rr=esb[:, 128 * kc:128 * kc + 128], s0=(n_ == 0):
                             e.matmul(o_, l, rr, start=s0, stop=False))
                    p.do("tensor", lambda e, o_=sB[:, 512:640],
                         l=v4[0:16, 512 * 12 + 128 * hi2:512 * 12 + 128 * hi2 + 128],
                         rr=esb[0:16, 640:768]:
                         e.matmul(o_, l, rr, start=False, stop=True), inc="PE")
                    a_pe_pv[i] = p.val("PE")
                    p.wait("vector", "PE", a_pe_sum[i])
                    p.do("vector", lambda e, s_=sB, d=rs_sb[par]:
                         e.reciprocal(d, s_[:, 0:1]), inc="DV")
                    p.wait("vector", "DV", p.val("DV"))
                    a_dv_r[i] = p.val("DV")

                def st_s3(i):
                    hi2, qt = i // 8, i % 8
                    h = 4 * g + hi2
                    par = (it + i) % 2
                    _sA, sB = PP[par]
                    p.wait("tensor", "DV", a_dv_r[i])
                    p.do("tensor", lambda e, s_=sB, d=rs_sb[par]: e.transpose(
                        s_[0:1, 4:132], d, idf[:, :]), inc="PE")
                    a_pe_rt[i] = p.val("PE")
                    p.wait("scalar", "PE", a_pe_rt[i])
                    p.do("scalar", lambda e, s_=sB, d=rtb[par]: e.copy(
                        d[0:1, :], s_[0:1, 4:132]), inc="SC")
                    a_sc_rt[i] = p.val("SC")
                    p.wait("tensor", "SC", a_sc_rt[i])
                    p.do("tensor", lambda e, s_=sB, d=rtb[par]: e.matmul(
                        s_[:, 256:384], ones1[:, :], d[0:1, :],
                        start=True, stop=True), inc="PE")
                    a_pe_rb[i] = p.val("PE")
                    p.wait("scalar", "PE", a_pe_rb[i])
                    p.wait("scalar", "DV", a_dv_mult.get(i - 2, 0))
                    p.do("scalar", lambda e, s_=sB, d=rb_sb[par]: e.copy(
                        d[:, :], s_[:, 256:384]), inc="SC")
                    a_sc_rb[i] = p.val("SC")
                    p.wait("vector", "SC", a_sc_rb[i])
                    p.wait("vector", "PE", a_pe_pv[i])
                    p.do("vector", lambda e, s_=sB, rbs=rb_sb[par], hh=h, q=qt:
                         e.tensor_mul(attnT[:, 1024 * hh + 128 * q:1024 * hh + 128 * q + 128],
                                      s_[:, 512:640], rbs[:, :]), inc="DV")
                    a_dv_mult[i] = p.val("DV")

                for os_ in range(34):
                    if os_ < 32:
                        st_s1(os_)
                    if 1 <= os_ <= 32:
                        st_s2(os_ - 1)
                    if os_ >= 2:
                        st_s3(os_ - 2)
                it += 32
                iter_dv[it - 1] = p.val("DV")
                iter_sc[it - 1] = p.val("SC")

                g_pe_sg = {}
                g_sc_eg = {}
                g_dv_sum = {}
                g_pe_egt = {}
                g_sc_egt = {}
                g_pe_og = {}
                g_sc_og = {}

                def glob_s1(hh2):
                    h = 4 * g + hh2
                    par = (it + hh2) % 2
                    hp = hh2 % 2
                    sA = PP[par][0]
                    esb = e_sb[par]
                    if hh2 < 2:
                        p.wait("tensor", "SC", p.val("SC"))
                        p.wait("tensor", "DV", p.val("DV"))
                    else:
                        p.wait("tensor", "SC", g_sc_eg[hh2 - 2])
                    for th in range(2):
                        p.do("tensor", lambda e, o_=sA[0:16, 512 * th:512 * th + 512],
                             l=qg4t[:, 16 * hh2:16 * hh2 + 16],
                             rr=k4t[:, 1552 * hh2 + OWN0 + 512 * th:1552 * hh2 + OWN0 + 512 * th + 512]:
                             e.matmul(o_, l, rr, start=True, stop=True),
                             inc="PE" if th == 1 else None)
                    g_pe_sg[hh2] = p.val("PE")
                    p.wait("vector", "PE", g_pe_sg[hh2])
                    p.do("vector", lambda e, s_=sA, c=9 + hp: e.reduce_max(
                        ln[0:16, c:c + 1], s_[0:16, 0:1024], axis=AX.X, negate=True), inc="DV")
                    p.wait("vector", "DV", p.val("DV"))
                    p.do("vector", lambda e, c=9 + hp, d=11 + hp: e.tensor_scalar_mul(
                        ln[0:16, d:d + 1], ln[0:16, c:c + 1], SCALING), inc="DV")
                    p.wait("vector", "DV", p.val("DV"))
                    p.do("vector", lambda e, hh=h, d=11 + hp: e.tensor_scalar_mul(
                        mg[0:16, hh:hh + 1], ln[0:16, d:d + 1], -1.0), inc="DV")
                    dv_mg = p.val("DV")
                    p.wait("scalar", "DV", dv_mg)
                    p.wait("scalar", "PE", g_pe_sg[hh2])
                    if hh2 >= 2:
                        p.wait("scalar", "PE", g_pe_egt[hh2 - 2])
                        p.wait("scalar", "DV", g_dv_sum[hh2 - 2])
                    else:
                        p.wait("scalar", "DV", p.val("DV"))
                    egtail = rb_sb[par][:, :].bitcast(BF16)
                    p.do("scalar", lambda e, s_=sA, d=esb, bi=11 + hp: e.activation(
                        d[0:16, 0:768], s_[0:16, 0:768], AF.Exp,
                        bias=ln[0:16, bi:bi + 1], scale=SCALING), inc="SC")
                    p.do("scalar", lambda e, s_=sA, d=egtail, bi=11 + hp: e.activation(
                        d[0:16, 0:256], s_[0:16, 768:1024], AF.Exp,
                        bias=ln[0:16, bi:bi + 1], scale=SCALING), inc="SC")
                    g_sc_eg[hh2] = p.val("SC")
                    p.wait("vector", "SC", g_sc_eg[hh2])
                    p.do("vector", lambda e, d=esb, c=13 + hp: e.reduce_sum(
                        ln[0:16, c:c + 1], d[0:16, 0:768], axis=AX.X), inc="DV")
                    p.wait("vector", "DV", p.val("DV"))
                    p.do("vector", lambda e, d=egtail, c=9 + hp: e.reduce_sum(
                        ln[0:16, c:c + 1], d[0:16, 0:256], axis=AX.X), inc="DV")
                    p.wait("vector", "DV", p.val("DV"))
                    p.do("vector", lambda e, hh=h, c1=13 + hp, c2=9 + hp: e.tensor_add(
                        sg[0:16, hh:hh + 1], ln[0:16, c1:c1 + 1], ln[0:16, c2:c2 + 1]), inc="DV")
                    g_dv_sum[hh2] = p.val("DV")

                def glob_s2(hh2):
                    h = 4 * g + hh2
                    par = (it + hh2) % 2
                    sB = PP[par][1]
                    esb = e_sb[par]
                    p.wait("tensor", "SC", g_sc_eg[hh2])
                    if hh2 >= 2:
                        p.wait("tensor", "SC", g_sc_og[hh2 - 2])
                    egtail = rb_sb[par][:, :].bitcast(BF16)
                    for j in range(8):
                        if j < 6:
                            srcj = esb[0:16, 128 * j:128 * j + 128]
                        else:
                            srcj = egtail[0:16, 128 * (j - 6):128 * (j - 6) + 128]
                        dstj = sB[:, 8 * j:8 * j + 8].bitcast(BF16)
                        p.do("tensor", lambda e, d=dstj, s_=srcj:
                             e.transpose(d[0:128, 0:16], s_, idb[0:16, 0:16]),
                             inc="PE" if j == 7 else None)
                    g_pe_egt[hh2] = p.val("PE")
                    p.wait("scalar", "PE", g_pe_egt[hh2])
                    p.do("scalar", lambda e, s_=sB: e.copy(
                        egt[:, :], s_[:, 0:64].bitcast(BF16)), inc="SC")
                    g_sc_egt[hh2] = p.val("SC")
                    p.wait("tensor", "SC", g_sc_egt[hh2])
                    for j in range(8):
                        p.do("tensor", lambda e, o_=sB[:, 512:528],
                             l=v4[:, 512 * (2 + j) + 128 * hh2:512 * (2 + j) + 128 * hh2 + 128],
                             rr=egt[:, 16 * j:16 * j + 16], s0=(j == 0), s1=(j == 7):
                             e.matmul(o_, l, rr, start=s0, stop=s1),
                             inc="PE" if j == 7 else None)
                    g_pe_og[hh2] = p.val("PE")
                    p.wait("scalar", "PE", g_pe_og[hh2])
                    p.do("scalar", lambda e, s_=sB, hh=h: e.copy(
                        og[:, 16 * hh:16 * hh + 16], s_[:, 512:528]), inc="SC")
                    g_sc_og[hh2] = p.val("SC")

                for gs in range(5):
                    if gs < 4:
                        glob_s1(gs)
                    if gs >= 1:
                        glob_s2(gs - 1)
                it += 4
                iter_dv[it - 1] = p.val("DV")
                iter_sc[it - 1] = p.val("SC")

            pa_pe = p.val("PE")
            pa_sc = p.val("SC")
            pa_dv = p.val("DV")

            p.wait("gpsimd", "PC", 16 * len(pieces))
            pf_wl = {}
            for hcp in range(2):
                p.do("gpsimd", lambda e, d=w1b[hcp % 2], s_=w1_d[hcp]:
                     e.dma_start(out=d[:, :], in_=s_), inc="WL", inc_n=16)
                p.do("gpsimd", lambda e, d=w3b[hcp % 2], s_=w3_d[hcp]:
                     e.dma_start(out=d[:, :], in_=s_), inc="WL", inc_n=16)
                pf_wl[hcp] = p.val("WL")

            p.wait("gpsimd", "SC", pa_sc)
            p.wait("gpsimd", "DV", pa_dv)
            p.do("gpsimd", lambda e: e.dma_start(out=part_src.ap()[0:128, :], in_=og[:, :]),
                 inc="PS", inc_n=16)
            p.do("gpsimd", lambda e: e.dma_start(
                out=part_src.ap()[128, :].rearrange("(q h) -> q h", q=16),
                in_=mg[0:16, :]), inc="PS", inc_n=16)
            p.do("gpsimd", lambda e: e.dma_start(
                out=part_src.ap()[129, :].rearrange("(q h) -> q h", q=16),
                in_=sg[0:16, :]), inc="PS", inc_n=16)
            p.wait("gpsimd", "PS", p.val("PS"))
            p.do("gpsimd", lambda e: e.collective_compute(
                "AllGather", ALU.bypass, replica_groups=[list(range(NCORES))],
                ins=[part_src.ap()], outs=[ag_out.ap()]), inc="CC")
            cc_done = p.val("CC")

            wo_pe = {}
            y_ev = {}
            for dc in range(16):
                if dc == 8:
                    p.wait("gpsimd", "CC", cc_done)
                    agv = ag_out.ap().rearrange("(r y) c -> r y c", y=130)
                    p.do("gpsimd", lambda e: e.dma_start(
                        out=mfl[0:1, :].rearrange("p (r x) -> p r x", r=8),
                        in_=agv[:, 128, :].unsqueeze(0)), inc="WL", inc_n=16)
                    p.do("gpsimd", lambda e: e.dma_start(
                        out=sfl[0:1, :].rearrange("p (r x) -> p r x", r=8),
                        in_=agv[:, 129, :].unsqueeze(0)), inc="WL", inc_n=16)
                    p.do("gpsimd", lambda e: e.dma_start(
                        out=oga[:, :].rearrange("p (r c) -> p r c", r=8),
                        in_=agv[:, 0:128, :].rearrange("r p c -> p r c")), inc="WL", inc_n=16)
                    ld_mg = p.val("WL")
                    p.wait("vector", "WL", ld_mg)
                    p.wait("vector", "PE", pa_pe)
                    p.do("vector", lambda e: e.tensor_reduce(
                        tmg[0:1, 0:256], mfl[0:1, :].rearrange("p (r x) -> p x r", r=8),
                        axis=AX.X, op=ALU.max), inc="DV")
                    p.wait("vector", "DV", p.val("DV"))
                    p.do("vector", lambda e: e.tensor_sub(
                        wb2[0:1, 0:2048].rearrange("p (r x) -> p r x", r=8),
                        mfl[0:1, :].rearrange("p (r x) -> p r x", r=8),
                        tmg[0:1, 0:256].unsqueeze(1).broadcast_to([1, 8, 256])), inc="DV")
                    dv_diff = p.val("DV")
                    p.wait("scalar", "DV", dv_diff)
                    p.wait("scalar", "PE", pa_pe)
                    p.do("scalar", lambda e: e.activation(
                        wb2[0:1, 0:2048], wb2[0:1, 0:2048], AF.Exp), inc="SC")
                    sc_alpha = p.val("SC")
                    p.wait("vector", "SC", sc_alpha)
                    p.do("vector", lambda e: e.tensor_mul(sfl[0:1, :], sfl[0:1, :], wb2[0:1, 0:2048]), inc="DV")
                    p.wait("vector", "DV", p.val("DV"))
                    p.do("vector", lambda e: e.tensor_reduce(
                        tmg[0:1, 256:512], sfl[0:1, :].rearrange("p (r x) -> p x r", r=8),
                        axis=AX.X, op=ALU.add), inc="DV")
                    p.wait("vector", "DV", p.val("DV"))
                    p.do("vector", lambda e: e.reciprocal(tmg[0:1, 512:768], tmg[0:1, 256:512]), inc="DV")
                    p.wait("vector", "DV", p.val("DV"))
                    p.do("vector", lambda e: e.tensor_mul(
                        wb2[0:1, 0:2048].rearrange("p (r x) -> p r x", r=8),
                        wb2[0:1, 0:2048].rearrange("p (r x) -> p r x", r=8),
                        tmg[0:1, 512:768].unsqueeze(1).broadcast_to([1, 8, 256])), inc="DV")
                    dv_w = p.val("DV")
                    p.wait("scalar", "DV", dv_w)
                    p.do("scalar", lambda e: e.copy(wbf[0:1, :], wb2[0:1, 0:2048]), inc="SC")
                    sc_wbf = p.val("SC")

                if dc >= 2:
                    p.wait("sync", "PE", wo_pe[dc - 2])
                else:
                    p.wait("sync", "PE", pa_pe)
                p.wait("sync", "PC", pc_ready_wo[dc])
                p.do("sync", lambda e, d=wo_sb[dc % 2], s_=wobf_d[dc]:
                     e.dma_start(out=d[:, :], in_=s_), inc="LD", inc_n=16)
                wl_wo = p.val("LD")
                lo, hi_ = PP[dc % 2]
                p.wait("tensor", "LD", wl_wo)
                p.wait("tensor", "SC", y_ev.get(dc - 2, 0))
                if dc == 0:
                    p.wait("tensor", "SC", pa_sc)
                    p.wait("tensor", "DV", pa_dv)
                wo_t = wo_sb[dc % 2]
                for th in range(2):
                    for dvc in range(16):
                        p.do("tensor", lambda e, o_=lo[:, 512 * th:512 * th + 512],
                             l=wo_t[:, 128 * dvc:128 * dvc + 128],
                             rr=attnT[:, 1024 * dvc + 512 * th:1024 * dvc + 512 * th + 512],
                             s0=(dvc == 0), s1=(dvc == 15):
                             e.matmul(o_, l, rr, start=s0, stop=s1),
                             inc="PE" if dvc == 15 else None)
                for dvc in range(16):
                    p.do("tensor", lambda e, o_=hi_[:, 896:912],
                         l=wo_t[:, 128 * dvc:128 * dvc + 128],
                         rr=attnT[:, 1024 * dvc:1024 * dvc + 16],
                         s0=(dvc == 0), s1=(dvc == 15):
                         e.matmul(o_, l, rr, start=s0, stop=s1),
                         inc="PE" if dvc == 15 else None)
                wo_pe[dc] = p.val("PE")
                pe_y = p.val("PE")
                p.wait("scalar", "PE", pe_y)
                if dc >= 2:
                    p.wait("scalar", "YD", 16 * (dc - 1))
                p.do("scalar", lambda e, d=y_sb[dc % 2], lo_=lo:
                     e.copy(d[:, :], lo_[:, 0:1024]), inc="SC")
                p.do("scalar", lambda e, hh=hi_, d=dc: e.copy(
                    y_l[:, 16 * d:16 * d + 16], hh[:, 896:912]), inc="SC")
                sc_yev = p.val("SC")
                y_ev[dc] = sc_yev
                p.wait("sync", "SC", sc_yev)
                p.do("sync", lambda e, d=dc, s_=y_sb[dc % 2]: e.dma_start(
                    out=y_scr[128 * d:128 * d + 128, :], in_=s_[:, :]), inc="YD", inc_n=16)
            py_pe = p.val("PE")
            py_sc = p.val("SC")
            py_yd = p.val("YD")

            p.wait("sync", "DV", dv_w)
            p.do("sync", lambda e: e.dma_start(out=mrow_d.unsqueeze(0), in_=wb2[0:1, 0:2048]),
                 inc="LD", inc_n=16)
            p.wait("sync", "LD", p.val("LD"))
            p.do("sync", lambda e: e.dma_start(
                out=tmg[:, :], in_=mrow_d.unsqueeze(0).broadcast_to([128, 2048])),
                inc="LD", inc_n=16)
            ld_bc = p.val("LD")
            p.wait("vector", "LD", ld_bc)
            p.do("vector", lambda e: e.tensor_mul(wb2[:, :], oga[:, :], tmg[:, :]), inc="DV")
            p.wait("vector", "DV", p.val("DV"))
            p.do("vector", lambda e: e.tensor_reduce(
                oga[:, 0:256], wb2[:, :].rearrange("p (r x) -> p x r", r=8),
                axis=AX.X, op=ALU.add), inc="DV")
            p.wait("vector", "DV", p.val("DV"))
            dv_on = p.val("DV")
            p.wait("scalar", "DV", dv_on)
            p.do("scalar", lambda e: e.copy(o_n[:, 0:256], oga[:, 0:256]), inc="SC")
            sc_on = p.val("SC")
            p.wait("vector", "SC", sc_on)
            p.do("vector", lambda e: e.memset(ln[0:1, 12:13], 0.0), inc="DV")
            mg_dv = p.val("DV")
            mg_sc = p.val("SC")

            MERGE_SLOTS = set()
            for _s in range(64):
                lo_b = MM["hid"] + 2048 * _s
                hi_b = lo_b + 2048
                for (r0, r1) in ((MM["mfl"], MM["mfl"] + 32768),
                                 (MM["wb2"], MM["wb2"] + 12288)):
                    if lo_b < r1 and hi_b > r0:
                        MERGE_SLOTS.add(_s)
            slot = [-1] * 64
            freelist = [(s, 0) for s in range(64)]
            chunk_prod = {}
            HSLOT = lambda sid: (hid[:, 1024 * sid:1024 * sid + 1024] if sid < 64
                                 else h0[sid - 64][:, :])
            pe_h1_last = [0]
            bfly_pend = []
            mg_dv_ref = [mg_dv]

            def bfly(pairs):
                for (c1, c2) in pairs:
                    s1, s2 = slot[c1], slot[c2]
                    if len(freelist) >= 2:
                        d1, _ = freelist.pop()
                        d2, _ = freelist.pop()
                        if d1 in MERGE_SLOTS or d2 in MERGE_SLOTS:
                            p.wait("vector", "DV", mg_dv_ref[0])
                    else:
                        d1, d2 = 64, 65
                        p.wait("vector", "PE", pe_h1_last[0])
                    for cch in (c1, c2):
                        eng_, v = chunk_prod[cch]
                        if eng_ == "SC":
                            p.wait("vector", "SC", v)
                    p.do("vector", lambda e, a=HSLOT(s1), b2=HSLOT(s2), d=HSLOT(d1):
                         e.tensor_add(d, a, b2))
                    p.do("vector", lambda e, a=HSLOT(s1), b2=HSLOT(s2), d=HSLOT(d2):
                         e.tensor_sub(d, a, b2), inc="DV")
                    v = p.val("DV")
                    chunk_prod[c1] = ("DV", v)
                    chunk_prod[c2] = ("DV", v)
                    freelist.append((s1, v))
                    freelist.append((s2, v))
                    slot[c1], slot[c2] = d1, d2

            pf_bar_dv = mg_dv
            w13_pe = {}
            pf_sc_sil = {}
            pf_dv_h0 = {}
            pf_pe_h1 = {}
            pf_ev = {}
            pf_pe_up = {}
            for hc in range(65):
                if hc < 64:
                    if hc >= 2:
                        p.wait("gpsimd", "PE", w13_pe[hc - 2])
                        p.do("gpsimd", lambda e, d=w1b[hc % 2], s_=w1_d[hc]:
                             e.dma_start(out=d[:, :], in_=s_), inc="WL", inc_n=16)
                        p.do("gpsimd", lambda e, d=w3b[hc % 2], s_=w3_d[hc]:
                             e.dma_start(out=d[:, :], in_=s_), inc="WL", inc_n=16)
                        wl13 = p.val("WL")
                    else:
                        wl13 = pf_wl[hc]
                    par = hc % 2
                    up_ps, gt_ps = PP[par]
                    w1t, w3t = w1b[par], w3b[par]
                    p.wait("tensor", "WL", wl13)
                    p.wait("tensor", "SC", pf_sc_sil.get(hc - 2, 0))
                    for th in range(2):
                        for dc in range(16):
                            p.do("tensor", lambda e, o_=up_ps[:, 512 * th:512 * th + 512],
                                 l=w1t[:, 128 * dc:128 * dc + 128],
                                 rr=xnT[:, 1024 * dc + 512 * th:1024 * dc + 512 * th + 512],
                                 s0=(dc == 0), s1=(dc == 15):
                                 e.matmul(o_, l, rr, start=s0, stop=s1),
                                 inc="PE" if (dc == 15 and th == 1) else None)
                    pf_pe_up[hc] = p.val("PE")
                    if hc == 0:
                        p.wait("scalar", "PC", 16 * len(pieces))
                if hc >= 1:
                    j = hc - 1
                    parj = j % 2
                    gtj = PP[parj][1]
                    h0j = h0[parj]
                    p.wait("tensor", "DV", pf_dv_h0[j])
                    for th in range(2):
                        p.do("tensor", lambda e, o_=gtj[:, 512 * th:512 * th + 512],
                             rr=h0j[:, 512 * th:512 * th + 512]:
                             e.matmul(o_, h128[:, :], rr, start=True, stop=True),
                             inc="PE" if th == 1 else None)
                    pf_pe_h1[j] = p.val("PE")
                    pe_h1_last[0] = pf_pe_h1[j]
                    sid, dvfree = freelist.pop()
                    slot[j] = sid
                    p.wait("scalar", "DV", dvfree)
                    if sid in MERGE_SLOTS:
                        p.wait("scalar", "DV", mg_dv)
                    p.wait("scalar", "PE", pf_pe_h1[j])
                    p.do("scalar", lambda e, s_=gtj, d=HSLOT(sid): e.activation(
                        d, s_[:, 0:1024], AF.Copy, scale=FWHT_SCALE), inc="SC")
                    chunk_prod[j] = ("SC", p.val("SC"))
                    pf_ev[j] = p.val("SC")
                    for bb in range(5):
                        mask_ = (1 << (bb + 1)) - 1
                        if (j & mask_) == mask_:
                            base = j & ~mask_
                            bfly_pend.extend(
                                (c, c ^ (1 << bb))
                                for c in range(base, base + (1 << (bb + 1)))
                                if not (c >> bb) & 1)
                if hc < 64:
                    p.wait("tensor", "SC", pf_ev.get(hc - 2, 0))
                    for th in range(2):
                        for dc in range(16):
                            p.do("tensor", lambda e, o_=gt_ps[:, 512 * th:512 * th + 512],
                                 l=w3t[:, 128 * dc:128 * dc + 128],
                                 rr=xnT[:, 1024 * dc + 512 * th:1024 * dc + 512 * th + 512],
                                 s0=(dc == 0), s1=(dc == 15):
                                 e.matmul(o_, l, rr, start=s0, stop=s1),
                                 inc="PE" if (dc == 15 and th == 1) else None)
                    w13_pe[hc] = p.val("PE")
                    pe_gate = p.val("PE")
                    p.wait("scalar", "PE", pf_pe_up[hc])
                    p.wait("scalar", "DV", pf_dv_h0.get(hc - 2, 0))
                    slb = sil[par]
                    p.do("scalar", lambda e, s_=up_ps, d=slb: e.activation(
                        d[:, 0:512], s_[:, 0:512], AF.Silu), inc="SC")
                    p.do("scalar", lambda e, s_=up_ps, d=slb: e.activation(
                        d[:, 512:1024], s_[:, 512:1024], AF.Silu), inc="SC")
                    pf_sc_sil[hc] = p.val("SC")
                    p.wait("vector", "SC", pf_sc_sil[hc])
                    p.wait("vector", "PE", pe_gate)
                    h0b = h0[par]
                    p.do("vector", lambda e, s_=gt_ps, d=h0b, sl=slb:
                         e.tensor_mul(d[:, 0:512], s_[:, 0:512], sl[:, 0:512]))
                    p.do("vector", lambda e, s_=gt_ps, d=h0b, sl=slb:
                         e.tensor_mul(d[:, 512:1024], s_[:, 512:1024], sl[:, 512:1024]), inc="DV")
                    pf_dv_h0[hc] = p.val("DV")
                    for _ in range(min(6, len(bfly_pend))):
                        bfly([bfly_pend.pop(0)])
            tail = [(62, 63), (60, 62), (61, 63),
                    (56, 60), (57, 61), (58, 62), (59, 63)]
            for k in range(8):
                tail += [(48 + k, 56 + k), (32 + k, 48 + k), (40 + k, 56 + k),
                         (k, 32 + k), (8 + k, 40 + k), (16 + k, 48 + k),
                         (24 + k, 56 + k)]
            pend_set = set(bfly_pend)
            done_first = [pr for pr in tail if pr in pend_set]
            rest = [pr for pr in bfly_pend if pr not in set(done_first)]
            b5 = [pr for pr in tail if pr not in pend_set]
            for pr in rest:
                bfly([pr])
            for pr in tail:
                if pr in pend_set or pr in b5:
                    bfly([pr])
            bfly_pend.clear()
            pf_dv = p.val("DV")
            pf_sc = p.val("SC")

            w2_pe = {}
            xr_ld = {}
            fadd_dv = {}
            dltsub_dv = {}
            ep_dv = {}
            ep_sc = {}
            pd_pe_f = {}
            pd_pe_yg = {}
            st_of = {}
            pf_last_pe = w13_pe[63]
            b5order = []
            for k in range(8):
                b5order += [k, 32 + k, 8 + k, 40 + k, 16 + k, 48 + k,
                            24 + k, 56 + k]
            for dc in range(17):
                if dc < 16:
                    if dc >= 2:
                        p.wait("gpsimd", "PE", pd_pe_yg[dc - 2])
                        p.wait("gpsimd", "DV", ep_dv.get(dc - 2, 0))
                        p.wait("gpsimd", "DV", fadd_dv[dc - 2])
                    else:
                        p.wait("gpsimd", "PE", pf_last_pe)
                        p.wait("gpsimd", "DV", pf_dv_h0[63])
                    p.wait("gpsimd", "PC", pc_ready_w2[dc])
                    p.do("gpsimd", lambda e, d=w2b[dc % 2], s_=w2bf_d[dc]:
                         e.dma_start(out=d[:, :], in_=s_), inc="WL", inc_n=16)
                    p.wait("gpsimd", "PC", pc_ready_wo[dc])
                    p.do("gpsimd", lambda e, d=wo2[dc % 2], s_=wobf_d[dc]:
                         e.dma_start(out=d[:, :], in_=s_), inc="WL", inc_n=16)
                    wl2 = p.val("WL")
                    p.wait("gpsimd", "YD", py_yd)
                    p.do("gpsimd", lambda e, d=y2[dc % 2], dcc=dc: e.dma_start(
                        out=d[:, :], in_=y_scr[128 * dcc:128 * dcc + 128, :]), inc="WL", inc_n=16)
                    ld_pd = p.val("WL")
                    if dc < 2:
                        p.wait("sync", "DV", pf_dv_h0[63])
                    for tt in range(8):
                        p.wait("sync", "DV", ep_dv.get((dc - 2, tt), 0))
                        p.do("sync", lambda e, d=xrs[dc % 2], dcc=dc, t_=tt: e.dma_start(
                            out=d[:, 128 * t_:128 * t_ + 128],
                            in_=x_kv[256 + 128 * t_:384 + 128 * t_, 128 * dcc:128 * dcc + 128]),
                            inc="LD", inc_n=16)
                        xr_ld[(dc, tt)] = p.val("LD")
                    par = dc % 2
                    f_ps, aux_ps = PP[par]
                    w2t = w2b[par]
                    p.wait("tensor", "WL", wl2)
                    p.wait("tensor", "DV", fadd_dv.get(dc - 2, 0))
                    if dc == 0:
                        p.wait("tensor", "SC", pf_sc)
                        p.wait("tensor", "DV", mg_dv)
                    hcorder = b5order if dc == 0 else list(range(64))
                    last_dv_waited = 0
                    for th in range(2):
                        for ii, hc in enumerate(hcorder):
                            if dc == 0 and th == 0:
                                v = chunk_prod[hc][1]
                                if v > last_dv_waited:
                                    p.wait("tensor", "DV", v)
                                    last_dv_waited = v
                            p.do("tensor", lambda e, o_=f_ps[:, 512 * th:512 * th + 512],
                                 l=w2t[:, 128 * hc:128 * hc + 128],
                                 rr=HSLOT(slot[hc])[:, 512 * th:512 * th + 512],
                                 s0=(ii == 0), s1=(ii == 63):
                                 e.matmul(o_, l, rr, start=s0, stop=s1),
                                 inc="PE" if (ii == 63 and th == 1) else None)
                    w2_pe[dc] = p.val("PE")
                    pd_pe_f[dc] = p.val("PE")
                    p.wait("tensor", "DV", ep_dv.get(dc - 2, 0))
                    p.wait("tensor", "SC", ep_sc.get(dc - 2, 0))
                    wo_t = wo2[par]
                    for dvc in range(16):
                        p.do("tensor", lambda e, o_=aux_ps[:, 896:912],
                             l=wo_t[:, 128 * dvc:128 * dvc + 128],
                             rr=o_n[:, 16 * dvc:16 * dvc + 16],
                             s0=(dvc == 0), s1=(dvc == 15):
                             e.matmul(o_, l, rr, start=s0, stop=s1),
                             inc="PE" if dvc == 15 else None)
                    pd_pe_yg[dc] = p.val("PE")
                    p.wait("vector", "PE", pd_pe_f[dc])
                    p.wait("vector", "WL", ld_pd)
                    fs = fsb[par]
                    p.do("vector", lambda e, s_=f_ps, d=fs, y_=y2[par]:
                         e.tensor_add(d[:, 0:512], s_[:, 0:512], y_[:, 0:512]))
                    p.do("vector", lambda e, s_=f_ps, d=fs, y_=y2[par]:
                         e.tensor_add(d[:, 512:1024], s_[:, 512:1024], y_[:, 512:1024]), inc="DV")
                    fadd_dv[dc] = p.val("DV")
                    p.wait("vector", "PE", pd_pe_yg[dc])
                    p.do("vector", lambda e, s_=aux_ps, dcc=dc: e.tensor_sub(
                        dlt[:, 16 * dcc:16 * dcc + 16], s_[:, 896:912],
                        y_l[:, 16 * dcc:16 * dcc + 16]), inc="DV")
                    dltsub_dv[dc] = p.val("DV")
                if dc >= 1:
                    j = dc - 1
                    parj = j % 2
                    auxj = PP[parj][1]
                    fsj = fsb[parj]
                    p.wait("tensor", "DV", dltsub_dv[j])
                    p.do("tensor", lambda e, o_=auxj[0:16, 256:384], dcc=j: e.transpose(
                        o_, dlt[:, 16 * dcc:16 * dcc + 16], idf[:, :]), inc="PE")
                    pe_dt = p.val("PE")
                    p.wait("scalar", "PE", pe_dt)
                    p.do("scalar", lambda e, d=dltTc[parj], s_=auxj: e.copy(
                        d[0:16, :], s_[0:16, 256:384]), inc="SC")
                    sc_dt = p.val("SC")
                    p.wait("tensor", "DV", fadd_dv[j])
                    p.wait("tensor", "SC", sc_dt)
                    for tt in range(8):
                        tp = auxj[:, 0:128] if tt % 2 == 0 else auxj[:, 512:640]
                        if tt >= 2:
                            p.wait("tensor", "DV", ep_dv.get((j, tt - 2), 0))
                        p.do("tensor", lambda e, o_=tp, s_=fsj[:, 128 * tt:128 * tt + 128]:
                             e.transpose(o_, s_, idf[:, :]), inc="PE")
                        pe_tp = p.val("PE")
                        ob = outs[tt % 4]
                        qrole, qsem = ("scalar", "ST") if tt % 2 == 0 else ("sync", "S2")
                        p.wait("vector", "PE", pe_tp)
                        if tt >= 4:
                            p.wait("vector", qsem, st_of.get((j, tt - 4), 0))
                        elif j >= 1:
                            p.wait("vector", qsem, st_of.get((j - 1, 4 + tt), 0))
                        p.wait("vector", "LD", xr_ld[(j, tt)])
                        p.do("vector", lambda e, o_=ob, s_=tp,
                             x_=xrs[parj][:, 128 * tt:128 * tt + 128]:
                             e.tensor_add(o_[:, :], s_, x_), inc="DV")
                        dv_o = p.val("DV")
                        ep_dv[(j, tt)] = dv_o
                        p.wait(qrole, "DV", dv_o)
                        p.do(qrole, lambda e, o_=ob, dcc=j, t_=tt: e.dma_start(
                            out=o_main[128 * t_:128 * t_ + 128, 128 * dcc:128 * dcc + 128],
                            in_=o_[:, :]), inc=qsem, inc_n=16)
                        st_of[(j, tt)] = p.val(qsem)
                        if tt == 0:
                            if j >= 2:
                                p.wait("vector", "ST", st_of.get(("fix", j - 2), 0))
                            p.do("vector", lambda e, o_=ob, d=fixc[j % 2], dt_=dltTc[parj]:
                                 e.tensor_add(d[0:16, :], o_[0:16, :], dt_[0:16, :]), inc="DV")
                            dv_fx = p.val("DV")
                            p.wait("scalar", "DV", dv_fx)
                            p.do("scalar", lambda e, d=fixc[j % 2], dcc=j: e.dma_start(
                                out=o_fix[0:16, 128 * dcc:128 * dcc + 128], in_=d[0:16, :]),
                                inc="ST", inc_n=16)
                            st_of[("fix", j)] = p.val("ST")
                    ep_dv[j] = p.val("DV")
                    ep_sc[j] = p.val("SC")

            st_total = p.val("ST")
            s2_total = p.val("S2")
            p.wait("scalar", "ST", st_total)
            p.wait("scalar", "S2", s2_total)
            p.wait("gpsimd", "ST", st_total)
            p.wait("sync", "S2", s2_total)
            p.wait("sync", "ST", st_total)

        for role, reg in (("sync", block.sync), ("gpsimd", block.gpsimd),
                          ("tensor", block.tensor), ("scalar", block.scalar),
                          ("vector", block.vector)):
            reg(lambda eng, role=role: build(P(role, eng, sems)))

    return nc


_NC_CACHE = None


def kernel(L0_attn_hidden_states, L0_attn_Wq, L0_attn_Wk, L0_attn_Wv,
           L0_attn_Wo, L0_ffn_W1, L0_ffn_W2, L0_ffn_W3):
    global _NC_CACHE
    in_maps = _prep_inputs(L0_attn_hidden_states, L0_attn_Wq, L0_attn_Wk,
                           L0_attn_Wv, L0_attn_Wo, L0_ffn_W1, L0_ffn_W2,
                           L0_ffn_W3)
    if _NC_CACHE is None:
        _NC_CACHE = build_program()
    res = run_bass_kernel_spmd(_NC_CACHE, in_maps, core_ids=list(range(NCORES)))
    out = np.empty((S, D), np.float32)
    for c in range(NCORES):
        out[c * TOK:(c + 1) * TOK] = res.results[c]["o_main"]
    out[0:16] = res.results[0]["o_fix"]
    return out.reshape(1, S, D)


# version 15
# speedup vs baseline: 1.0153x; 1.0153x over previous
import numpy as np
import ml_dtypes
import concourse.bass as bass
import concourse.mybir as mybir
from concourse.bass_utils import run_bass_kernel_spmd
from contextlib import ExitStack

F32 = mybir.dt.float32
BF16 = mybir.dt.bfloat16
AF = mybir.ActivationFunctionType
AX = mybir.AxisListType
ALU = mybir.AluOpType

NCORES = 8
S = 8192
D = 2048
HFF = 8192
NH = 16
HD = 128
TOK = 1024
WIN = 1536
SCALING = float(HD) ** -0.5
C_SOFT = 25.0
FWHT_SCALE = float(HFF) ** -0.5
BF = np.dtype(ml_dtypes.bfloat16)
OWN0 = 256


def _had(n):
    h = np.array([[1.0]], dtype=np.float32)
    while h.shape[0] < n:
        h = np.block([[h, h], [h, -h]])
    return h


def _masks_for_core(c):
    s = c * TOK
    ws = s - 256
    out = np.zeros((8, 5, 128, 128), dtype=np.float32)
    i = np.arange(128)
    for qt in range(8):
        qg = s + qt * 128 + i[None, :]
        for slot, kc in ((0, 0), (1, 1), (2, 3), (3, 4)):
            kg = ws + qt * 128 + kc * 128 + i[:, None]
            valid = (kg >= 0) & (kg < S)
            out[qt, slot] = (valid & ((np.abs(qg - kg) < 256) | (kg < 16))).astype(np.float32)
        kg = i[:16, None]
        in_window = (kg >= ws + qt * 128) & (kg < ws + qt * 128 + 640)
        out[qt, 4, :16, :] = (~in_window).astype(np.float32)
    return out.astype(BF)


def _prep_inputs(x, Wq, Wk, Wv, Wo, W1, W2, W3):
    x = np.asarray(x, dtype=np.float32).reshape(S, D)
    Wq, Wk, Wv, Wo = (np.asarray(w, np.float32) for w in (Wq, Wk, Wv, Wo))
    W1, W2, W3 = (np.asarray(w, np.float32) for w in (W1, W2, W3))

    def qk_layout(w):
        a = w.reshape(16, 128, 16, 128).transpose(2, 1, 0, 3)
        a = a.reshape(4, 4, 128, 16, 128).transpose(0, 2, 1, 3, 4)
        return np.ascontiguousarray(a.reshape(4, 128, 8192))

    wq_s = qk_layout(Wq)
    wk_s = qk_layout(Wk)
    a = Wv.reshape(16, 128, 4, 4, 128).transpose(2, 1, 0, 3, 4)
    wv_s = np.ascontiguousarray(a.reshape(4, 128, 8192))
    a = Wo.reshape(16, 128, 16, 128).transpose(2, 1, 0, 3)
    wo_s = np.ascontiguousarray(a.reshape(16, 128, 2048))

    def w13_layout(w):
        a = w.reshape(64, 128, 16, 128).transpose(0, 3, 2, 1)
        return np.ascontiguousarray(a.reshape(64, 128, 2048))

    w1_s = w13_layout(W1)
    w3_s = w13_layout(W3)
    a = W2.reshape(16, 128, 64, 128).transpose(0, 3, 2, 1)
    w2_s = np.ascontiguousarray(a.reshape(16, 128, 8192))

    common = {
        "wq_s": wq_s, "wk_s": wk_s, "wv_s": wv_s, "wo_s": wo_s,
        "w1_s": w1_s, "w3_s": w3_s, "w2_s": w2_s,
        "h128": _had(128).astype(BF),
        "id_b": np.eye(128, dtype=BF),
        "id_f": np.eye(128, dtype=np.float32),
        "ones_f": np.ones((128, 1), dtype=BF),
        "ones1_b": np.ones((1, 128), dtype=BF),
        "x_glob": np.ascontiguousarray(x[0:16]),
    }
    maps = []
    for c in range(NCORES):
        s = c * TOK
        xkv = np.zeros((WIN, D), np.float32)
        lo, hi = max(0, s - 256), min(S, s + TOK + 256)
        xkv[lo - (s - 256): hi - (s - 256)] = x[lo:hi]
        maps.append(dict(common, x_kv=xkv, masks=_masks_for_core(c)))
    return maps



class P:
    def __init__(self, role, eng, sems):
        self.role = role
        self.eng = eng
        self.sems = sems
        self.cnt = {n: 0 for n in sems}

    def do(self, role, fn, inc=None, inc_n=1):
        if inc is not None:
            self.cnt[inc] += inc_n
        if self.role == role:
            ins = fn(self.eng)
            if inc is not None:
                ins.then_inc(self.sems[inc], inc_n)
        return self.cnt[inc] if inc is not None else None

    def wait(self, role, sem, val):
        if val > 0 and self.role == role:
            self.eng.wait_ge(self.sems[sem], val)

    def val(self, sem):
        return self.cnt[sem]


BASE = 20480
LIMIT = 229376

def _mk_map2():
    m = {}
    off = BASE

    def add(name, nbytes):
        nonlocal off
        o = (off + 31) // 32 * 32
        m[name] = o
        off = o + nbytes
        return o

    for n, b in (("h128", 256), ("idb", 256), ("idf", 512), ("ones", 32),
                 ("ones1", 256), ("y_l", 1024), ("dlt", 1024), ("o_n", 512),
                 ("ln", 64), ("mg", 64), ("sg", 64), ("og", 1024),
                 ("outs", 1024), ("fsb", 8192), ("dltTc", 1024), ("fixc", 1024),
                 ("xnT", 32768)):
        add(n, b)
    ov = (off + 31) // 32 * 32
    m["hid"] = ov
    pf_tail = ov + 131072
    m["w1b"], m["w3b"] = pf_tail, pf_tail + 8192
    m["sil"], m["h0"] = pf_tail + 16384, pf_tail + 24576
    assert pf_tail + 28672 <= LIMIT, f"PF overflow: {pf_tail + 28672}"

    o = [ov]
    def pa(name, nbytes):
        oo = (o[0] + 31) // 32 * 32
        m[name] = oo
        o[0] = oo + nbytes
        assert o[0] <= ov + 131072, (name, o[0] - ov)
        return oo
    pa("attnT", 32768)
    pa("xnTh", 16896)
    pa("maskb", 10240)
    pa("k4t", 12416)
    pa("v4", 13312)
    pa("q4t", 8192)
    pa("qg4t", 128)
    pa("wbuf", 32768)
    pa("e_sb", 3072)
    pa("rb", 1024)
    pa("egt", 256)
    m["xt"] = m["k4t"]
    m["sq"] = m["k4t"] + 16384
    m["xn"] = m["k4t"] + 24576
    m["wo_sb"] = m["wbuf"]
    m["y_sb"] = m["wbuf"] + 8192
    m["mfl"] = m["k4t"]
    m["sfl"] = m["k4t"] + 8192
    m["oga"] = m["k4t"] + 16384
    m["tmg"] = m["k4t"] + 24576
    m["wb2"] = m["wbuf"] + 16384
    m["wbf"] = m["wbuf"] + 24576
    m["w2b"] = m["xnT"]
    m["wo2"] = m["w1b"]
    m["y2"] = m["w3b"]
    m["xrs"] = m["sil"]
    return m

MM = _mk_map2()


def build_program():
    nc = bass.Bass("TRN2", target_bir_lowering=False, detect_race_conditions=False)

    x_kv = nc.dram_tensor("x_kv", [WIN, D], F32, kind="ExternalInput").ap()
    x_glob = nc.dram_tensor("x_glob", [16, D], F32, kind="ExternalInput").ap()
    wq_d = nc.dram_tensor("wq_s", [4, 128, 8192], F32, kind="ExternalInput").ap()
    wk_d = nc.dram_tensor("wk_s", [4, 128, 8192], F32, kind="ExternalInput").ap()
    wv_d = nc.dram_tensor("wv_s", [4, 128, 8192], F32, kind="ExternalInput").ap()
    wo_d = nc.dram_tensor("wo_s", [16, 128, 2048], F32, kind="ExternalInput").ap()
    w1_d = nc.dram_tensor("w1_s", [64, 128, 2048], F32, kind="ExternalInput").ap()
    w3_d = nc.dram_tensor("w3_s", [64, 128, 2048], F32, kind="ExternalInput").ap()
    w2_d = nc.dram_tensor("w2_s", [16, 128, 8192], F32, kind="ExternalInput").ap()
    h128_d = nc.dram_tensor("h128", [128, 128], BF16, kind="ExternalInput").ap()
    idb_d = nc.dram_tensor("id_b", [128, 128], BF16, kind="ExternalInput").ap()
    idf_d = nc.dram_tensor("id_f", [128, 128], F32, kind="ExternalInput").ap()
    ones_d = nc.dram_tensor("ones_f", [128, 1], BF16, kind="ExternalInput").ap()
    ones1_d = nc.dram_tensor("ones1_b", [1, 128], BF16, kind="ExternalInput").ap()
    mask_d = nc.dram_tensor("masks", [8, 5, 128, 128], BF16, kind="ExternalInput").ap()

    o_main = nc.dram_tensor("o_main", [TOK, D], F32, kind="ExternalOutput").ap()
    o_fix = nc.dram_tensor("o_fix", [16, D], F32, kind="ExternalOutput").ap()

    y_scr = nc.dram_tensor("y_scr", [D, TOK], F32, kind="Internal").ap()
    w2bf_d = nc.dram_tensor("w2bf", [16, 128, 8192], BF16, kind="Internal").ap()
    mrow_d = nc.dram_tensor("mrow", [2048], F32, kind="Internal").ap()
    wobf_d = nc.dram_tensor("wobf", [16, 128, 2048], BF16, kind="Internal").ap()
    part_src = nc.dram_tensor("part_src", [130, 256], F32, kind="Internal")
    ag_out = nc.dram_tensor("ag_out", [1040, 256], F32, kind="Internal",
                            addr_space="Shared")

    at = nc.alloc_sbuf_tensor_at
    h128 = at("h128b", [128, 128], BF16, offset=MM["h128"])
    idb = at("idb", [128, 128], BF16, offset=MM["idb"])
    idf = at("idf", [128, 128], F32, offset=MM["idf"])
    ones = at("ones", [128, 1], BF16, offset=MM["ones"])
    ones1 = at("ones1", [1, 128], BF16, offset=MM["ones1"])
    y_l = at("y_l", [128, 256], F32, offset=MM["y_l"])
    dlt = at("dlt", [128, 256], F32, offset=MM["dlt"])
    o_n = at("o_n", [128, 256], BF16, offset=MM["o_n"])
    ln = at("ln", [128, 16], F32, offset=MM["ln"])
    mg = at("mg", [16, 16], F32, offset=MM["mg"])
    sg = at("sg", [16, 16], F32, offset=MM["sg"])
    og = at("og", [128, 256], F32, offset=MM["og"])
    outs = [at(f"outs{i}", [128, 128], F32, offset=MM["outs"] + 512 * i) for i in range(2)] + \
           [at(f"outs{i+2}", [128, 128], F32, offset=MM["og"] + 512 * i) for i in range(2)]
    fsb = [at(f"fsb{i}", [128, 1024], F32, offset=MM["fsb"] + 4096 * i) for i in range(2)]
    dltTc = [at(f"dltTc{i}", [16, 128], F32, offset=MM["dltTc"] + 512 * i) for i in range(2)]
    rtb = [at(f"rtb{i}", [1, 128], BF16, offset=MM["dltTc"] + 256 * i) for i in range(2)]
    fixc = [at(f"fixc{i}", [16, 128], F32, offset=MM["fixc"] + 512 * i) for i in range(2)]
    xnT = at("xnT", [128, 16 * 1024], BF16, offset=MM["xnT"])
    xnTh = at("xnTh", [128, 16 * 528], BF16, offset=MM["xnTh"])

    hid = at("hid", [128, 64 * 1024], BF16, offset=MM["hid"])
    w1b = [at(f"w1b{i}", [128, 2048], BF16, offset=MM["w1b"] + 4096 * i) for i in range(2)]
    bounce = [at(f"bounce{i}", [128, 4096], BF16,
                 offset=(MM["w1b"] if i == 0 else MM["sil"])) for i in range(2)]
    w3b = [at(f"w3b{i}", [128, 2048], BF16, offset=MM["w3b"] + 4096 * i) for i in range(2)]
    sil = [at(f"sil{i}", [128, 1024], F32, offset=MM["sil"] + 4096 * i) for i in range(2)]
    h0 = [at(f"h0{i}", [128, 1024], BF16, offset=MM["h0"] + 2048 * i) for i in range(2)]

    attnT = at("attnT", [128, NH * TOK], BF16, offset=MM["attnT"])
    maskb = at("maskb", [128, 8 * 5 * 128], BF16, offset=MM["maskb"])
    k4t = at("k4t", [128, 4 * 1552], BF16, offset=MM["k4t"])
    v4 = at("v4", [128, 13 * 512], BF16, offset=MM["v4"])
    q4t = at("q4t", [128, 4 * 1024], BF16, offset=MM["q4t"])
    qg4t = at("qg4t", [128, 64], BF16, offset=MM["qg4t"])
    wbuf = [at(f"wbuf{i}", [128, 8192], BF16, offset=MM["wbuf"] + 16384 * i) for i in range(2)]
    e_sb = [at(f"e{i}", [128, 768], BF16, offset=MM["e_sb"] + 1536 * i) for i in range(2)]
    rb_sb = [at(f"rb{i}", [128, 128], F32, offset=MM["rb"] + 512 * i) for i in range(2)]
    rs_sb = [ln[:, 13:14], ln[:, 14:15]]
    egt = at("egt", [128, 128], BF16, offset=MM["egt"])

    xt = [at(f"xt{i}", [128, 2048], F32, offset=MM["xt"] + 8192 * i) for i in range(2)]
    sq = at("sq", [128, 2048], F32, offset=MM["sq"])
    xn = [at(f"xn{i}", [128, 2048], BF16, offset=MM["xn"] + 4096 * i) for i in range(2)]

    wo_sb = [at(f"wo{i}", [128, 2048], BF16, offset=MM["wo_sb"] + 4096 * i) for i in range(2)]
    y_sb = [at(f"ysb{i}", [128, 1024], F32, offset=MM["y_sb"] + 4096 * i) for i in range(2)]

    mfl = at("mfl", [1, 2048], F32, offset=MM["mfl"])
    sfl = at("sfl", [1, 2048], F32, offset=MM["sfl"])
    oga = at("oga", [128, 2048], F32, offset=MM["oga"])
    tmg = at("tmg", [128, 2048], F32, offset=MM["tmg"])
    wb2 = at("wb2", [128, 2048], F32, offset=MM["wb2"])
    wbf = at("wbf", [1, 2048], BF16, offset=MM["wbf"])

    w2b = [at(f"w2b{i}", [128, 8192], BF16, offset=MM["w2b"] + 16384 * i) for i in range(2)]
    wo2 = [at(f"wo2{i}", [128, 2048], BF16, offset=MM["wo2"] + 4096 * i) for i in range(2)]
    y2 = [at(f"y2{i}", [128, 1024], F32, offset=MM["y2"] + 4096 * i) for i in range(2)]
    xrs = [at(f"xrs{i}", [128, 1024], F32, offset=MM["xrs"] + 4096 * i) for i in range(2)]

    pA = nc.alloc_psum_tensor("pA", [128, 1024], F32)
    pB = nc.alloc_psum_tensor("pB", [128, 1024], F32)
    pC = nc.alloc_psum_tensor("pC", [128, 1024], F32)
    pD = nc.alloc_psum_tensor("pD", [128, 1024], F32)
    PP = [(pA, pB), (pC, pD)]

    def xcol(dc, kl):
        if 256 <= kl < 1280:
            return xnT, dc * 1024 + (kl - 256)
        if kl < 256:
            return xnTh, dc * 528 + kl
        if kl < 1536:
            return xnTh, dc * 528 + 256 + (kl - 1280)
        return xnTh, dc * 528 + 512 + (kl - 1536)

    K_SPANS = [(0, 256), (256, 256), (512, 256), (768, 256), (1024, 256), (1280, 256), (1536, 16)]

    with ExitStack() as ctx:
        sem_names = ["LD", "WL", "PE", "SC", "DV", "ST", "CC", "YD", "PS", "PC", "S2"]
        sems = {n: ctx.enter_context(nc.semaphore(n)) for n in sem_names}
        block = ctx.enter_context(nc.Block())

        def build(p: P):
            for dst, src in ((h128, h128_d), (idb, idb_d), (idf, idf_d),
                             (ones, ones_d), (ones1, ones1_d)):
                p.do("sync", lambda e, d=dst, s_=src: e.dma_start(out=d[:, :], in_=s_),
                     inc="LD", inc_n=16)
            p.do("sync", lambda e: e.dma_start(
                out=maskb[:, :].rearrange("p (qt s f) -> p qt s f", qt=8, s=5),
                in_=mask_d.rearrange("qt s p f -> p qt s f")), inc="LD", inc_n=16)
            pre_ld = p.val("LD")
            p.do("vector", lambda e: e.memset(ln[:, 15:16], -C_SOFT), inc="DV")
            dv_cst = p.val("DV")

            ln_dv = {}
            evac_sc = {}
            tp_pe = {}
            for t in range(13):
                rows = 16 if t == 12 else 128
                buf = xt[t % 2]
                if t >= 2:
                    p.wait("sync", "DV", ln_dv[t - 2])
                src = x_glob if t == 12 else x_kv[128 * t:128 * t + 128, :]
                p.do("sync", lambda e, b=buf, s_=src, rws=rows:
                     e.dma_start(out=b[0:rws, :], in_=s_), inc="LD", inc_n=16)
                ld_now = p.val("LD")

                p.wait("vector", "LD", ld_now)
                p.do("vector", lambda e, b=buf, rws=rows: e.reduce_sum(
                    ln[0:rws, 0:1], b[0:rws, :], axis=AX.X, negate=True), inc="DV")
                p.wait("vector", "DV", p.val("DV"))
                p.wait("scalar", "LD", ld_now)
                p.do("scalar", lambda e, b=buf, rws=rows: e.activation(
                    sq[0:rws, :], b[0:rws, :], AF.Square, accum_out=ln[0:rws, 1:2]), inc="SC")
                sc_sq = p.val("SC")
                p.do("vector", lambda e, rws=rows: e.tensor_scalar_mul(
                    ln[0:rws, 2:3], ln[0:rws, 0:1], 1.0 / D), inc="DV")
                p.wait("vector", "DV", p.val("DV"))
                p.wait("vector", "SC", sc_sq)
                p.do("vector", lambda e, rws=rows: e.tensor_scalar(
                    ln[0:rws, 3:4], ln[0:rws, 1:2], 1.0 / D, 1e-5,
                    ALU.mult, ALU.add), inc="DV")
                p.wait("vector", "DV", p.val("DV"))
                p.do("vector", lambda e, rws=rows: e.tensor_mul(
                    ln[0:rws, 4:5], ln[0:rws, 2:3], ln[0:rws, 2:3]), inc="DV")
                p.wait("vector", "DV", p.val("DV"))
                p.do("vector", lambda e, rws=rows: e.tensor_sub(
                    ln[0:rws, 5:6], ln[0:rws, 3:4], ln[0:rws, 4:5]), inc="DV")
                dv_var = p.val("DV")
                p.wait("scalar", "DV", dv_var)
                p.do("scalar", lambda e, rws=rows: e.activation(
                    ln[0:rws, 6:7], ln[0:rws, 5:6], AF.Ln), inc="SC")
                p.wait("scalar", "SC", p.val("SC"))
                p.do("scalar", lambda e, rws=rows: e.activation(
                    ln[0:rws, 7:8], ln[0:rws, 6:7], AF.Exp, scale=-0.5), inc="SC")
                sc_rstd = p.val("SC")
                p.wait("vector", "SC", sc_rstd)
                p.do("vector", lambda e, rws=rows: e.tensor_scalar_mul(
                    ln[0:rws, 8:9], ln[0:rws, 2:3], ln[0:rws, 7:8]), inc="DV")
                p.wait("vector", "DV", p.val("DV"))
                if t >= 2:
                    p.wait("vector", "PE", tp_pe[t - 2])
                xnb = xn[t % 2]
                p.do("vector", lambda e, b=buf, xb=xnb, rws=rows: e.tensor_scalar(
                    xb[0:rws, :], b[0:rws, :], ln[0:rws, 7:8], ln[0:rws, 8:9],
                    ALU.mult, ALU.add), inc="DV")
                ln_dv[t] = p.val("DV")

                p.wait("tensor", "LD", pre_ld)
                p.wait("tensor", "DV", ln_dv[t])
                ps = PP[t % 2][0]
                for half in range(2):
                    if half == 1:
                        p.wait("tensor", "SC", evac_sc[(t, 0)])
                    elif t >= 2:
                        p.wait("tensor", "SC", evac_sc[(t - 2, 1)])
                    for j in range(8):
                        dc = half * 8 + j
                        dst = ps[:, 64 * j:64 * j + 64].bitcast(BF16)
                        p.do("tensor", lambda e, d=dst, xb=xnb, dcc=dc, rws=rows:
                             e.transpose(d[0:128, 0:rws],
                                         xb[0:rws, 128 * dcc:128 * dcc + 128],
                                         idb[0:rws, 0:rws]),
                             inc="PE" if j == 7 else None)
                    pe_now = p.val("PE")
                    if half == 1:
                        tp_pe[t] = pe_now
                    p.wait("scalar", "PE", pe_now)
                    src_ap = ps[:, 0:512].bitcast(BF16).rearrange("p (j k) -> p j k", j=8)
                    kl = 128 * t if t < 12 else 1536
                    ten0, col0 = xcol(0, kl)
                    stride = 1024 if ten0 is xnT else 528
                    dst_ap = ten0[:, :].rearrange("p (dc k) -> p dc k", dc=16)[
                        :, 8 * half:8 * half + 8, col0:col0 + rows]
                    p.do("scalar", lambda e, sp=src_ap, d=dst_ap, rws=rows:
                         e.copy(d, sp[:, :, 0:rws]), inc="SC")
                    evac_sc[(t, half)] = p.val("SC")
            p0_dv = p.val("DV")

            wl_done = {}
            sec_pe = {}
            nload = [0]

            def wload(src_ap, pe_release):
                i = nload[0]
                nload[0] += 1
                if i >= 2:
                    p.wait("gpsimd", "PE", sec_pe.get(i - 2, 0))
                elif pe_release:
                    p.wait("gpsimd", "PE", pe_release)
                p.do("gpsimd", lambda e, d=wbuf[i % 2], s_=src_ap:
                     e.dma_start(out=d[:, :], in_=s_), inc="WL", inc_n=16)
                wl_done[i] = p.val("WL")
                return i

            iter_dv = {}
            iter_sc = {}
            it = 0
            pieces = []
            pc_ready_wo = {}
            pc_ready_w2 = {}
            for i in range(8):
                pieces.append((wo_d[2 * i:2 * i + 2].rearrange("a p f -> p a f"),
                               wobf_d[2 * i:2 * i + 2].rearrange("a p f -> p a f"), True))
                pc_ready_wo[2 * i] = pc_ready_wo[2 * i + 1] = 16 * len(pieces)
                for dcp in (2 * i, 2 * i + 1):
                    for hp in range(2):
                        pieces.append((w2_d[dcp][:, 4096 * hp:4096 * hp + 4096],
                                       w2bf_d[dcp][:, 4096 * hp:4096 * hp + 4096], False))
                    pc_ready_w2[dcp] = 16 * len(pieces)
            pc_k = [0]

            def emit_precast(n):
                for _ in range(n):
                    k = pc_k[0]
                    if k >= len(pieces):
                        return
                    pc_k[0] += 1
                    srcap, dstap, three_d = pieces[k]
                    bb = bounce[k % 2]
                    bview = (bb[:, :].rearrange("p (a f) -> p a f", a=2)
                             if three_d else bb[:, :])
                    if k >= 2:
                        p.wait("gpsimd", "PC", 16 * (k - 1))
                    p.do("gpsimd", lambda e, d=bview, s_=srcap: e.dma_start(out=d, in_=s_),
                         inc="WL", inc_n=16)
                    wlv = p.val("WL")
                    p.wait("sync", "WL", wlv)
                    p.do("sync", lambda e, d=dstap, s_=bview: e.dma_start(out=d, in_=s_),
                         inc="PC", inc_n=16)

            for g in range(4):
                iq = wload(wq_d[g], 0)
                wq = wbuf[iq % 2]
                p.wait("tensor", "WL", wl_done[iq])
                p.wait("tensor", "SC", p.val("SC"))
                qev = {}
                for dqc in range(4):
                    lo, hi_ = PP[dqc % 2]
                    p.wait("tensor", "SC", qev.get(dqc - 2, 0))
                    for th in range(2):
                        for dc in range(16):
                            p.do("tensor", lambda e, o_=lo[:, 512 * th:512 * th + 512],
                                 l=wq[:, 2048 * dqc + 128 * dc:2048 * dqc + 128 * dc + 128],
                                 rr=xnT[:, 1024 * dc + 512 * th:1024 * dc + 512 * th + 512],
                                 s0=(dc == 0), s1=(dc == 15):
                                 e.matmul(o_, l, rr, start=s0, stop=s1),
                                 inc="PE" if dc == 15 else None)
                    for dc in range(16):
                        p.do("tensor", lambda e, o_=hi_[:, 896:912],
                             l=wq[:, 2048 * dqc + 128 * dc:2048 * dqc + 128 * dc + 128],
                             rr=xnTh[:, 528 * dc + 512:528 * dc + 528],
                             s0=(dc == 0), s1=(dc == 15):
                             e.matmul(o_, l, rr, start=s0, stop=s1),
                             inc="PE" if dc == 15 else None)
                    pe_q = p.val("PE")
                    p.wait("scalar", "PE", pe_q)
                    p.do("scalar", lambda e, lo_=lo, dq=dqc: e.copy(
                        q4t[:, 1024 * dq:1024 * dq + 1024], lo_[:, 0:1024]), inc="SC")
                    p.do("scalar", lambda e, hh=hi_, dq=dqc: e.copy(
                        qg4t[:, 16 * dq:16 * dq + 16], hh[:, 896:912]), inc="SC")
                    qev[dqc] = p.val("SC")
                sec_pe[iq] = p.val("PE")

                ik = wload(wk_d[g], 0)
                wk = wbuf[ik % 2]
                p.wait("tensor", "WL", wl_done[ik])
                p.wait("tensor", "SC", p.val("SC"))
                kev = {}
                for dkc in range(4):
                    lo, hi_ = PP[dkc % 2]
                    p.wait("tensor", "SC", kev.get(dkc - 2, 0))
                    for sp0, w in K_SPANS:
                        for dc in range(16):
                            ten, col = xcol(dc, sp0)
                            if sp0 < 1024:
                                o_ap = lo[:, sp0:sp0 + w]
                            elif sp0 < 1536:
                                o_ap = hi_[:, sp0 - 1024:sp0 - 1024 + w]
                            else:
                                o_ap = hi_[:, 896:912]
                            p.do("tensor", lambda e, o_=o_ap,
                                 l=wk[:, 2048 * dkc + 128 * dc:2048 * dkc + 128 * dc + 128],
                                 rr=ten[:, col:col + w],
                                 s0=(dc == 0), s1=(dc == 15):
                                 e.matmul(o_, l, rr, start=s0, stop=s1),
                                 inc="PE" if dc == 15 else None)
                    pe_k = p.val("PE")
                    p.wait("scalar", "PE", pe_k)
                    p.do("scalar", lambda e, lo_=lo, dk=dkc: e.copy(
                        k4t[:, 1552 * dk:1552 * dk + 1024], lo_[:, 0:1024]), inc="SC")
                    p.do("scalar", lambda e, hh=hi_, dk=dkc: e.copy(
                        k4t[:, 1552 * dk + 1024:1552 * dk + 1536], hh[:, 0:512]), inc="SC")
                    p.do("scalar", lambda e, hh=hi_, dk=dkc: e.copy(
                        k4t[:, 1552 * dk + 1536:1552 * dk + 1552], hh[:, 896:912]), inc="SC")
                    kev[dkc] = p.val("SC")
                sec_pe[ik] = p.val("PE")

                iv = wload(wv_d[g], 0)
                wv = wbuf[iv % 2]
                p.wait("tensor", "WL", wl_done[iv])
                p.wait("tensor", "SC", p.val("SC"))
                vev = {}
                for kt in range(13):
                    lo, _hi = PP[kt % 2]
                    p.wait("tensor", "SC", vev.get(kt - 2, 0))
                    rows = 16 if kt == 12 else 128
                    kl = 1536 if kt == 12 else 128 * kt
                    for dc in range(16):
                        ten, col = xcol(dc, kl)
                        p.do("tensor", lambda e, o_=lo[0:rows, 0:512],
                             l=ten[:, col:col + rows],
                             rr=wv[:, 512 * dc:512 * dc + 512],
                             s0=(dc == 0), s1=(dc == 15):
                             e.matmul(o_, l, rr, start=s0, stop=s1),
                             inc="PE" if dc == 15 else None)
                    pe_v = p.val("PE")
                    p.wait("scalar", "PE", pe_v)
                    p.do("scalar", lambda e, lo_=lo, k=kt, rws=rows: e.copy(
                        v4[0:rws, 512 * k:512 * k + 512], lo_[0:rws, 0:512]), inc="SC")
                    vev[kt] = p.val("SC")
                sec_pe[iv] = p.val("PE")
                sc_proj = p.val("SC")
                if g >= 1:
                    emit_precast(10 if g < 3 else 20)

                p.wait("tensor", "SC", sc_proj)
                p.wait("scalar", "DV", dv_cst)
                mview = maskb[:, :].rearrange("p (qt s f) -> p qt s f", qt=8, s=5)
                a_sc_exp = {}
                a_sc_exp1 = {}
                a_dv_mask = {}
                a_pe_st = {}
                a_pe_sum = {}
                a_pe_pv = {}
                a_dv_r = {}
                a_sc_rt = {}
                a_pe_rt = {}
                a_pe_rb = {}
                a_sc_rb = {}
                a_dv_mult = {}

                def st_s1(i):
                    hi2, qt = i // 8, i % 8
                    par = (it + i) % 2
                    sA, _sB = PP[par]
                    esb = e_sb[par]
                    p.wait("tensor", "SC", a_sc_exp.get(i - 2, 0))
                    for kc in range(5):
                        p.do("tensor", lambda e, o_=sA[:, 128 * kc:128 * kc + 128],
                             l=k4t[:, 1552 * hi2 + 128 * (qt + kc):1552 * hi2 + 128 * (qt + kc) + 128],
                             rr=q4t[:, 1024 * hi2 + 128 * qt:1024 * hi2 + 128 * qt + 128]:
                             e.matmul(o_, l, rr, start=True, stop=True))
                    p.do("tensor", lambda e, o_=sA[0:16, 640:768],
                         l=k4t[:, 1552 * hi2 + 1536:1552 * hi2 + 1552],
                         rr=q4t[:, 1024 * hi2 + 128 * qt:1024 * hi2 + 128 * qt + 128]:
                         e.matmul(o_, l, rr, start=True, stop=True), inc="PE")
                    a_pe_st[i] = p.val("PE")
                    p.wait("scalar", "PE", a_pe_st[i])
                    p.wait("scalar", "DV", a_dv_mask.get(i - 2, 0))
                    p.do("scalar", lambda e, s_=sA, d=esb: e.activation(
                        d[:, 128:512], s_[:, 128:512], AF.Exp, bias=ln[:, 15:16], scale=SCALING), inc="SC")
                    a_sc_exp1[i] = p.val("SC")
                    p.do("scalar", lambda e, s_=sA, d=esb: e.activation(
                        d[:, 0:128], s_[:, 0:128], AF.Exp, bias=ln[:, 15:16], scale=SCALING), inc="SC")
                    p.do("scalar", lambda e, s_=sA, d=esb: e.activation(
                        d[:, 512:640], s_[:, 512:640], AF.Exp, bias=ln[:, 15:16], scale=SCALING), inc="SC")
                    p.do("scalar", lambda e, s_=sA, d=esb: e.activation(
                        d[0:16, 640:768], s_[0:16, 640:768], AF.Exp, bias=ln[0:16, 15:16], scale=SCALING), inc="SC")
                    a_sc_exp[i] = p.val("SC")
                    p.wait("vector", "SC", a_sc_exp[i])
                    p.do("vector", lambda e, d=esb, q=qt: e.tensor_mul(
                        d[:, 0:128], d[:, 0:128], mview[:, q, 0, :]))
                    if qt == 0:
                        p.do("vector", lambda e, d=esb: e.tensor_mul(
                            d[:, 128:256], d[:, 128:256], mview[:, 0, 1, :]))
                    if qt == 7:
                        p.do("vector", lambda e, d=esb: e.tensor_mul(
                            d[:, 384:512], d[:, 384:512], mview[:, 7, 2, :]))
                    p.do("vector", lambda e, d=esb, q=qt: e.tensor_mul(
                        d[:, 512:640], d[:, 512:640], mview[:, q, 3, :]))
                    p.do("vector", lambda e, d=esb, q=qt: e.tensor_mul(
                        d[0:16, 640:768], d[0:16, 640:768], mview[0:16, q, 4, :]), inc="DV")
                    a_dv_mask[i] = p.val("DV")

                def st_s2(i):
                    hi2, qt = i // 8, i % 8
                    par = (it + i) % 2
                    _sA, sB = PP[par]
                    esb = e_sb[par]
                    free_kc = [kc for kc in (1, 2, 3)
                               if not (kc == 1 and qt == 0) and not (kc == 3 and qt == 7)]
                    mask_kc = [kc for kc in range(5) if kc not in free_kc]
                    p.wait("tensor", "SC", a_sc_exp1[i])
                    p.wait("tensor", "DV", a_dv_mult.get(i - 2, 0))
                    p.wait("tensor", "SC", a_sc_rb.get(i - 2, 0))
                    for n_, kc in enumerate(free_kc):
                        p.do("tensor", lambda e, o_=sB[:, 0:1],
                             l=esb[:, 128 * kc:128 * kc + 128], s0=(n_ == 0):
                             e.matmul(o_, l, ones[:, :], start=s0, stop=False))
                    p.wait("tensor", "DV", a_dv_mask[i])
                    for kc in mask_kc:
                        p.do("tensor", lambda e, o_=sB[:, 0:1],
                             l=esb[:, 128 * kc:128 * kc + 128]:
                             e.matmul(o_, l, ones[:, :], start=False, stop=False))
                    p.do("tensor", lambda e, o_=sB[:, 0:1], l=esb[0:16, 640:768]:
                         e.matmul(o_, l, ones[0:16, :], start=False, stop=True), inc="PE")
                    a_pe_sum[i] = p.val("PE")
                    for n_, kc in enumerate(free_kc + mask_kc):
                        p.do("tensor", lambda e, o_=sB[:, 512:640],
                             l=v4[:, 512 * (qt + kc) + 128 * hi2:512 * (qt + kc) + 128 * hi2 + 128],
                             rr=esb[:, 128 * kc:128 * kc + 128], s0=(n_ == 0):
                             e.matmul(o_, l, rr, start=s0, stop=False))
                    p.do("tensor", lambda e, o_=sB[:, 512:640],
                         l=v4[0:16, 512 * 12 + 128 * hi2:512 * 12 + 128 * hi2 + 128],
                         rr=esb[0:16, 640:768]:
                         e.matmul(o_, l, rr, start=False, stop=True), inc="PE")
                    a_pe_pv[i] = p.val("PE")
                    p.wait("vector", "PE", a_pe_sum[i])
                    p.do("vector", lambda e, s_=sB, d=rs_sb[par]:
                         e.reciprocal(d, s_[:, 0:1]), inc="DV")
                    p.wait("vector", "DV", p.val("DV"))
                    a_dv_r[i] = p.val("DV")

                def st_s3(i):
                    hi2, qt = i // 8, i % 8
                    h = 4 * g + hi2
                    par = (it + i) % 2
                    _sA, sB = PP[par]
                    p.wait("tensor", "DV", a_dv_r[i])
                    p.do("tensor", lambda e, s_=sB, d=rs_sb[par]: e.transpose(
                        s_[0:1, 4:132], d, idf[:, :]), inc="PE")
                    a_pe_rt[i] = p.val("PE")
                    p.wait("scalar", "PE", a_pe_rt[i])
                    p.do("scalar", lambda e, s_=sB, d=rtb[par]: e.copy(
                        d[0:1, :], s_[0:1, 4:132]), inc="SC")
                    a_sc_rt[i] = p.val("SC")
                    p.wait("tensor", "SC", a_sc_rt[i])
                    p.do("tensor", lambda e, s_=sB, d=rtb[par]: e.matmul(
                        s_[:, 256:384], ones1[:, :], d[0:1, :],
                        start=True, stop=True), inc="PE")
                    a_pe_rb[i] = p.val("PE")
                    p.wait("scalar", "PE", a_pe_rb[i])
                    p.wait("scalar", "DV", a_dv_mult.get(i - 2, 0))
                    p.do("scalar", lambda e, s_=sB, d=rb_sb[par]: e.copy(
                        d[:, :], s_[:, 256:384]), inc="SC")
                    a_sc_rb[i] = p.val("SC")
                    p.wait("vector", "SC", a_sc_rb[i])
                    p.wait("vector", "PE", a_pe_pv[i])
                    p.do("vector", lambda e, s_=sB, rbs=rb_sb[par], hh=h, q=qt:
                         e.tensor_mul(attnT[:, 1024 * hh + 128 * q:1024 * hh + 128 * q + 128],
                                      s_[:, 512:640], rbs[:, :]), inc="DV")
                    a_dv_mult[i] = p.val("DV")

                for os_ in range(34):
                    if os_ < 32:
                        st_s1(os_)
                    if 1 <= os_ <= 32:
                        st_s2(os_ - 1)
                    if os_ >= 2:
                        st_s3(os_ - 2)
                it += 32
                iter_dv[it - 1] = p.val("DV")
                iter_sc[it - 1] = p.val("SC")

                g_pe_sg = {}
                g_sc_eg = {}
                g_dv_sum = {}
                g_pe_egt = {}
                g_sc_egt = {}
                g_pe_og = {}
                g_sc_og = {}

                def glob_s1(hh2):
                    h = 4 * g + hh2
                    par = (it + hh2) % 2
                    hp = hh2 % 2
                    sA = PP[par][0]
                    esb = e_sb[par]
                    if hh2 < 2:
                        p.wait("tensor", "SC", p.val("SC"))
                        p.wait("tensor", "DV", p.val("DV"))
                    else:
                        p.wait("tensor", "SC", g_sc_eg[hh2 - 2])
                    for th in range(2):
                        p.do("tensor", lambda e, o_=sA[0:16, 512 * th:512 * th + 512],
                             l=qg4t[:, 16 * hh2:16 * hh2 + 16],
                             rr=k4t[:, 1552 * hh2 + OWN0 + 512 * th:1552 * hh2 + OWN0 + 512 * th + 512]:
                             e.matmul(o_, l, rr, start=True, stop=True),
                             inc="PE" if th == 1 else None)
                    g_pe_sg[hh2] = p.val("PE")
                    p.wait("vector", "PE", g_pe_sg[hh2])
                    p.do("vector", lambda e, s_=sA, c=9 + hp: e.reduce_max(
                        ln[0:16, c:c + 1], s_[0:16, 0:1024], axis=AX.X, negate=True), inc="DV")
                    p.wait("vector", "DV", p.val("DV"))
                    p.do("vector", lambda e, c=9 + hp, d=11 + hp: e.tensor_scalar_mul(
                        ln[0:16, d:d + 1], ln[0:16, c:c + 1], SCALING), inc="DV")
                    p.wait("vector", "DV", p.val("DV"))
                    p.do("vector", lambda e, hh=h, d=11 + hp: e.tensor_scalar_mul(
                        mg[0:16, hh:hh + 1], ln[0:16, d:d + 1], -1.0), inc="DV")
                    dv_mg = p.val("DV")
                    p.wait("scalar", "DV", dv_mg)
                    p.wait("scalar", "PE", g_pe_sg[hh2])
                    if hh2 >= 2:
                        p.wait("scalar", "PE", g_pe_egt[hh2 - 2])
                        p.wait("scalar", "DV", g_dv_sum[hh2 - 2])
                    else:
                        p.wait("scalar", "DV", p.val("DV"))
                    egtail = rb_sb[par][:, :].bitcast(BF16)
                    p.do("scalar", lambda e, s_=sA, d=esb, bi=11 + hp: e.activation(
                        d[0:16, 0:768], s_[0:16, 0:768], AF.Exp,
                        bias=ln[0:16, bi:bi + 1], scale=SCALING), inc="SC")
                    p.do("scalar", lambda e, s_=sA, d=egtail, bi=11 + hp: e.activation(
                        d[0:16, 0:256], s_[0:16, 768:1024], AF.Exp,
                        bias=ln[0:16, bi:bi + 1], scale=SCALING), inc="SC")
                    g_sc_eg[hh2] = p.val("SC")
                    p.wait("vector", "SC", g_sc_eg[hh2])
                    p.do("vector", lambda e, d=esb, c=13 + hp: e.reduce_sum(
                        ln[0:16, c:c + 1], d[0:16, 0:768], axis=AX.X), inc="DV")
                    p.wait("vector", "DV", p.val("DV"))
                    p.do("vector", lambda e, d=egtail, c=9 + hp: e.reduce_sum(
                        ln[0:16, c:c + 1], d[0:16, 0:256], axis=AX.X), inc="DV")
                    p.wait("vector", "DV", p.val("DV"))
                    p.do("vector", lambda e, hh=h, c1=13 + hp, c2=9 + hp: e.tensor_add(
                        sg[0:16, hh:hh + 1], ln[0:16, c1:c1 + 1], ln[0:16, c2:c2 + 1]), inc="DV")
                    g_dv_sum[hh2] = p.val("DV")

                def glob_s2(hh2):
                    h = 4 * g + hh2
                    par = (it + hh2) % 2
                    sB = PP[par][1]
                    esb = e_sb[par]
                    p.wait("tensor", "SC", g_sc_eg[hh2])
                    if hh2 >= 2:
                        p.wait("tensor", "SC", g_sc_og[hh2 - 2])
                    egtail = rb_sb[par][:, :].bitcast(BF16)
                    for j in range(8):
                        if j < 6:
                            srcj = esb[0:16, 128 * j:128 * j + 128]
                        else:
                            srcj = egtail[0:16, 128 * (j - 6):128 * (j - 6) + 128]
                        dstj = sB[:, 8 * j:8 * j + 8].bitcast(BF16)
                        p.do("tensor", lambda e, d=dstj, s_=srcj:
                             e.transpose(d[0:128, 0:16], s_, idb[0:16, 0:16]),
                             inc="PE" if j == 7 else None)
                    g_pe_egt[hh2] = p.val("PE")
                    p.wait("scalar", "PE", g_pe_egt[hh2])
                    p.do("scalar", lambda e, s_=sB: e.copy(
                        egt[:, :], s_[:, 0:64].bitcast(BF16)), inc="SC")
                    g_sc_egt[hh2] = p.val("SC")
                    p.wait("tensor", "SC", g_sc_egt[hh2])
                    for j in range(8):
                        p.do("tensor", lambda e, o_=sB[:, 512:528],
                             l=v4[:, 512 * (2 + j) + 128 * hh2:512 * (2 + j) + 128 * hh2 + 128],
                             rr=egt[:, 16 * j:16 * j + 16], s0=(j == 0), s1=(j == 7):
                             e.matmul(o_, l, rr, start=s0, stop=s1),
                             inc="PE" if j == 7 else None)
                    g_pe_og[hh2] = p.val("PE")
                    p.wait("scalar", "PE", g_pe_og[hh2])
                    p.do("scalar", lambda e, s_=sB, hh=h: e.copy(
                        og[:, 16 * hh:16 * hh + 16], s_[:, 512:528]), inc="SC")
                    g_sc_og[hh2] = p.val("SC")

                for gs in range(5):
                    if gs < 4:
                        glob_s1(gs)
                    if gs >= 1:
                        glob_s2(gs - 1)
                it += 4
                iter_dv[it - 1] = p.val("DV")
                iter_sc[it - 1] = p.val("SC")

            pa_pe = p.val("PE")
            pa_sc = p.val("SC")
            pa_dv = p.val("DV")

            p.wait("gpsimd", "PC", 16 * len(pieces))
            pf_wl = {}
            for hcp in range(2):
                p.do("gpsimd", lambda e, d=w1b[hcp % 2], s_=w1_d[hcp]:
                     e.dma_start(out=d[:, :], in_=s_), inc="WL", inc_n=16)
                p.do("gpsimd", lambda e, d=w3b[hcp % 2], s_=w3_d[hcp]:
                     e.dma_start(out=d[:, :], in_=s_), inc="WL", inc_n=16)
                pf_wl[hcp] = p.val("WL")

            p.wait("gpsimd", "SC", pa_sc)
            p.wait("gpsimd", "DV", pa_dv)
            p.do("gpsimd", lambda e: e.dma_start(out=part_src.ap()[0:128, :], in_=og[:, :]),
                 inc="PS", inc_n=16)
            p.do("gpsimd", lambda e: e.dma_start(
                out=part_src.ap()[128, :].rearrange("(q h) -> q h", q=16),
                in_=mg[0:16, :]), inc="PS", inc_n=16)
            p.do("gpsimd", lambda e: e.dma_start(
                out=part_src.ap()[129, :].rearrange("(q h) -> q h", q=16),
                in_=sg[0:16, :]), inc="PS", inc_n=16)
            p.wait("gpsimd", "PS", p.val("PS"))
            p.do("gpsimd", lambda e: e.collective_compute(
                "AllGather", ALU.bypass, replica_groups=[list(range(NCORES))],
                ins=[part_src.ap()], outs=[ag_out.ap()]), inc="CC")
            cc_done = p.val("CC")

            wo_pe = {}
            y_ev = {}
            for dc in range(16):
                if dc == 8:
                    p.wait("gpsimd", "CC", cc_done)
                    agv = ag_out.ap().rearrange("(r y) c -> r y c", y=130)
                    p.do("gpsimd", lambda e: e.dma_start(
                        out=mfl[0:1, :].rearrange("p (r x) -> p r x", r=8),
                        in_=agv[:, 128, :].unsqueeze(0)), inc="WL", inc_n=16)
                    p.do("gpsimd", lambda e: e.dma_start(
                        out=sfl[0:1, :].rearrange("p (r x) -> p r x", r=8),
                        in_=agv[:, 129, :].unsqueeze(0)), inc="WL", inc_n=16)
                    p.do("gpsimd", lambda e: e.dma_start(
                        out=oga[:, :].rearrange("p (r c) -> p r c", r=8),
                        in_=agv[:, 0:128, :].rearrange("r p c -> p r c")), inc="WL", inc_n=16)
                    ld_mg = p.val("WL")
                    p.wait("vector", "WL", ld_mg)
                    p.wait("vector", "PE", pa_pe)
                    p.do("vector", lambda e: e.tensor_reduce(
                        tmg[0:1, 0:256], mfl[0:1, :].rearrange("p (r x) -> p x r", r=8),
                        axis=AX.X, op=ALU.max), inc="DV")
                    p.wait("vector", "DV", p.val("DV"))
                    p.do("vector", lambda e: e.tensor_sub(
                        wb2[0:1, 0:2048].rearrange("p (r x) -> p r x", r=8),
                        mfl[0:1, :].rearrange("p (r x) -> p r x", r=8),
                        tmg[0:1, 0:256].unsqueeze(1).broadcast_to([1, 8, 256])), inc="DV")
                    dv_diff = p.val("DV")
                    p.wait("scalar", "DV", dv_diff)
                    p.wait("scalar", "PE", pa_pe)
                    p.do("scalar", lambda e: e.activation(
                        wb2[0:1, 0:2048], wb2[0:1, 0:2048], AF.Exp), inc="SC")
                    sc_alpha = p.val("SC")
                    p.wait("vector", "SC", sc_alpha)
                    p.do("vector", lambda e: e.tensor_mul(sfl[0:1, :], sfl[0:1, :], wb2[0:1, 0:2048]), inc="DV")
                    p.wait("vector", "DV", p.val("DV"))
                    p.do("vector", lambda e: e.tensor_reduce(
                        tmg[0:1, 256:512], sfl[0:1, :].rearrange("p (r x) -> p x r", r=8),
                        axis=AX.X, op=ALU.add), inc="DV")
                    p.wait("vector", "DV", p.val("DV"))
                    p.do("vector", lambda e: e.reciprocal(tmg[0:1, 512:768], tmg[0:1, 256:512]), inc="DV")
                    p.wait("vector", "DV", p.val("DV"))
                    p.do("vector", lambda e: e.tensor_mul(
                        wb2[0:1, 0:2048].rearrange("p (r x) -> p r x", r=8),
                        wb2[0:1, 0:2048].rearrange("p (r x) -> p r x", r=8),
                        tmg[0:1, 512:768].unsqueeze(1).broadcast_to([1, 8, 256])), inc="DV")
                    dv_w = p.val("DV")
                    p.wait("scalar", "DV", dv_w)
                    p.do("scalar", lambda e: e.copy(wbf[0:1, :], wb2[0:1, 0:2048]), inc="SC")
                    sc_wbf = p.val("SC")

                if dc >= 2:
                    p.wait("sync", "PE", wo_pe[dc - 2])
                else:
                    p.wait("sync", "PE", pa_pe)
                p.wait("sync", "PC", pc_ready_wo[dc])
                p.do("sync", lambda e, d=wo_sb[dc % 2], s_=wobf_d[dc]:
                     e.dma_start(out=d[:, :], in_=s_), inc="LD", inc_n=16)
                wl_wo = p.val("LD")
                lo, hi_ = PP[dc % 2]
                p.wait("tensor", "LD", wl_wo)
                p.wait("tensor", "SC", y_ev.get(dc - 2, 0))
                if dc == 0:
                    p.wait("tensor", "SC", pa_sc)
                    p.wait("tensor", "DV", pa_dv)
                wo_t = wo_sb[dc % 2]
                for th in range(2):
                    for dvc in range(16):
                        p.do("tensor", lambda e, o_=lo[:, 512 * th:512 * th + 512],
                             l=wo_t[:, 128 * dvc:128 * dvc + 128],
                             rr=attnT[:, 1024 * dvc + 512 * th:1024 * dvc + 512 * th + 512],
                             s0=(dvc == 0), s1=(dvc == 15):
                             e.matmul(o_, l, rr, start=s0, stop=s1),
                             inc="PE" if dvc == 15 else None)
                for dvc in range(16):
                    p.do("tensor", lambda e, o_=hi_[:, 896:912],
                         l=wo_t[:, 128 * dvc:128 * dvc + 128],
                         rr=attnT[:, 1024 * dvc:1024 * dvc + 16],
                         s0=(dvc == 0), s1=(dvc == 15):
                         e.matmul(o_, l, rr, start=s0, stop=s1),
                         inc="PE" if dvc == 15 else None)
                wo_pe[dc] = p.val("PE")
                pe_y = p.val("PE")
                p.wait("scalar", "PE", pe_y)
                if dc >= 2:
                    p.wait("scalar", "YD", 16 * (dc - 1))
                p.do("scalar", lambda e, d=y_sb[dc % 2], lo_=lo:
                     e.copy(d[:, :], lo_[:, 0:1024]), inc="SC")
                p.do("scalar", lambda e, hh=hi_, d=dc: e.copy(
                    y_l[:, 16 * d:16 * d + 16], hh[:, 896:912]), inc="SC")
                sc_yev = p.val("SC")
                y_ev[dc] = sc_yev
                p.wait("sync", "SC", sc_yev)
                p.do("sync", lambda e, d=dc, s_=y_sb[dc % 2]: e.dma_start(
                    out=y_scr[128 * d:128 * d + 128, :], in_=s_[:, :]), inc="YD", inc_n=16)
            py_pe = p.val("PE")
            py_sc = p.val("SC")
            py_yd = p.val("YD")

            p.wait("sync", "DV", dv_w)
            p.do("sync", lambda e: e.dma_start(out=mrow_d.unsqueeze(0), in_=wb2[0:1, 0:2048]),
                 inc="LD", inc_n=16)
            p.wait("sync", "LD", p.val("LD"))
            p.do("sync", lambda e: e.dma_start(
                out=tmg[:, :], in_=mrow_d.unsqueeze(0).broadcast_to([128, 2048])),
                inc="LD", inc_n=16)
            ld_bc = p.val("LD")
            p.wait("vector", "LD", ld_bc)
            p.do("vector", lambda e: e.tensor_mul(wb2[:, :], oga[:, :], tmg[:, :]), inc="DV")
            p.wait("vector", "DV", p.val("DV"))
            p.do("vector", lambda e: e.tensor_reduce(
                oga[:, 0:256], wb2[:, :].rearrange("p (r x) -> p x r", r=8),
                axis=AX.X, op=ALU.add), inc="DV")
            p.wait("vector", "DV", p.val("DV"))
            dv_on = p.val("DV")
            p.wait("scalar", "DV", dv_on)
            p.do("scalar", lambda e: e.copy(o_n[:, 0:256], oga[:, 0:256]), inc="SC")
            sc_on = p.val("SC")
            p.wait("vector", "SC", sc_on)
            p.do("vector", lambda e: e.memset(ln[0:1, 12:13], 0.0), inc="DV")
            mg_dv = p.val("DV")
            mg_sc = p.val("SC")

            MERGE_SLOTS = set()
            for _s in range(64):
                lo_b = MM["hid"] + 2048 * _s
                hi_b = lo_b + 2048
                for (r0, r1) in ((MM["mfl"], MM["mfl"] + 32768),
                                 (MM["wb2"], MM["wb2"] + 12288)):
                    if lo_b < r1 and hi_b > r0:
                        MERGE_SLOTS.add(_s)
            slot = [-1] * 64
            freelist = [(s, 0) for s in range(64)]
            chunk_prod = {}
            HSLOT = lambda sid: (hid[:, 1024 * sid:1024 * sid + 1024] if sid < 64
                                 else h0[sid - 64][:, :])
            pe_h1_last = [0]
            bfly_pend = []
            mg_dv_ref = [mg_dv]

            def bfly(pairs):
                for (c1, c2) in pairs:
                    s1, s2 = slot[c1], slot[c2]
                    if len(freelist) >= 2:
                        d1, _ = freelist.pop()
                        d2, _ = freelist.pop()
                        if d1 in MERGE_SLOTS or d2 in MERGE_SLOTS:
                            p.wait("vector", "DV", mg_dv_ref[0])
                    else:
                        d1, d2 = 64, 65
                        p.wait("vector", "PE", pe_h1_last[0])
                    for cch in (c1, c2):
                        eng_, v = chunk_prod[cch]
                        if eng_ == "SC":
                            p.wait("vector", "SC", v)
                    p.do("vector", lambda e, a=HSLOT(s1), b2=HSLOT(s2), d=HSLOT(d1):
                         e.tensor_add(d, a, b2))
                    p.do("vector", lambda e, a=HSLOT(s1), b2=HSLOT(s2), d=HSLOT(d2):
                         e.tensor_sub(d, a, b2), inc="DV")
                    v = p.val("DV")
                    chunk_prod[c1] = ("DV", v)
                    chunk_prod[c2] = ("DV", v)
                    freelist.append((s1, v))
                    freelist.append((s2, v))
                    slot[c1], slot[c2] = d1, d2

            pf_bar_dv = mg_dv
            w13_pe = {}
            pf_sc_sil = {}
            pf_dv_h0 = {}
            pf_pe_h1 = {}
            pf_ev = {}
            pf_pe_up = {}
            for hc in range(65):
                if hc < 64:
                    if hc >= 2:
                        p.wait("gpsimd", "PE", w13_pe[hc - 2])
                        p.do("gpsimd", lambda e, d=w1b[hc % 2], s_=w1_d[hc]:
                             e.dma_start(out=d[:, :], in_=s_), inc="WL", inc_n=16)
                        p.do("gpsimd", lambda e, d=w3b[hc % 2], s_=w3_d[hc]:
                             e.dma_start(out=d[:, :], in_=s_), inc="WL", inc_n=16)
                        wl13 = p.val("WL")
                    else:
                        wl13 = pf_wl[hc]
                    par = hc % 2
                    up_ps, gt_ps = PP[par]
                    w1t, w3t = w1b[par], w3b[par]
                    p.wait("tensor", "WL", wl13)
                    p.wait("tensor", "SC", pf_sc_sil.get(hc - 2, 0))
                    for th in range(2):
                        for dc in range(16):
                            p.do("tensor", lambda e, o_=up_ps[:, 512 * th:512 * th + 512],
                                 l=w1t[:, 128 * dc:128 * dc + 128],
                                 rr=xnT[:, 1024 * dc + 512 * th:1024 * dc + 512 * th + 512],
                                 s0=(dc == 0), s1=(dc == 15):
                                 e.matmul(o_, l, rr, start=s0, stop=s1),
                                 inc="PE" if (dc == 15 and th == 1) else None)
                    pf_pe_up[hc] = p.val("PE")
                    if hc == 0:
                        p.wait("scalar", "PC", 16 * len(pieces))
                if hc >= 1:
                    j = hc - 1
                    parj = j % 2
                    gtj = PP[parj][1]
                    h0j = h0[parj]
                    p.wait("tensor", "DV", pf_dv_h0[j])
                    for th in range(2):
                        p.do("tensor", lambda e, o_=gtj[:, 512 * th:512 * th + 512],
                             rr=h0j[:, 512 * th:512 * th + 512]:
                             e.matmul(o_, h128[:, :], rr, start=True, stop=True),
                             inc="PE" if th == 1 else None)
                    pf_pe_h1[j] = p.val("PE")
                    pe_h1_last[0] = pf_pe_h1[j]
                    sid, dvfree = freelist.pop()
                    slot[j] = sid
                    p.wait("scalar", "DV", dvfree)
                    if sid in MERGE_SLOTS:
                        p.wait("scalar", "DV", mg_dv)
                    p.wait("scalar", "PE", pf_pe_h1[j])
                    p.do("scalar", lambda e, s_=gtj, d=HSLOT(sid): e.activation(
                        d, s_[:, 0:1024], AF.Copy, scale=FWHT_SCALE), inc="SC")
                    chunk_prod[j] = ("SC", p.val("SC"))
                    pf_ev[j] = p.val("SC")
                    for bb in range(5):
                        mask_ = (1 << (bb + 1)) - 1
                        if (j & mask_) == mask_:
                            base = j & ~mask_
                            bfly_pend.extend(
                                (c, c ^ (1 << bb))
                                for c in range(base, base + (1 << (bb + 1)))
                                if not (c >> bb) & 1)
                if hc < 64:
                    p.wait("tensor", "SC", pf_ev.get(hc - 2, 0))
                    for th in range(2):
                        for dc in range(16):
                            p.do("tensor", lambda e, o_=gt_ps[:, 512 * th:512 * th + 512],
                                 l=w3t[:, 128 * dc:128 * dc + 128],
                                 rr=xnT[:, 1024 * dc + 512 * th:1024 * dc + 512 * th + 512],
                                 s0=(dc == 0), s1=(dc == 15):
                                 e.matmul(o_, l, rr, start=s0, stop=s1),
                                 inc="PE" if (dc == 15 and th == 1) else None)
                    w13_pe[hc] = p.val("PE")
                    pe_gate = p.val("PE")
                    p.wait("scalar", "PE", pf_pe_up[hc])
                    p.wait("scalar", "DV", pf_dv_h0.get(hc - 2, 0))
                    slb = sil[par]
                    p.do("scalar", lambda e, s_=up_ps, d=slb: e.activation(
                        d[:, 0:512], s_[:, 0:512], AF.Silu), inc="SC")
                    p.do("scalar", lambda e, s_=up_ps, d=slb: e.activation(
                        d[:, 512:1024], s_[:, 512:1024], AF.Silu), inc="SC")
                    pf_sc_sil[hc] = p.val("SC")
                    p.wait("vector", "SC", pf_sc_sil[hc])
                    p.wait("vector", "PE", pe_gate)
                    h0b = h0[par]
                    p.do("vector", lambda e, s_=gt_ps, d=h0b, sl=slb:
                         e.tensor_mul(d[:, 0:512], s_[:, 0:512], sl[:, 0:512]))
                    p.do("vector", lambda e, s_=gt_ps, d=h0b, sl=slb:
                         e.tensor_mul(d[:, 512:1024], s_[:, 512:1024], sl[:, 512:1024]), inc="DV")
                    pf_dv_h0[hc] = p.val("DV")
                    for _ in range(min(6, len(bfly_pend))):
                        bfly([bfly_pend.pop(0)])
            tail = [(62, 63), (60, 62), (61, 63),
                    (56, 60), (57, 61), (58, 62), (59, 63)]
            for k in range(8):
                tail += [(48 + k, 56 + k), (32 + k, 48 + k), (40 + k, 56 + k),
                         (k, 32 + k), (8 + k, 40 + k), (16 + k, 48 + k),
                         (24 + k, 56 + k)]
            pend_set = set(bfly_pend)
            done_first = [pr for pr in tail if pr in pend_set]
            rest = [pr for pr in bfly_pend if pr not in set(done_first)]
            b5 = [pr for pr in tail if pr not in pend_set]
            for pr in rest:
                bfly([pr])
            for pr in tail:
                if pr in pend_set or pr in b5:
                    bfly([pr])
            bfly_pend.clear()
            pf_dv = p.val("DV")
            pf_sc = p.val("SC")

            w2_pe = {}
            xr_ld = {}
            fadd_dv = {}
            dltsub_dv = {}
            ep_dv = {}
            ep_sc = {}
            pd_pe_f = {}
            pd_pe_yg = {}
            st_of = {}
            pf_last_pe = w13_pe[63]
            b5order = []
            for k in range(8):
                b5order += [k, 32 + k, 8 + k, 40 + k, 16 + k, 48 + k,
                            24 + k, 56 + k]
            for dc in range(17):
                if dc < 16:
                    if dc >= 2:
                        p.wait("gpsimd", "PE", pd_pe_yg[dc - 2])
                        p.wait("gpsimd", "DV", fadd_dv[dc - 2])
                    else:
                        p.wait("gpsimd", "PE", pf_last_pe)
                        p.wait("gpsimd", "DV", pf_dv_h0[63])
                    p.wait("gpsimd", "PC", pc_ready_w2[dc])
                    p.do("gpsimd", lambda e, d=w2b[dc % 2], s_=w2bf_d[dc]:
                         e.dma_start(out=d[:, :], in_=s_), inc="WL", inc_n=16)
                    p.wait("gpsimd", "PC", pc_ready_wo[dc])
                    p.do("gpsimd", lambda e, d=wo2[dc % 2], s_=wobf_d[dc]:
                         e.dma_start(out=d[:, :], in_=s_), inc="WL", inc_n=16)
                    wl2 = p.val("WL")
                    p.wait("gpsimd", "YD", py_yd)
                    p.do("gpsimd", lambda e, d=y2[dc % 2], dcc=dc: e.dma_start(
                        out=d[:, :], in_=y_scr[128 * dcc:128 * dcc + 128, :]), inc="WL", inc_n=16)
                    ld_pd = p.val("WL")
                    if dc < 2:
                        p.wait("sync", "DV", pf_dv_h0[63])
                    else:
                        p.wait("sync", "DV", ep_dv.get(dc - 2, 0))
                    p.do("sync", lambda e, d=xrs[dc % 2], dcc=dc: e.dma_start(
                        out=d[:, :].rearrange("p (t c) -> p t c", t=8),
                        in_=x_kv[256:1280, 128 * dcc:128 * dcc + 128]
                        .rearrange("(t p) c -> p t c", t=8)),
                        inc="LD", inc_n=16)
                    xr_ld[dc] = p.val("LD")
                    par = dc % 2
                    f_ps, aux_ps = PP[par]
                    w2t = w2b[par]
                    p.wait("tensor", "WL", wl2)
                    p.wait("tensor", "DV", fadd_dv.get(dc - 2, 0))
                    if dc == 0:
                        p.wait("tensor", "SC", pf_sc)
                        p.wait("tensor", "DV", mg_dv)
                    hcorder = b5order if dc == 0 else list(range(64))
                    last_dv_waited = 0
                    for th in range(2):
                        for ii, hc in enumerate(hcorder):
                            if dc == 0 and th == 0:
                                v = chunk_prod[hc][1]
                                if v > last_dv_waited:
                                    p.wait("tensor", "DV", v)
                                    last_dv_waited = v
                            p.do("tensor", lambda e, o_=f_ps[:, 512 * th:512 * th + 512],
                                 l=w2t[:, 128 * hc:128 * hc + 128],
                                 rr=HSLOT(slot[hc])[:, 512 * th:512 * th + 512],
                                 s0=(ii == 0), s1=(ii == 63):
                                 e.matmul(o_, l, rr, start=s0, stop=s1),
                                 inc="PE" if (ii == 63 and th == 1) else None)
                    w2_pe[dc] = p.val("PE")
                    pd_pe_f[dc] = p.val("PE")
                    p.wait("tensor", "DV", ep_dv.get(dc - 2, 0))
                    p.wait("tensor", "SC", ep_sc.get(dc - 2, 0))
                    wo_t = wo2[par]
                    for dvc in range(16):
                        p.do("tensor", lambda e, o_=aux_ps[:, 896:912],
                             l=wo_t[:, 128 * dvc:128 * dvc + 128],
                             rr=o_n[:, 16 * dvc:16 * dvc + 16],
                             s0=(dvc == 0), s1=(dvc == 15):
                             e.matmul(o_, l, rr, start=s0, stop=s1),
                             inc="PE" if dvc == 15 else None)
                    pd_pe_yg[dc] = p.val("PE")
                    p.wait("vector", "PE", pd_pe_f[dc])
                    p.wait("vector", "WL", ld_pd)
                    fs = fsb[par]
                    p.do("vector", lambda e, s_=f_ps, d=fs, y_=y2[par]:
                         e.tensor_add(d[:, 0:512], s_[:, 0:512], y_[:, 0:512]))
                    p.do("vector", lambda e, s_=f_ps, d=fs, y_=y2[par]:
                         e.tensor_add(d[:, 512:1024], s_[:, 512:1024], y_[:, 512:1024]), inc="DV")
                    fadd_dv[dc] = p.val("DV")
                    p.wait("vector", "PE", pd_pe_yg[dc])
                    p.do("vector", lambda e, s_=aux_ps, dcc=dc: e.tensor_sub(
                        dlt[:, 16 * dcc:16 * dcc + 16], s_[:, 896:912],
                        y_l[:, 16 * dcc:16 * dcc + 16]), inc="DV")
                    dltsub_dv[dc] = p.val("DV")
                if dc >= 1:
                    j = dc - 1
                    parj = j % 2
                    auxj = PP[parj][1]
                    fsj = fsb[parj]
                    p.wait("tensor", "DV", dltsub_dv[j])
                    p.do("tensor", lambda e, o_=auxj[0:16, 256:384], dcc=j: e.transpose(
                        o_, dlt[:, 16 * dcc:16 * dcc + 16], idf[:, :]), inc="PE")
                    pe_dt = p.val("PE")
                    p.wait("scalar", "PE", pe_dt)
                    p.do("scalar", lambda e, d=dltTc[parj], s_=auxj: e.copy(
                        d[0:16, :], s_[0:16, 256:384]), inc="SC")
                    sc_dt = p.val("SC")
                    p.wait("tensor", "DV", fadd_dv[j])
                    p.wait("tensor", "SC", sc_dt)
                    for tt in range(8):
                        tp = auxj[:, 0:128] if tt % 2 == 0 else auxj[:, 512:640]
                        if tt >= 2:
                            p.wait("tensor", "DV", ep_dv.get((j, tt - 2), 0))
                        p.do("tensor", lambda e, o_=tp, s_=fsj[:, 128 * tt:128 * tt + 128]:
                             e.transpose(o_, s_, idf[:, :]), inc="PE")
                        pe_tp = p.val("PE")
                        ob = outs[tt % 4]
                        qrole, qsem = ("scalar", "ST") if tt % 2 == 0 else ("sync", "S2")
                        p.wait("vector", "PE", pe_tp)
                        if tt >= 4:
                            p.wait("vector", qsem, st_of.get((j, tt - 4), 0))
                        elif j >= 1:
                            p.wait("vector", qsem, st_of.get((j - 1, 4 + tt), 0))
                        p.wait("vector", "LD", xr_ld[j])
                        p.do("vector", lambda e, o_=ob, s_=tp,
                             x_=xrs[parj][:, 128 * tt:128 * tt + 128]:
                             e.tensor_add(o_[:, :], s_, x_), inc="DV")
                        dv_o = p.val("DV")
                        ep_dv[(j, tt)] = dv_o
                        p.wait(qrole, "DV", dv_o)
                        p.do(qrole, lambda e, o_=ob, dcc=j, t_=tt: e.dma_start(
                            out=o_main[128 * t_:128 * t_ + 128, 128 * dcc:128 * dcc + 128],
                            in_=o_[:, :]), inc=qsem, inc_n=16)
                        st_of[(j, tt)] = p.val(qsem)
                        if tt == 0:
                            if j >= 2:
                                p.wait("vector", "ST", st_of.get(("fix", j - 2), 0))
                            p.do("vector", lambda e, o_=ob, d=fixc[j % 2], dt_=dltTc[parj]:
                                 e.tensor_add(d[0:16, :], o_[0:16, :], dt_[0:16, :]), inc="DV")
                            dv_fx = p.val("DV")
                            p.wait("scalar", "DV", dv_fx)
                            p.do("scalar", lambda e, d=fixc[j % 2], dcc=j: e.dma_start(
                                out=o_fix[0:16, 128 * dcc:128 * dcc + 128], in_=d[0:16, :]),
                                inc="ST", inc_n=16)
                            st_of[("fix", j)] = p.val("ST")
                    ep_dv[j] = p.val("DV")
                    ep_sc[j] = p.val("SC")

            st_total = p.val("ST")
            s2_total = p.val("S2")
            p.wait("scalar", "ST", st_total)
            p.wait("scalar", "S2", s2_total)
            p.wait("gpsimd", "ST", st_total)
            p.wait("sync", "S2", s2_total)
            p.wait("sync", "ST", st_total)

        for role, reg in (("sync", block.sync), ("gpsimd", block.gpsimd),
                          ("tensor", block.tensor), ("scalar", block.scalar),
                          ("vector", block.vector)):
            reg(lambda eng, role=role: build(P(role, eng, sems)))

    return nc


_NC_CACHE = None


def kernel(L0_attn_hidden_states, L0_attn_Wq, L0_attn_Wk, L0_attn_Wv,
           L0_attn_Wo, L0_ffn_W1, L0_ffn_W2, L0_ffn_W3):
    global _NC_CACHE
    in_maps = _prep_inputs(L0_attn_hidden_states, L0_attn_Wq, L0_attn_Wk,
                           L0_attn_Wv, L0_attn_Wo, L0_ffn_W1, L0_ffn_W2,
                           L0_ffn_W3)
    if _NC_CACHE is None:
        _NC_CACHE = build_program()
    res = run_bass_kernel_spmd(_NC_CACHE, in_maps, core_ids=list(range(NCORES)))
    out = np.empty((S, D), np.float32)
    for c in range(NCORES):
        out[c * TOK:(c + 1) * TOK] = res.results[c]["o_main"]
    out[0:16] = res.results[0]["o_fix"]
    return out.reshape(1, S, D)


# version 16
# speedup vs baseline: 1.0478x; 1.0320x over previous
import numpy as np
import ml_dtypes
import concourse.bass as bass
import concourse.mybir as mybir
from concourse.bass_utils import run_bass_kernel_spmd
from contextlib import ExitStack

F32 = mybir.dt.float32
BF16 = mybir.dt.bfloat16
AF = mybir.ActivationFunctionType
AX = mybir.AxisListType
ALU = mybir.AluOpType

NCORES = 8
S = 8192
D = 2048
HFF = 8192
NH = 16
HD = 128
TOK = 1024
WIN = 1536
SCALING = float(HD) ** -0.5
C_SOFT = 25.0
FWHT_SCALE = float(HFF) ** -0.5
BF = np.dtype(ml_dtypes.bfloat16)
OWN0 = 256


def _had(n):
    h = np.array([[1.0]], dtype=np.float32)
    while h.shape[0] < n:
        h = np.block([[h, h], [h, -h]])
    return h


def _masks_for_core(c):
    s = c * TOK
    ws = s - 256
    out = np.zeros((8, 5, 128, 128), dtype=np.float32)
    i = np.arange(128)
    for qt in range(8):
        qg = s + qt * 128 + i[None, :]
        for slot, kc in ((0, 0), (1, 1), (2, 3), (3, 4)):
            kg = ws + qt * 128 + kc * 128 + i[:, None]
            valid = (kg >= 0) & (kg < S)
            out[qt, slot] = (valid & ((np.abs(qg - kg) < 256) | (kg < 16))).astype(np.float32)
        kg = i[:16, None]
        in_window = (kg >= ws + qt * 128) & (kg < ws + qt * 128 + 640)
        out[qt, 4, :16, :] = (~in_window).astype(np.float32)
    return out.astype(BF)


def _prep_inputs(x, Wq, Wk, Wv, Wo, W1, W2, W3):
    x = np.asarray(x, dtype=np.float32).reshape(S, D)
    Wq, Wk, Wv, Wo = (np.asarray(w, np.float32) for w in (Wq, Wk, Wv, Wo))
    W1, W2, W3 = (np.asarray(w, np.float32) for w in (W1, W2, W3))

    def qk_layout(w):
        a = w.reshape(16, 128, 16, 128).transpose(2, 1, 0, 3)
        a = a.reshape(4, 4, 128, 16, 128).transpose(0, 2, 1, 3, 4)
        return np.ascontiguousarray(a.reshape(4, 128, 8192))

    wq_s = qk_layout(Wq)
    wk_s = qk_layout(Wk)
    a = Wv.reshape(16, 128, 4, 4, 128).transpose(2, 1, 0, 3, 4)
    wv_s = np.ascontiguousarray(a.reshape(4, 128, 8192))
    a = Wo.reshape(16, 128, 16, 128).transpose(2, 1, 0, 3)
    wo_s = np.ascontiguousarray(a.reshape(16, 128, 2048))

    def w13_layout(w):
        a = w.reshape(64, 128, 16, 128).transpose(0, 3, 2, 1)
        return np.ascontiguousarray(a.reshape(64, 128, 2048))

    w1_s = w13_layout(W1)
    w3_s = w13_layout(W3)
    a = W2.reshape(16, 128, 64, 128).transpose(0, 3, 2, 1)
    w2_s = np.ascontiguousarray(a.reshape(16, 128, 8192))

    common = {
        "wq_s": wq_s, "wk_s": wk_s, "wv_s": wv_s, "wo_s": wo_s,
        "w1_s": w1_s, "w3_s": w3_s, "w2_s": w2_s,
        "h128": _had(128).astype(BF),
        "id_b": np.eye(128, dtype=BF),
        "id_f": np.eye(128, dtype=np.float32),
        "ones_f": np.ones((128, 1), dtype=BF),
        "ones1_b": np.ones((1, 128), dtype=BF),
        "x_glob": np.ascontiguousarray(x[0:16]),
    }
    maps = []
    for c in range(NCORES):
        s = c * TOK
        xkv = np.zeros((WIN, D), np.float32)
        lo, hi = max(0, s - 256), min(S, s + TOK + 256)
        xkv[lo - (s - 256): hi - (s - 256)] = x[lo:hi]
        maps.append(dict(common, x_kv=xkv, masks=_masks_for_core(c)))
    return maps



class P:
    def __init__(self, role, eng, sems):
        self.role = role
        self.eng = eng
        self.sems = sems
        self.cnt = {n: 0 for n in sems}

    def do(self, role, fn, inc=None, inc_n=1):
        if inc is not None:
            self.cnt[inc] += inc_n
        if self.role == role:
            ins = fn(self.eng)
            if inc is not None:
                ins.then_inc(self.sems[inc], inc_n)
        return self.cnt[inc] if inc is not None else None

    def wait(self, role, sem, val):
        if val > 0 and self.role == role:
            self.eng.wait_ge(self.sems[sem], val)

    def val(self, sem):
        return self.cnt[sem]


BASE = 20480
LIMIT = 229376

def _mk_map2():
    m = {}
    off = BASE

    def add(name, nbytes):
        nonlocal off
        o = (off + 31) // 32 * 32
        m[name] = o
        off = o + nbytes
        return o

    for n, b in (("h128", 256), ("idb", 256), ("idf", 512), ("ones", 32),
                 ("ones1", 256), ("y_l", 1024), ("dlt", 1024), ("o_n", 512),
                 ("ln", 64), ("mg", 64), ("sg", 64), ("og", 1024),
                 ("outs", 1024), ("fsb", 8192), ("dltTc", 1024), ("fixc", 1024),
                 ("xnT", 32768)):
        add(n, b)
    ov = (off + 31) // 32 * 32
    m["hid"] = ov
    pf_tail = ov + 131072
    m["w1b"], m["w3b"] = pf_tail, pf_tail + 8192
    m["sil"], m["h0"] = pf_tail + 16384, pf_tail + 24576
    assert pf_tail + 28672 <= LIMIT, f"PF overflow: {pf_tail + 28672}"

    o = [ov]
    def pa(name, nbytes):
        oo = (o[0] + 31) // 32 * 32
        m[name] = oo
        o[0] = oo + nbytes
        assert o[0] <= ov + 131072, (name, o[0] - ov)
        return oo
    pa("attnT", 32768)
    pa("xnTh", 16896)
    pa("maskb", 10240)
    pa("k4t", 12416)
    pa("v4", 13312)
    pa("q4t", 8192)
    pa("qg4t", 128)
    pa("wbuf", 32768)
    pa("e_sb", 3072)
    pa("rb", 1024)
    pa("egt", 256)
    m["xt"] = m["k4t"]
    m["sq"] = m["k4t"] + 16384
    m["xn"] = m["k4t"] + 24576
    m["wo_sb"] = m["wbuf"]
    m["y_sb"] = m["wbuf"] + 8192
    m["mfl"] = m["k4t"]
    m["sfl"] = m["k4t"] + 8192
    m["oga"] = m["k4t"] + 16384
    m["tmg"] = m["k4t"] + 24576
    m["wb2"] = m["wbuf"] + 16384
    m["wbf"] = m["wbuf"] + 24576
    m["w2b"] = m["xnT"]
    m["wo2"] = m["w1b"]
    m["y2"] = m["w3b"]
    m["xrs"] = m["sil"]
    return m

MM = _mk_map2()


def build_program():
    nc = bass.Bass("TRN2", target_bir_lowering=False, detect_race_conditions=False)

    x_kv = nc.dram_tensor("x_kv", [WIN, D], F32, kind="ExternalInput").ap()
    x_glob = nc.dram_tensor("x_glob", [16, D], F32, kind="ExternalInput").ap()
    wq_d = nc.dram_tensor("wq_s", [4, 128, 8192], F32, kind="ExternalInput").ap()
    wk_d = nc.dram_tensor("wk_s", [4, 128, 8192], F32, kind="ExternalInput").ap()
    wv_d = nc.dram_tensor("wv_s", [4, 128, 8192], F32, kind="ExternalInput").ap()
    wo_d = nc.dram_tensor("wo_s", [16, 128, 2048], F32, kind="ExternalInput").ap()
    w1_d = nc.dram_tensor("w1_s", [64, 128, 2048], F32, kind="ExternalInput").ap()
    w3_d = nc.dram_tensor("w3_s", [64, 128, 2048], F32, kind="ExternalInput").ap()
    w2_d = nc.dram_tensor("w2_s", [16, 128, 8192], F32, kind="ExternalInput").ap()
    h128_d = nc.dram_tensor("h128", [128, 128], BF16, kind="ExternalInput").ap()
    idb_d = nc.dram_tensor("id_b", [128, 128], BF16, kind="ExternalInput").ap()
    idf_d = nc.dram_tensor("id_f", [128, 128], F32, kind="ExternalInput").ap()
    ones_d = nc.dram_tensor("ones_f", [128, 1], BF16, kind="ExternalInput").ap()
    ones1_d = nc.dram_tensor("ones1_b", [1, 128], BF16, kind="ExternalInput").ap()
    mask_d = nc.dram_tensor("masks", [8, 5, 128, 128], BF16, kind="ExternalInput").ap()

    o_main = nc.dram_tensor("o_main", [TOK, D], F32, kind="ExternalOutput").ap()
    o_fix = nc.dram_tensor("o_fix", [16, D], F32, kind="ExternalOutput").ap()

    y_scr = nc.dram_tensor("y_scr", [D, TOK], F32, kind="Internal").ap()
    w2bf_d = nc.dram_tensor("w2bf", [16, 128, 8192], BF16, kind="Internal").ap()
    mrow_d = nc.dram_tensor("mrow", [2048], F32, kind="Internal").ap()
    wobf_d = nc.dram_tensor("wobf", [16, 128, 2048], BF16, kind="Internal").ap()
    part_src = nc.dram_tensor("part_src", [130, 256], F32, kind="Internal")
    ag_out = nc.dram_tensor("ag_out", [1040, 256], F32, kind="Internal",
                            addr_space="Shared")

    at = nc.alloc_sbuf_tensor_at
    h128 = at("h128b", [128, 128], BF16, offset=MM["h128"])
    idb = at("idb", [128, 128], BF16, offset=MM["idb"])
    idf = at("idf", [128, 128], F32, offset=MM["idf"])
    ones = at("ones", [128, 1], BF16, offset=MM["ones"])
    ones1 = at("ones1", [1, 128], BF16, offset=MM["ones1"])
    y_l = at("y_l", [128, 256], F32, offset=MM["y_l"])
    dlt = at("dlt", [128, 256], F32, offset=MM["dlt"])
    o_n = at("o_n", [128, 256], BF16, offset=MM["o_n"])
    ln = at("ln", [128, 16], F32, offset=MM["ln"])
    mg = at("mg", [16, 16], F32, offset=MM["mg"])
    sg = at("sg", [16, 16], F32, offset=MM["sg"])
    og = at("og", [128, 256], F32, offset=MM["og"])
    outs = [at(f"outs{i}", [128, 128], F32, offset=MM["outs"] + 512 * i) for i in range(2)] + \
           [at(f"outs{i+2}", [128, 128], F32, offset=MM["og"] + 512 * i) for i in range(2)]
    fsb = [at(f"fsb{i}", [128, 1024], F32, offset=MM["fsb"] + 4096 * i) for i in range(2)]
    dltTc = [at(f"dltTc{i}", [16, 128], F32, offset=MM["dltTc"] + 512 * i) for i in range(2)]
    rtb = [at(f"rtb{i}", [1, 128], BF16, offset=MM["dltTc"] + 256 * i) for i in range(2)]
    fixc = [at(f"fixc{i}", [16, 128], F32, offset=MM["fixc"] + 512 * i) for i in range(2)]
    xnT = at("xnT", [128, 16 * 1024], BF16, offset=MM["xnT"])
    xnTh = at("xnTh", [128, 16 * 528], BF16, offset=MM["xnTh"])

    hid = at("hid", [128, 64 * 1024], BF16, offset=MM["hid"])
    w1b = [at(f"w1b{i}", [128, 2048], BF16, offset=MM["w1b"] + 4096 * i) for i in range(2)]
    bounce = [at(f"bounce{i}", [128, 4096], BF16,
                 offset=(MM["w1b"] if i == 0 else MM["sil"])) for i in range(2)]
    w3b = [at(f"w3b{i}", [128, 2048], BF16, offset=MM["w3b"] + 4096 * i) for i in range(2)]
    sil = [at(f"sil{i}", [128, 1024], F32, offset=MM["sil"] + 4096 * i) for i in range(2)]
    h0 = [at(f"h0{i}", [128, 1024], BF16, offset=MM["h0"] + 2048 * i) for i in range(2)]

    attnT = at("attnT", [128, NH * TOK], BF16, offset=MM["attnT"])
    maskb = at("maskb", [128, 8 * 5 * 128], BF16, offset=MM["maskb"])
    k4t = at("k4t", [128, 4 * 1552], BF16, offset=MM["k4t"])
    v4 = at("v4", [128, 13 * 512], BF16, offset=MM["v4"])
    q4t = at("q4t", [128, 4 * 1024], BF16, offset=MM["q4t"])
    qg4t = at("qg4t", [128, 64], BF16, offset=MM["qg4t"])
    wbuf = [at(f"wbuf{i}", [128, 8192], BF16, offset=MM["wbuf"] + 16384 * i) for i in range(2)]
    e_sb = [at(f"e{i}", [128, 768], BF16, offset=MM["e_sb"] + 1536 * i) for i in range(2)]
    rb_sb = [at(f"rb{i}", [128, 128], F32, offset=MM["rb"] + 512 * i) for i in range(2)]
    rs_sb = [ln[:, 13:14], ln[:, 14:15]]
    egt = at("egt", [128, 128], BF16, offset=MM["egt"])

    xt = [at(f"xt{i}", [128, 2048], F32, offset=MM["xt"] + 8192 * i) for i in range(2)]
    sq = at("sq", [128, 2048], F32, offset=MM["sq"])
    xn = [at(f"xn{i}", [128, 2048], BF16, offset=MM["xn"] + 4096 * i) for i in range(2)]

    wo_sb = [at(f"wo{i}", [128, 2048], BF16, offset=MM["wo_sb"] + 4096 * i) for i in range(2)]
    y_sb = [at(f"ysb{i}", [128, 1024], F32, offset=MM["y_sb"] + 4096 * i) for i in range(2)]

    mfl = at("mfl", [1, 2048], F32, offset=MM["mfl"])
    sfl = at("sfl", [1, 2048], F32, offset=MM["sfl"])
    oga = at("oga", [128, 2048], F32, offset=MM["oga"])
    tmg = at("tmg", [128, 2048], F32, offset=MM["tmg"])
    wb2 = at("wb2", [128, 2048], F32, offset=MM["wb2"])
    wbf = at("wbf", [1, 2048], BF16, offset=MM["wbf"])

    w2b = [at(f"w2b{i}", [128, 8192], BF16, offset=MM["w2b"] + 16384 * i) for i in range(2)]
    wo2 = [at(f"wo2{i}", [128, 2048], BF16, offset=MM["wo2"] + 4096 * i) for i in range(2)]
    y2 = [at(f"y2{i}", [128, 1024], F32, offset=MM["y2"] + 4096 * i) for i in range(2)]
    xrs = [at(f"xrs{i}", [128, 1024], F32, offset=MM["xrs"] + 4096 * i) for i in range(2)]

    pA = nc.alloc_psum_tensor("pA", [128, 1024], F32)
    pB = nc.alloc_psum_tensor("pB", [128, 1024], F32)
    pC = nc.alloc_psum_tensor("pC", [128, 1024], F32)
    pD = nc.alloc_psum_tensor("pD", [128, 1024], F32)
    PP = [(pA, pB), (pC, pD)]

    def xcol(dc, kl):
        if 256 <= kl < 1280:
            return xnT, dc * 1024 + (kl - 256)
        if kl < 256:
            return xnTh, dc * 528 + kl
        if kl < 1536:
            return xnTh, dc * 528 + 256 + (kl - 1280)
        return xnTh, dc * 528 + 512 + (kl - 1536)

    K_SPANS = [(0, 256), (256, 256), (512, 256), (768, 256), (1024, 256), (1280, 256), (1536, 16)]

    with ExitStack() as ctx:
        sem_names = ["LD", "WL", "PE", "SC", "DV", "ST", "CC", "YD", "PS", "PC", "S2"]
        sems = {n: ctx.enter_context(nc.semaphore(n)) for n in sem_names}
        block = ctx.enter_context(nc.Block())

        def build(p: P):
            for dst, src in ((h128, h128_d), (idb, idb_d), (idf, idf_d),
                             (ones, ones_d), (ones1, ones1_d)):
                p.do("sync", lambda e, d=dst, s_=src: e.dma_start(out=d[:, :], in_=s_),
                     inc="LD", inc_n=16)
            p.do("sync", lambda e: e.dma_start(
                out=maskb[:, :].rearrange("p (qt s f) -> p qt s f", qt=8, s=5),
                in_=mask_d.rearrange("qt s p f -> p qt s f")), inc="LD", inc_n=16)
            pre_ld = p.val("LD")
            p.do("vector", lambda e: e.memset(ln[:, 15:16], -C_SOFT), inc="DV")
            dv_cst = p.val("DV")

            ln_dv = {}
            evac_sc = {}
            tp_pe = {}
            for t in range(13):
                rows = 16 if t == 12 else 128
                buf = xt[t % 2]
                if t >= 2:
                    p.wait("sync", "DV", ln_dv[t - 2])
                src = x_glob if t == 12 else x_kv[128 * t:128 * t + 128, :]
                p.do("sync", lambda e, b=buf, s_=src, rws=rows:
                     e.dma_start(out=b[0:rws, :], in_=s_), inc="LD", inc_n=16)
                ld_now = p.val("LD")

                p.wait("vector", "LD", ld_now)
                p.do("vector", lambda e, b=buf, rws=rows: e.reduce_sum(
                    ln[0:rws, 0:1], b[0:rws, :], axis=AX.X, negate=True), inc="DV")
                p.wait("vector", "DV", p.val("DV"))
                p.wait("scalar", "LD", ld_now)
                p.do("scalar", lambda e, b=buf, rws=rows: e.activation(
                    sq[0:rws, :], b[0:rws, :], AF.Square, accum_out=ln[0:rws, 1:2]), inc="SC")
                sc_sq = p.val("SC")
                p.do("vector", lambda e, rws=rows: e.tensor_scalar_mul(
                    ln[0:rws, 2:3], ln[0:rws, 0:1], 1.0 / D), inc="DV")
                p.wait("vector", "DV", p.val("DV"))
                p.wait("vector", "SC", sc_sq)
                p.do("vector", lambda e, rws=rows: e.tensor_scalar(
                    ln[0:rws, 3:4], ln[0:rws, 1:2], 1.0 / D, 1e-5,
                    ALU.mult, ALU.add), inc="DV")
                p.wait("vector", "DV", p.val("DV"))
                p.do("vector", lambda e, rws=rows: e.tensor_mul(
                    ln[0:rws, 4:5], ln[0:rws, 2:3], ln[0:rws, 2:3]), inc="DV")
                p.wait("vector", "DV", p.val("DV"))
                p.do("vector", lambda e, rws=rows: e.tensor_sub(
                    ln[0:rws, 5:6], ln[0:rws, 3:4], ln[0:rws, 4:5]), inc="DV")
                dv_var = p.val("DV")
                p.wait("scalar", "DV", dv_var)
                p.do("scalar", lambda e, rws=rows: e.activation(
                    ln[0:rws, 6:7], ln[0:rws, 5:6], AF.Ln), inc="SC")
                p.wait("scalar", "SC", p.val("SC"))
                p.do("scalar", lambda e, rws=rows: e.activation(
                    ln[0:rws, 7:8], ln[0:rws, 6:7], AF.Exp, scale=-0.5), inc="SC")
                sc_rstd = p.val("SC")
                p.wait("vector", "SC", sc_rstd)
                p.do("vector", lambda e, rws=rows: e.tensor_scalar_mul(
                    ln[0:rws, 8:9], ln[0:rws, 2:3], ln[0:rws, 7:8]), inc="DV")
                p.wait("vector", "DV", p.val("DV"))
                if t >= 2:
                    p.wait("vector", "PE", tp_pe[t - 2])
                xnb = xn[t % 2]
                p.do("vector", lambda e, b=buf, xb=xnb, rws=rows: e.tensor_scalar(
                    xb[0:rws, :], b[0:rws, :], ln[0:rws, 7:8], ln[0:rws, 8:9],
                    ALU.mult, ALU.add), inc="DV")
                ln_dv[t] = p.val("DV")

                p.wait("tensor", "LD", pre_ld)
                p.wait("tensor", "DV", ln_dv[t])
                ps = PP[t % 2][0]
                for half in range(2):
                    if half == 1:
                        p.wait("tensor", "SC", evac_sc[(t, 0)])
                    elif t >= 2:
                        p.wait("tensor", "SC", evac_sc[(t - 2, 1)])
                    for j in range(8):
                        dc = half * 8 + j
                        dst = ps[:, 64 * j:64 * j + 64].bitcast(BF16)
                        p.do("tensor", lambda e, d=dst, xb=xnb, dcc=dc, rws=rows:
                             e.transpose(d[0:128, 0:rws],
                                         xb[0:rws, 128 * dcc:128 * dcc + 128],
                                         idb[0:rws, 0:rws]),
                             inc="PE" if j == 7 else None)
                    pe_now = p.val("PE")
                    if half == 1:
                        tp_pe[t] = pe_now
                    p.wait("scalar", "PE", pe_now)
                    src_ap = ps[:, 0:512].bitcast(BF16).rearrange("p (j k) -> p j k", j=8)
                    kl = 128 * t if t < 12 else 1536
                    ten0, col0 = xcol(0, kl)
                    stride = 1024 if ten0 is xnT else 528
                    dst_ap = ten0[:, :].rearrange("p (dc k) -> p dc k", dc=16)[
                        :, 8 * half:8 * half + 8, col0:col0 + rows]
                    p.do("scalar", lambda e, sp=src_ap, d=dst_ap, rws=rows:
                         e.copy(d, sp[:, :, 0:rws]), inc="SC")
                    evac_sc[(t, half)] = p.val("SC")
            p0_dv = p.val("DV")

            wl_done = {}
            sec_pe = {}
            nload = [0]

            def wload(src_ap, pe_release):
                i = nload[0]
                nload[0] += 1
                if i >= 2:
                    p.wait("gpsimd", "PE", sec_pe.get(i - 2, 0))
                elif pe_release:
                    p.wait("gpsimd", "PE", pe_release)
                p.do("gpsimd", lambda e, d=wbuf[i % 2], s_=src_ap:
                     e.dma_start(out=d[:, :], in_=s_), inc="WL", inc_n=16)
                wl_done[i] = p.val("WL")
                return i

            iter_dv = {}
            iter_sc = {}
            it = 0
            pieces = []
            pc_ready_wo = {}
            pc_ready_w2 = {}
            for i in range(8):
                pieces.append((wo_d[2 * i:2 * i + 2].rearrange("a p f -> p a f"),
                               wobf_d[2 * i:2 * i + 2].rearrange("a p f -> p a f"), True))
                pc_ready_wo[2 * i] = pc_ready_wo[2 * i + 1] = 16 * len(pieces)
                for dcp in (2 * i, 2 * i + 1):
                    for hp in range(2):
                        pieces.append((w2_d[dcp][:, 4096 * hp:4096 * hp + 4096],
                                       w2bf_d[dcp][:, 4096 * hp:4096 * hp + 4096], False))
                    pc_ready_w2[dcp] = 16 * len(pieces)
            pc_k = [0]

            def emit_precast(n):
                for _ in range(n):
                    k = pc_k[0]
                    if k >= len(pieces):
                        return
                    pc_k[0] += 1
                    srcap, dstap, three_d = pieces[k]
                    bb = bounce[k % 2]
                    bview = (bb[:, :].rearrange("p (a f) -> p a f", a=2)
                             if three_d else bb[:, :])
                    if k >= 2:
                        p.wait("gpsimd", "PC", 16 * (k - 1))
                    p.do("gpsimd", lambda e, d=bview, s_=srcap: e.dma_start(out=d, in_=s_),
                         inc="WL", inc_n=16)
                    wlv = p.val("WL")
                    p.wait("sync", "WL", wlv)
                    p.do("sync", lambda e, d=dstap, s_=bview: e.dma_start(out=d, in_=s_),
                         inc="PC", inc_n=16)

            for g in range(4):
                iq = wload(wq_d[g], 0)
                wq = wbuf[iq % 2]
                p.wait("tensor", "WL", wl_done[iq])
                p.wait("tensor", "SC", p.val("SC"))
                qev = {}
                for dqc in range(4):
                    lo, hi_ = PP[dqc % 2]
                    p.wait("tensor", "SC", qev.get(dqc - 2, 0))
                    for th in range(2):
                        for dc in range(16):
                            p.do("tensor", lambda e, o_=lo[:, 512 * th:512 * th + 512],
                                 l=wq[:, 2048 * dqc + 128 * dc:2048 * dqc + 128 * dc + 128],
                                 rr=xnT[:, 1024 * dc + 512 * th:1024 * dc + 512 * th + 512],
                                 s0=(dc == 0), s1=(dc == 15):
                                 e.matmul(o_, l, rr, start=s0, stop=s1),
                                 inc="PE" if dc == 15 else None)
                    for dc in range(16):
                        p.do("tensor", lambda e, o_=hi_[:, 896:912],
                             l=wq[:, 2048 * dqc + 128 * dc:2048 * dqc + 128 * dc + 128],
                             rr=xnTh[:, 528 * dc + 512:528 * dc + 528],
                             s0=(dc == 0), s1=(dc == 15):
                             e.matmul(o_, l, rr, start=s0, stop=s1),
                             inc="PE" if dc == 15 else None)
                    pe_q = p.val("PE")
                    p.wait("scalar", "PE", pe_q)
                    p.do("scalar", lambda e, lo_=lo, dq=dqc: e.copy(
                        q4t[:, 1024 * dq:1024 * dq + 1024], lo_[:, 0:1024]), inc="SC")
                    p.do("scalar", lambda e, hh=hi_, dq=dqc: e.copy(
                        qg4t[:, 16 * dq:16 * dq + 16], hh[:, 896:912]), inc="SC")
                    qev[dqc] = p.val("SC")
                sec_pe[iq] = p.val("PE")

                ik = wload(wk_d[g], 0)
                wk = wbuf[ik % 2]
                p.wait("tensor", "WL", wl_done[ik])
                p.wait("tensor", "SC", p.val("SC"))
                kev = {}
                for dkc in range(4):
                    lo, hi_ = PP[dkc % 2]
                    p.wait("tensor", "SC", kev.get(dkc - 2, 0))
                    for sp0, w in K_SPANS:
                        for dc in range(16):
                            ten, col = xcol(dc, sp0)
                            if sp0 < 1024:
                                o_ap = lo[:, sp0:sp0 + w]
                            elif sp0 < 1536:
                                o_ap = hi_[:, sp0 - 1024:sp0 - 1024 + w]
                            else:
                                o_ap = hi_[:, 896:912]
                            p.do("tensor", lambda e, o_=o_ap,
                                 l=wk[:, 2048 * dkc + 128 * dc:2048 * dkc + 128 * dc + 128],
                                 rr=ten[:, col:col + w],
                                 s0=(dc == 0), s1=(dc == 15):
                                 e.matmul(o_, l, rr, start=s0, stop=s1),
                                 inc="PE" if dc == 15 else None)
                    pe_k = p.val("PE")
                    p.wait("scalar", "PE", pe_k)
                    p.do("scalar", lambda e, lo_=lo, dk=dkc: e.copy(
                        k4t[:, 1552 * dk:1552 * dk + 1024], lo_[:, 0:1024]), inc="SC")
                    p.do("scalar", lambda e, hh=hi_, dk=dkc: e.copy(
                        k4t[:, 1552 * dk + 1024:1552 * dk + 1536], hh[:, 0:512]), inc="SC")
                    p.do("scalar", lambda e, hh=hi_, dk=dkc: e.copy(
                        k4t[:, 1552 * dk + 1536:1552 * dk + 1552], hh[:, 896:912]), inc="SC")
                    kev[dkc] = p.val("SC")
                sec_pe[ik] = p.val("PE")

                iv = wload(wv_d[g], 0)
                wv = wbuf[iv % 2]
                p.wait("tensor", "WL", wl_done[iv])
                p.wait("tensor", "SC", p.val("SC"))
                vev = {}
                for kt in range(13):
                    lo, _hi = PP[kt % 2]
                    p.wait("tensor", "SC", vev.get(kt - 2, 0))
                    rows = 16 if kt == 12 else 128
                    kl = 1536 if kt == 12 else 128 * kt
                    for dc in range(16):
                        ten, col = xcol(dc, kl)
                        p.do("tensor", lambda e, o_=lo[0:rows, 0:512],
                             l=ten[:, col:col + rows],
                             rr=wv[:, 512 * dc:512 * dc + 512],
                             s0=(dc == 0), s1=(dc == 15):
                             e.matmul(o_, l, rr, start=s0, stop=s1),
                             inc="PE" if dc == 15 else None)
                    pe_v = p.val("PE")
                    p.wait("scalar", "PE", pe_v)
                    p.do("scalar", lambda e, lo_=lo, k=kt, rws=rows: e.copy(
                        v4[0:rws, 512 * k:512 * k + 512], lo_[0:rws, 0:512]), inc="SC")
                    vev[kt] = p.val("SC")
                sec_pe[iv] = p.val("PE")
                sc_proj = p.val("SC")
                if g >= 1:
                    emit_precast(10 if g < 3 else 20)

                p.wait("tensor", "SC", sc_proj)
                p.wait("scalar", "DV", dv_cst)
                mview = maskb[:, :].rearrange("p (qt s f) -> p qt s f", qt=8, s=5)
                a_sc_exp = {}
                a_sc_exp1 = {}
                a_dv_mask = {}
                a_pe_st = {}
                a_pe_sum = {}
                a_pe_pv = {}
                a_dv_r = {}
                a_sc_rt = {}
                a_pe_rt = {}
                a_pe_rb = {}
                a_sc_rb = {}
                a_dv_mult = {}

                def st_s1(i):
                    hi2, qt = i // 8, i % 8
                    par = (it + i) % 2
                    sA, _sB = PP[par]
                    esb = e_sb[par]
                    p.wait("tensor", "SC", a_sc_exp.get(i - 2, 0))
                    for kc in range(5):
                        p.do("tensor", lambda e, o_=sA[:, 128 * kc:128 * kc + 128],
                             l=k4t[:, 1552 * hi2 + 128 * (qt + kc):1552 * hi2 + 128 * (qt + kc) + 128],
                             rr=q4t[:, 1024 * hi2 + 128 * qt:1024 * hi2 + 128 * qt + 128]:
                             e.matmul(o_, l, rr, start=True, stop=True))
                    p.do("tensor", lambda e, o_=sA[0:16, 640:768],
                         l=k4t[:, 1552 * hi2 + 1536:1552 * hi2 + 1552],
                         rr=q4t[:, 1024 * hi2 + 128 * qt:1024 * hi2 + 128 * qt + 128]:
                         e.matmul(o_, l, rr, start=True, stop=True), inc="PE")
                    a_pe_st[i] = p.val("PE")
                    p.wait("scalar", "PE", a_pe_st[i])
                    p.wait("scalar", "DV", a_dv_mask.get(i - 2, 0))
                    p.do("scalar", lambda e, s_=sA, d=esb: e.activation(
                        d[:, 128:512], s_[:, 128:512], AF.Exp, bias=ln[:, 15:16], scale=SCALING), inc="SC")
                    a_sc_exp1[i] = p.val("SC")
                    p.do("scalar", lambda e, s_=sA, d=esb: e.activation(
                        d[:, 0:128], s_[:, 0:128], AF.Exp, bias=ln[:, 15:16], scale=SCALING), inc="SC")
                    p.do("scalar", lambda e, s_=sA, d=esb: e.activation(
                        d[:, 512:640], s_[:, 512:640], AF.Exp, bias=ln[:, 15:16], scale=SCALING), inc="SC")
                    p.do("scalar", lambda e, s_=sA, d=esb: e.activation(
                        d[0:16, 640:768], s_[0:16, 640:768], AF.Exp, bias=ln[0:16, 15:16], scale=SCALING), inc="SC")
                    a_sc_exp[i] = p.val("SC")
                    p.wait("vector", "SC", a_sc_exp[i])
                    p.do("vector", lambda e, d=esb, q=qt: e.tensor_mul(
                        d[:, 0:128], d[:, 0:128], mview[:, q, 0, :]))
                    if qt == 0:
                        p.do("vector", lambda e, d=esb: e.tensor_mul(
                            d[:, 128:256], d[:, 128:256], mview[:, 0, 1, :]))
                    if qt == 7:
                        p.do("vector", lambda e, d=esb: e.tensor_mul(
                            d[:, 384:512], d[:, 384:512], mview[:, 7, 2, :]))
                    p.do("vector", lambda e, d=esb, q=qt: e.tensor_mul(
                        d[:, 512:640], d[:, 512:640], mview[:, q, 3, :]))
                    p.do("vector", lambda e, d=esb, q=qt: e.tensor_mul(
                        d[0:16, 640:768], d[0:16, 640:768], mview[0:16, q, 4, :]), inc="DV")
                    a_dv_mask[i] = p.val("DV")

                def st_s2(i):
                    hi2, qt = i // 8, i % 8
                    par = (it + i) % 2
                    _sA, sB = PP[par]
                    esb = e_sb[par]
                    free_kc = [kc for kc in (1, 2, 3)
                               if not (kc == 1 and qt == 0) and not (kc == 3 and qt == 7)]
                    mask_kc = [kc for kc in range(5) if kc not in free_kc]
                    p.wait("tensor", "SC", a_sc_exp1[i])
                    p.wait("tensor", "DV", a_dv_mult.get(i - 2, 0))
                    p.wait("tensor", "SC", a_sc_rb.get(i - 2, 0))
                    for n_, kc in enumerate(free_kc):
                        p.do("tensor", lambda e, o_=sB[:, 0:1],
                             l=esb[:, 128 * kc:128 * kc + 128], s0=(n_ == 0):
                             e.matmul(o_, l, ones[:, :], start=s0, stop=False))
                    p.wait("tensor", "DV", a_dv_mask[i])
                    for kc in mask_kc:
                        p.do("tensor", lambda e, o_=sB[:, 0:1],
                             l=esb[:, 128 * kc:128 * kc + 128]:
                             e.matmul(o_, l, ones[:, :], start=False, stop=False))
                    p.do("tensor", lambda e, o_=sB[:, 0:1], l=esb[0:16, 640:768]:
                         e.matmul(o_, l, ones[0:16, :], start=False, stop=True), inc="PE")
                    a_pe_sum[i] = p.val("PE")
                    for n_, kc in enumerate(free_kc + mask_kc):
                        p.do("tensor", lambda e, o_=sB[:, 512:640],
                             l=v4[:, 512 * (qt + kc) + 128 * hi2:512 * (qt + kc) + 128 * hi2 + 128],
                             rr=esb[:, 128 * kc:128 * kc + 128], s0=(n_ == 0):
                             e.matmul(o_, l, rr, start=s0, stop=False))
                    p.do("tensor", lambda e, o_=sB[:, 512:640],
                         l=v4[0:16, 512 * 12 + 128 * hi2:512 * 12 + 128 * hi2 + 128],
                         rr=esb[0:16, 640:768]:
                         e.matmul(o_, l, rr, start=False, stop=True), inc="PE")
                    a_pe_pv[i] = p.val("PE")
                    p.wait("vector", "PE", a_pe_sum[i])
                    p.do("vector", lambda e, s_=sB, d=rs_sb[par]:
                         e.reciprocal(d, s_[:, 0:1]), inc="DV")
                    p.wait("vector", "DV", p.val("DV"))
                    a_dv_r[i] = p.val("DV")

                def st_s3(i):
                    hi2, qt = i // 8, i % 8
                    h = 4 * g + hi2
                    par = (it + i) % 2
                    _sA, sB = PP[par]
                    p.wait("tensor", "DV", a_dv_r[i])
                    p.do("tensor", lambda e, s_=sB, d=rs_sb[par]: e.transpose(
                        s_[0:1, 4:132], d, idf[:, :]), inc="PE")
                    a_pe_rt[i] = p.val("PE")
                    p.wait("scalar", "PE", a_pe_rt[i])
                    p.do("scalar", lambda e, s_=sB, d=rtb[par]: e.copy(
                        d[0:1, :], s_[0:1, 4:132]), inc="SC")
                    a_sc_rt[i] = p.val("SC")
                    p.wait("tensor", "SC", a_sc_rt[i])
                    p.do("tensor", lambda e, s_=sB, d=rtb[par]: e.matmul(
                        s_[:, 256:384], ones1[:, :], d[0:1, :],
                        start=True, stop=True), inc="PE")
                    a_pe_rb[i] = p.val("PE")
                    p.wait("scalar", "PE", a_pe_rb[i])
                    p.wait("scalar", "DV", a_dv_mult.get(i - 2, 0))
                    p.do("scalar", lambda e, s_=sB, d=rb_sb[par]: e.copy(
                        d[:, :], s_[:, 256:384]), inc="SC")
                    a_sc_rb[i] = p.val("SC")
                    p.wait("vector", "SC", a_sc_rb[i])
                    p.wait("vector", "PE", a_pe_pv[i])
                    p.do("vector", lambda e, s_=sB, rbs=rb_sb[par], hh=h, q=qt:
                         e.tensor_mul(attnT[:, 1024 * hh + 128 * q:1024 * hh + 128 * q + 128],
                                      s_[:, 512:640], rbs[:, :]), inc="DV")
                    a_dv_mult[i] = p.val("DV")

                for os_ in range(34):
                    if os_ < 32:
                        st_s1(os_)
                    if 1 <= os_ <= 32:
                        st_s2(os_ - 1)
                    if os_ >= 2:
                        st_s3(os_ - 2)
                it += 32
                iter_dv[it - 1] = p.val("DV")
                iter_sc[it - 1] = p.val("SC")

                g_pe_sg = {}
                g_sc_eg = {}
                g_dv_sum = {}
                g_pe_egt = {}
                g_sc_egt = {}
                g_pe_og = {}
                g_sc_og = {}

                def glob_s1(hh2):
                    h = 4 * g + hh2
                    par = (it + hh2) % 2
                    hp = hh2 % 2
                    sA = PP[par][0]
                    esb = e_sb[par]
                    if hh2 < 2:
                        p.wait("tensor", "SC", p.val("SC"))
                        p.wait("tensor", "DV", p.val("DV"))
                    else:
                        p.wait("tensor", "SC", g_sc_eg[hh2 - 2])
                    for th in range(2):
                        p.do("tensor", lambda e, o_=sA[0:16, 512 * th:512 * th + 512],
                             l=qg4t[:, 16 * hh2:16 * hh2 + 16],
                             rr=k4t[:, 1552 * hh2 + OWN0 + 512 * th:1552 * hh2 + OWN0 + 512 * th + 512]:
                             e.matmul(o_, l, rr, start=True, stop=True),
                             inc="PE" if th == 1 else None)
                    g_pe_sg[hh2] = p.val("PE")
                    p.wait("vector", "PE", g_pe_sg[hh2])
                    p.do("vector", lambda e, s_=sA, c=9 + hp: e.reduce_max(
                        ln[0:16, c:c + 1], s_[0:16, 0:1024], axis=AX.X, negate=True), inc="DV")
                    p.wait("vector", "DV", p.val("DV"))
                    p.do("vector", lambda e, c=9 + hp, d=11 + hp: e.tensor_scalar_mul(
                        ln[0:16, d:d + 1], ln[0:16, c:c + 1], SCALING), inc="DV")
                    p.wait("vector", "DV", p.val("DV"))
                    p.do("vector", lambda e, hh=h, d=11 + hp: e.tensor_scalar_mul(
                        mg[0:16, hh:hh + 1], ln[0:16, d:d + 1], -1.0), inc="DV")
                    dv_mg = p.val("DV")
                    p.wait("scalar", "DV", dv_mg)
                    p.wait("scalar", "PE", g_pe_sg[hh2])
                    if hh2 >= 2:
                        p.wait("scalar", "PE", g_pe_egt[hh2 - 2])
                        p.wait("scalar", "DV", g_dv_sum[hh2 - 2])
                    else:
                        p.wait("scalar", "DV", p.val("DV"))
                    egtail = rb_sb[par][:, :].bitcast(BF16)
                    p.do("scalar", lambda e, s_=sA, d=esb, bi=11 + hp: e.activation(
                        d[0:16, 0:768], s_[0:16, 0:768], AF.Exp,
                        bias=ln[0:16, bi:bi + 1], scale=SCALING), inc="SC")
                    p.do("scalar", lambda e, s_=sA, d=egtail, bi=11 + hp: e.activation(
                        d[0:16, 0:256], s_[0:16, 768:1024], AF.Exp,
                        bias=ln[0:16, bi:bi + 1], scale=SCALING), inc="SC")
                    g_sc_eg[hh2] = p.val("SC")
                    p.wait("vector", "SC", g_sc_eg[hh2])
                    p.do("vector", lambda e, d=esb, c=13 + hp: e.reduce_sum(
                        ln[0:16, c:c + 1], d[0:16, 0:768], axis=AX.X), inc="DV")
                    p.wait("vector", "DV", p.val("DV"))
                    p.do("vector", lambda e, d=egtail, c=9 + hp: e.reduce_sum(
                        ln[0:16, c:c + 1], d[0:16, 0:256], axis=AX.X), inc="DV")
                    p.wait("vector", "DV", p.val("DV"))
                    p.do("vector", lambda e, hh=h, c1=13 + hp, c2=9 + hp: e.tensor_add(
                        sg[0:16, hh:hh + 1], ln[0:16, c1:c1 + 1], ln[0:16, c2:c2 + 1]), inc="DV")
                    g_dv_sum[hh2] = p.val("DV")

                def glob_s2(hh2):
                    h = 4 * g + hh2
                    par = (it + hh2) % 2
                    sB = PP[par][1]
                    esb = e_sb[par]
                    p.wait("tensor", "SC", g_sc_eg[hh2])
                    if hh2 >= 2:
                        p.wait("tensor", "SC", g_sc_og[hh2 - 2])
                    egtail = rb_sb[par][:, :].bitcast(BF16)
                    for j in range(8):
                        if j < 6:
                            srcj = esb[0:16, 128 * j:128 * j + 128]
                        else:
                            srcj = egtail[0:16, 128 * (j - 6):128 * (j - 6) + 128]
                        dstj = sB[:, 8 * j:8 * j + 8].bitcast(BF16)
                        p.do("tensor", lambda e, d=dstj, s_=srcj:
                             e.transpose(d[0:128, 0:16], s_, idb[0:16, 0:16]),
                             inc="PE" if j == 7 else None)
                    g_pe_egt[hh2] = p.val("PE")
                    p.wait("scalar", "PE", g_pe_egt[hh2])
                    p.do("scalar", lambda e, s_=sB: e.copy(
                        egt[:, :], s_[:, 0:64].bitcast(BF16)), inc="SC")
                    g_sc_egt[hh2] = p.val("SC")
                    p.wait("tensor", "SC", g_sc_egt[hh2])
                    for j in range(8):
                        p.do("tensor", lambda e, o_=sB[:, 512:528],
                             l=v4[:, 512 * (2 + j) + 128 * hh2:512 * (2 + j) + 128 * hh2 + 128],
                             rr=egt[:, 16 * j:16 * j + 16], s0=(j == 0), s1=(j == 7):
                             e.matmul(o_, l, rr, start=s0, stop=s1),
                             inc="PE" if j == 7 else None)
                    g_pe_og[hh2] = p.val("PE")
                    p.wait("scalar", "PE", g_pe_og[hh2])
                    p.do("scalar", lambda e, s_=sB, hh=h: e.copy(
                        og[:, 16 * hh:16 * hh + 16], s_[:, 512:528]), inc="SC")
                    g_sc_og[hh2] = p.val("SC")

                for gs in range(5):
                    if gs < 4:
                        glob_s1(gs)
                    if gs >= 1:
                        glob_s2(gs - 1)
                it += 4
                iter_dv[it - 1] = p.val("DV")
                iter_sc[it - 1] = p.val("SC")

            pa_pe = p.val("PE")
            pa_sc = p.val("SC")
            pa_dv = p.val("DV")

            p.wait("gpsimd", "PC", 16 * len(pieces))
            pf_wl = {}
            for hcp in range(2):
                p.do("gpsimd", lambda e, d=w1b[hcp % 2], s_=w1_d[hcp]:
                     e.dma_start(out=d[:, :], in_=s_), inc="WL", inc_n=16)
                p.do("gpsimd", lambda e, d=w3b[hcp % 2], s_=w3_d[hcp]:
                     e.dma_start(out=d[:, :], in_=s_), inc="WL", inc_n=16)
                pf_wl[hcp] = p.val("WL")

            p.wait("gpsimd", "SC", pa_sc)
            p.wait("gpsimd", "DV", pa_dv)
            p.do("gpsimd", lambda e: e.dma_start(out=part_src.ap()[0:128, :], in_=og[:, :]),
                 inc="PS", inc_n=16)
            p.do("gpsimd", lambda e: e.dma_start(
                out=part_src.ap()[128, :].rearrange("(q h) -> q h", q=16),
                in_=mg[0:16, :]), inc="PS", inc_n=16)
            p.do("gpsimd", lambda e: e.dma_start(
                out=part_src.ap()[129, :].rearrange("(q h) -> q h", q=16),
                in_=sg[0:16, :]), inc="PS", inc_n=16)
            p.wait("gpsimd", "PS", p.val("PS"))
            p.do("gpsimd", lambda e: e.collective_compute(
                "AllGather", ALU.bypass, replica_groups=[list(range(NCORES))],
                ins=[part_src.ap()], outs=[ag_out.ap()]), inc="CC")
            cc_done = p.val("CC")

            wo_pe = {}
            y_ev = {}
            for dc in range(16):
                if dc == 8:
                    p.wait("gpsimd", "CC", cc_done)
                    agv = ag_out.ap().rearrange("(r y) c -> r y c", y=130)
                    p.do("gpsimd", lambda e: e.dma_start(
                        out=mfl[0:1, :].rearrange("p (r x) -> p r x", r=8),
                        in_=agv[:, 128, :].unsqueeze(0)), inc="WL", inc_n=16)
                    p.do("gpsimd", lambda e: e.dma_start(
                        out=sfl[0:1, :].rearrange("p (r x) -> p r x", r=8),
                        in_=agv[:, 129, :].unsqueeze(0)), inc="WL", inc_n=16)
                    p.do("gpsimd", lambda e: e.dma_start(
                        out=oga[:, :].rearrange("p (r c) -> p r c", r=8),
                        in_=agv[:, 0:128, :].rearrange("r p c -> p r c")), inc="WL", inc_n=16)
                    ld_mg = p.val("WL")
                    p.wait("vector", "WL", ld_mg)
                    p.wait("vector", "PE", pa_pe)
                    p.do("vector", lambda e: e.tensor_reduce(
                        tmg[0:1, 0:256], mfl[0:1, :].rearrange("p (r x) -> p x r", r=8),
                        axis=AX.X, op=ALU.max), inc="DV")
                    p.wait("vector", "DV", p.val("DV"))
                    p.do("vector", lambda e: e.tensor_sub(
                        wb2[0:1, 0:2048].rearrange("p (r x) -> p r x", r=8),
                        mfl[0:1, :].rearrange("p (r x) -> p r x", r=8),
                        tmg[0:1, 0:256].unsqueeze(1).broadcast_to([1, 8, 256])), inc="DV")
                    dv_diff = p.val("DV")
                    p.wait("scalar", "DV", dv_diff)
                    p.wait("scalar", "PE", pa_pe)
                    p.do("scalar", lambda e: e.activation(
                        wb2[0:1, 0:2048], wb2[0:1, 0:2048], AF.Exp), inc="SC")
                    sc_alpha = p.val("SC")
                    p.wait("vector", "SC", sc_alpha)
                    p.do("vector", lambda e: e.tensor_mul(sfl[0:1, :], sfl[0:1, :], wb2[0:1, 0:2048]), inc="DV")
                    p.wait("vector", "DV", p.val("DV"))
                    p.do("vector", lambda e: e.tensor_reduce(
                        tmg[0:1, 256:512], sfl[0:1, :].rearrange("p (r x) -> p x r", r=8),
                        axis=AX.X, op=ALU.add), inc="DV")
                    p.wait("vector", "DV", p.val("DV"))
                    p.do("vector", lambda e: e.reciprocal(tmg[0:1, 512:768], tmg[0:1, 256:512]), inc="DV")
                    p.wait("vector", "DV", p.val("DV"))
                    p.do("vector", lambda e: e.tensor_mul(
                        wb2[0:1, 0:2048].rearrange("p (r x) -> p r x", r=8),
                        wb2[0:1, 0:2048].rearrange("p (r x) -> p r x", r=8),
                        tmg[0:1, 512:768].unsqueeze(1).broadcast_to([1, 8, 256])), inc="DV")
                    dv_w = p.val("DV")
                    p.wait("scalar", "DV", dv_w)
                    p.do("scalar", lambda e: e.copy(wbf[0:1, :], wb2[0:1, 0:2048]), inc="SC")
                    sc_wbf = p.val("SC")

                if dc >= 2:
                    p.wait("sync", "PE", wo_pe[dc - 2])
                else:
                    p.wait("sync", "PE", pa_pe)
                p.wait("sync", "PC", pc_ready_wo[dc])
                p.do("sync", lambda e, d=wo_sb[dc % 2], s_=wobf_d[dc]:
                     e.dma_start(out=d[:, :], in_=s_), inc="LD", inc_n=16)
                wl_wo = p.val("LD")
                lo, hi_ = PP[dc % 2]
                p.wait("tensor", "LD", wl_wo)
                p.wait("tensor", "SC", y_ev.get(dc - 2, 0))
                if dc == 0:
                    p.wait("tensor", "SC", pa_sc)
                    p.wait("tensor", "DV", pa_dv)
                wo_t = wo_sb[dc % 2]
                for th in range(2):
                    for dvc in range(16):
                        p.do("tensor", lambda e, o_=lo[:, 512 * th:512 * th + 512],
                             l=wo_t[:, 128 * dvc:128 * dvc + 128],
                             rr=attnT[:, 1024 * dvc + 512 * th:1024 * dvc + 512 * th + 512],
                             s0=(dvc == 0), s1=(dvc == 15):
                             e.matmul(o_, l, rr, start=s0, stop=s1),
                             inc="PE" if dvc == 15 else None)
                for dvc in range(16):
                    p.do("tensor", lambda e, o_=hi_[:, 896:912],
                         l=wo_t[:, 128 * dvc:128 * dvc + 128],
                         rr=attnT[:, 1024 * dvc:1024 * dvc + 16],
                         s0=(dvc == 0), s1=(dvc == 15):
                         e.matmul(o_, l, rr, start=s0, stop=s1),
                         inc="PE" if dvc == 15 else None)
                wo_pe[dc] = p.val("PE")
                pe_y = p.val("PE")
                p.wait("scalar", "PE", pe_y)
                if dc >= 2:
                    p.wait("scalar", "YD", 16 * (dc - 1))
                p.do("scalar", lambda e, d=y_sb[dc % 2], lo_=lo:
                     e.copy(d[:, :], lo_[:, 0:1024]), inc="SC")
                p.do("scalar", lambda e, hh=hi_, d=dc: e.copy(
                    y_l[:, 16 * d:16 * d + 16], hh[:, 896:912]), inc="SC")
                sc_yev = p.val("SC")
                y_ev[dc] = sc_yev
                p.wait("sync", "SC", sc_yev)
                p.do("sync", lambda e, d=dc, s_=y_sb[dc % 2]: e.dma_start(
                    out=y_scr[128 * d:128 * d + 128, :], in_=s_[:, :]), inc="YD", inc_n=16)
            py_pe = p.val("PE")
            py_sc = p.val("SC")
            py_yd = p.val("YD")

            p.wait("sync", "DV", dv_w)
            p.do("sync", lambda e: e.dma_start(out=mrow_d.unsqueeze(0), in_=wb2[0:1, 0:2048]),
                 inc="LD", inc_n=16)
            p.wait("sync", "LD", p.val("LD"))
            p.do("sync", lambda e: e.dma_start(
                out=tmg[:, :], in_=mrow_d.unsqueeze(0).broadcast_to([128, 2048])),
                inc="LD", inc_n=16)
            ld_bc = p.val("LD")
            p.wait("vector", "LD", ld_bc)
            p.do("vector", lambda e: e.tensor_mul(wb2[:, :], oga[:, :], tmg[:, :]), inc="DV")
            p.wait("vector", "DV", p.val("DV"))
            p.do("vector", lambda e: e.tensor_reduce(
                oga[:, 0:256], wb2[:, :].rearrange("p (r x) -> p x r", r=8),
                axis=AX.X, op=ALU.add), inc="DV")
            p.wait("vector", "DV", p.val("DV"))
            dv_on = p.val("DV")
            p.wait("scalar", "DV", dv_on)
            p.do("scalar", lambda e: e.copy(o_n[:, 0:256], oga[:, 0:256]), inc="SC")
            sc_on = p.val("SC")
            p.wait("vector", "SC", sc_on)
            p.do("vector", lambda e: e.memset(ln[0:1, 12:13], 0.0), inc="DV")
            mg_dv = p.val("DV")
            mg_sc = p.val("SC")

            MERGE_SLOTS = set()
            for _s in range(64):
                lo_b = MM["hid"] + 2048 * _s
                hi_b = lo_b + 2048
                for (r0, r1) in ((MM["mfl"], MM["mfl"] + 32768),
                                 (MM["wb2"], MM["wb2"] + 12288)):
                    if lo_b < r1 and hi_b > r0:
                        MERGE_SLOTS.add(_s)
            slot = [-1] * 64
            freelist = [(s, 0) for s in range(64)]
            chunk_prod = {}
            HSLOT = lambda sid: (hid[:, 1024 * sid:1024 * sid + 1024] if sid < 64
                                 else h0[sid - 64][:, :])
            pe_h1_last = [0]
            bfly_pend = []
            mg_dv_ref = [mg_dv]

            def bfly(pairs):
                for (c1, c2) in pairs:
                    s1, s2 = slot[c1], slot[c2]
                    if len(freelist) >= 2:
                        d1, _ = freelist.pop()
                        d2, _ = freelist.pop()
                        if d1 in MERGE_SLOTS or d2 in MERGE_SLOTS:
                            p.wait("vector", "DV", mg_dv_ref[0])
                    else:
                        d1, d2 = 64, 65
                        p.wait("vector", "PE", pe_h1_last[0])
                    for cch in (c1, c2):
                        eng_, v = chunk_prod[cch]
                        if eng_ == "SC":
                            p.wait("vector", "SC", v)
                    p.do("vector", lambda e, a=HSLOT(s1), b2=HSLOT(s2), d=HSLOT(d1):
                         e.tensor_add(d, a, b2))
                    p.do("vector", lambda e, a=HSLOT(s1), b2=HSLOT(s2), d=HSLOT(d2):
                         e.tensor_sub(d, a, b2), inc="DV")
                    v = p.val("DV")
                    chunk_prod[c1] = ("DV", v)
                    chunk_prod[c2] = ("DV", v)
                    freelist.append((s1, v))
                    freelist.append((s2, v))
                    slot[c1], slot[c2] = d1, d2

            pf_bar_dv = mg_dv
            w13_pe = {}
            pf_sc_sil = {}
            pf_dv_h0 = {}
            pf_pe_h1 = {}
            pf_ev = {}
            pf_pe_up = {}
            for hc in range(65):
                if hc < 64:
                    if hc >= 2:
                        p.wait("gpsimd", "PE", w13_pe[hc - 2])
                        p.do("gpsimd", lambda e, d=w1b[hc % 2], s_=w1_d[hc]:
                             e.dma_start(out=d[:, :], in_=s_), inc="WL", inc_n=16)
                        p.do("gpsimd", lambda e, d=w3b[hc % 2], s_=w3_d[hc]:
                             e.dma_start(out=d[:, :], in_=s_), inc="WL", inc_n=16)
                        wl13 = p.val("WL")
                    else:
                        wl13 = pf_wl[hc]
                    par = hc % 2
                    up_ps, gt_ps = PP[par]
                    w1t, w3t = w1b[par], w3b[par]
                    p.wait("tensor", "WL", wl13)
                    p.wait("tensor", "SC", pf_sc_sil.get(hc - 2, 0))
                    for th in range(2):
                        for dc in range(16):
                            p.do("tensor", lambda e, o_=up_ps[:, 512 * th:512 * th + 512],
                                 l=w1t[:, 128 * dc:128 * dc + 128],
                                 rr=xnT[:, 1024 * dc + 512 * th:1024 * dc + 512 * th + 512],
                                 s0=(dc == 0), s1=(dc == 15):
                                 e.matmul(o_, l, rr, start=s0, stop=s1),
                                 inc="PE" if (dc == 15 and th == 1) else None)
                    pf_pe_up[hc] = p.val("PE")
                    if hc == 0:
                        p.wait("scalar", "PC", 16 * len(pieces))
                if hc >= 1:
                    j = hc - 1
                    parj = j % 2
                    gtj = PP[parj][1]
                    h0j = h0[parj]
                    p.wait("tensor", "DV", pf_dv_h0[j])
                    for th in range(2):
                        p.do("tensor", lambda e, o_=gtj[:, 512 * th:512 * th + 512],
                             rr=h0j[:, 512 * th:512 * th + 512]:
                             e.matmul(o_, h128[:, :], rr, start=True, stop=True),
                             inc="PE" if th == 1 else None)
                    pf_pe_h1[j] = p.val("PE")
                    pe_h1_last[0] = pf_pe_h1[j]
                    sid, dvfree = freelist.pop()
                    slot[j] = sid
                    p.wait("scalar", "DV", dvfree)
                    if sid in MERGE_SLOTS:
                        p.wait("scalar", "DV", mg_dv)
                    p.wait("scalar", "PE", pf_pe_h1[j])
                    p.do("scalar", lambda e, s_=gtj, d=HSLOT(sid): e.activation(
                        d, s_[:, 0:1024], AF.Copy, scale=FWHT_SCALE), inc="SC")
                    chunk_prod[j] = ("SC", p.val("SC"))
                    pf_ev[j] = p.val("SC")
                    for bb in range(5):
                        mask_ = (1 << (bb + 1)) - 1
                        if (j & mask_) == mask_:
                            base = j & ~mask_
                            bfly_pend.extend(
                                (c, c ^ (1 << bb))
                                for c in range(base, base + (1 << (bb + 1)))
                                if not (c >> bb) & 1)
                if hc < 64:
                    p.wait("tensor", "SC", pf_ev.get(hc - 2, 0))
                    for th in range(2):
                        for dc in range(16):
                            p.do("tensor", lambda e, o_=gt_ps[:, 512 * th:512 * th + 512],
                                 l=w3t[:, 128 * dc:128 * dc + 128],
                                 rr=xnT[:, 1024 * dc + 512 * th:1024 * dc + 512 * th + 512],
                                 s0=(dc == 0), s1=(dc == 15):
                                 e.matmul(o_, l, rr, start=s0, stop=s1),
                                 inc="PE" if (dc == 15 and th == 1) else None)
                    w13_pe[hc] = p.val("PE")
                    pe_gate = p.val("PE")
                    p.wait("scalar", "PE", pf_pe_up[hc])
                    p.wait("scalar", "DV", pf_dv_h0.get(hc - 2, 0))
                    slb = sil[par]
                    p.do("scalar", lambda e, s_=up_ps, d=slb: e.activation(
                        d[:, 0:512], s_[:, 0:512], AF.Silu), inc="SC")
                    p.do("scalar", lambda e, s_=up_ps, d=slb: e.activation(
                        d[:, 512:1024], s_[:, 512:1024], AF.Silu), inc="SC")
                    pf_sc_sil[hc] = p.val("SC")
                    p.wait("vector", "SC", pf_sc_sil[hc])
                    p.wait("vector", "PE", pe_gate)
                    h0b = h0[par]
                    p.do("vector", lambda e, s_=gt_ps, d=h0b, sl=slb:
                         e.tensor_mul(d[:, 0:512], s_[:, 0:512], sl[:, 0:512]))
                    p.do("vector", lambda e, s_=gt_ps, d=h0b, sl=slb:
                         e.tensor_mul(d[:, 512:1024], s_[:, 512:1024], sl[:, 512:1024]), inc="DV")
                    pf_dv_h0[hc] = p.val("DV")
                    for _ in range(min(6, len(bfly_pend))):
                        bfly([bfly_pend.pop(0)])
            tail = [(62, 63), (60, 62), (61, 63),
                    (56, 60), (57, 61), (58, 62), (59, 63)]
            for k in range(8):
                tail += [(48 + k, 56 + k), (32 + k, 48 + k), (40 + k, 56 + k),
                         (k, 32 + k), (8 + k, 40 + k), (16 + k, 48 + k),
                         (24 + k, 56 + k)]
            pend_set = set(bfly_pend)
            done_first = [pr for pr in tail if pr in pend_set]
            rest = [pr for pr in bfly_pend if pr not in set(done_first)]
            b5 = [pr for pr in tail if pr not in pend_set]
            for pr in rest:
                bfly([pr])
            for pr in tail:
                if pr in pend_set or pr in b5:
                    bfly([pr])
            bfly_pend.clear()
            pf_dv = p.val("DV")
            pf_sc = p.val("SC")

            w2_pe = {}
            xr_ld = {}
            fadd_dv = {}
            dltsub_dv = {}
            ep_dv = {}
            ep_sc = {}
            pd_pe_f = {}
            pd_pe_yg = {}
            st_of = {}
            pf_last_pe = w13_pe[63]
            b5order = []
            for k in range(8):
                b5order += [k, 32 + k, 8 + k, 40 + k, 16 + k, 48 + k,
                            24 + k, 56 + k]
            for dc in range(17):
                if dc < 16:
                    if dc >= 2:
                        p.wait("gpsimd", "PE", pd_pe_yg[dc - 2])
                        p.wait("gpsimd", "DV", fadd_dv[dc - 2])
                    else:
                        p.wait("gpsimd", "PE", pf_last_pe)
                        p.wait("gpsimd", "DV", pf_dv_h0[63])
                    p.wait("gpsimd", "PC", pc_ready_w2[dc])
                    p.do("gpsimd", lambda e, d=w2b[dc % 2], s_=w2bf_d[dc]:
                         e.dma_start(out=d[:, :], in_=s_), inc="WL", inc_n=16)
                    p.wait("gpsimd", "PC", pc_ready_wo[dc])
                    p.do("gpsimd", lambda e, d=wo2[dc % 2], s_=wobf_d[dc]:
                         e.dma_start(out=d[:, :], in_=s_), inc="WL", inc_n=16)
                    wl2 = p.val("WL")
                    p.wait("gpsimd", "YD", py_yd)
                    p.do("gpsimd", lambda e, d=y2[dc % 2], dcc=dc: e.dma_start(
                        out=d[:, :], in_=y_scr[128 * dcc:128 * dcc + 128, :]), inc="WL", inc_n=16)
                    ld_pd = p.val("WL")
                    if dc < 2:
                        p.wait("sync", "DV", pf_dv_h0[63])
                    else:
                        p.wait("sync", "DV", ep_dv.get(dc - 2, 0))
                    p.do("sync", lambda e, d=xrs[dc % 2], dcc=dc: e.dma_start(
                        out=d[:, :].rearrange("p (t c) -> p t c", t=8),
                        in_=x_kv[256:1280, 128 * dcc:128 * dcc + 128]
                        .rearrange("(t p) c -> p t c", t=8)),
                        inc="LD", inc_n=16)
                    xr_ld[dc] = p.val("LD")
                    par = dc % 2
                    f_ps, aux_ps = PP[par]
                    w2t = w2b[par]
                    p.wait("tensor", "WL", wl2)
                    p.wait("tensor", "DV", fadd_dv.get(dc - 2, 0))
                    if dc == 0:
                        p.wait("tensor", "SC", pf_sc)
                        p.wait("tensor", "DV", mg_dv)
                    hcorder = b5order if dc == 0 else list(range(64))
                    last_dv_waited = 0
                    for th in range(2):
                        for ii, hc in enumerate(hcorder):
                            if dc == 0 and th == 0:
                                v = chunk_prod[hc][1]
                                if v > last_dv_waited:
                                    p.wait("tensor", "DV", v)
                                    last_dv_waited = v
                            p.do("tensor", lambda e, o_=f_ps[:, 512 * th:512 * th + 512],
                                 l=w2t[:, 128 * hc:128 * hc + 128],
                                 rr=HSLOT(slot[hc])[:, 512 * th:512 * th + 512],
                                 s0=(ii == 0), s1=(ii == 63):
                                 e.matmul(o_, l, rr, start=s0, stop=s1),
                                 inc="PE" if (ii == 63 and th == 1) else None)
                    w2_pe[dc] = p.val("PE")
                    pd_pe_f[dc] = p.val("PE")
                    p.wait("tensor", "DV", ep_dv.get(dc - 2, 0))
                    p.wait("tensor", "SC", ep_sc.get(dc - 2, 0))
                    wo_t = wo2[par]
                    for dvc in range(16):
                        p.do("tensor", lambda e, o_=aux_ps[:, 896:912],
                             l=wo_t[:, 128 * dvc:128 * dvc + 128],
                             rr=o_n[:, 16 * dvc:16 * dvc + 16],
                             s0=(dvc == 0), s1=(dvc == 15):
                             e.matmul(o_, l, rr, start=s0, stop=s1),
                             inc="PE" if dvc == 15 else None)
                    pd_pe_yg[dc] = p.val("PE")
                    p.wait("vector", "PE", pd_pe_f[dc])
                    p.wait("vector", "WL", ld_pd)
                    p.wait("vector", "ST", st_of.get((dc - 2, "lo"), 0))
                    p.wait("vector", "S2", st_of.get((dc - 2, "hi"), 0))
                    fs = fsb[par]
                    p.do("vector", lambda e, s_=f_ps, d=fs, y_=y2[par]:
                         e.tensor_add(d[:, 0:512], s_[:, 0:512], y_[:, 0:512]))
                    p.do("vector", lambda e, s_=f_ps, d=fs, y_=y2[par]:
                         e.tensor_add(d[:, 512:1024], s_[:, 512:1024], y_[:, 512:1024]), inc="DV")
                    fadd_dv[dc] = p.val("DV")
                    p.wait("vector", "PE", pd_pe_yg[dc])
                    p.do("vector", lambda e, s_=aux_ps, dcc=dc: e.tensor_sub(
                        dlt[:, 16 * dcc:16 * dcc + 16], s_[:, 896:912],
                        y_l[:, 16 * dcc:16 * dcc + 16]), inc="DV")
                    dltsub_dv[dc] = p.val("DV")
                if dc >= 1:
                    j = dc - 1
                    parj = j % 2
                    auxj = PP[parj][1]
                    fsj = fsb[parj]
                    p.wait("tensor", "DV", dltsub_dv[j])
                    p.do("tensor", lambda e, o_=auxj[0:16, 256:384], dcc=j: e.transpose(
                        o_, dlt[:, 16 * dcc:16 * dcc + 16], idf[:, :]), inc="PE")
                    pe_dt = p.val("PE")
                    p.wait("scalar", "PE", pe_dt)
                    p.do("scalar", lambda e, d=dltTc[parj], s_=auxj: e.copy(
                        d[0:16, :], s_[0:16, 256:384]), inc="SC")
                    sc_dt = p.val("SC")
                    p.wait("tensor", "DV", fadd_dv[j])
                    p.wait("tensor", "SC", sc_dt)
                    for tt in range(8):
                        tp = auxj[:, 0:128] if tt % 2 == 0 else auxj[:, 512:640]
                        if tt >= 2:
                            p.wait("tensor", "DV", ep_dv.get((j, tt - 2), 0))
                        p.do("tensor", lambda e, o_=tp, s_=fsj[:, 128 * tt:128 * tt + 128]:
                             e.transpose(o_, s_, idf[:, :]), inc="PE")
                        pe_tp = p.val("PE")
                        p.wait("vector", "PE", pe_tp)
                        p.wait("vector", "LD", xr_ld[j])
                        p.do("vector", lambda e, o_=fsj[:, 128 * tt:128 * tt + 128], s_=tp,
                             x_=xrs[parj][:, 128 * tt:128 * tt + 128]:
                             e.tensor_add(o_, s_, x_), inc="DV")
                        dv_o = p.val("DV")
                        ep_dv[(j, tt)] = dv_o
                        if tt == 0:
                            if j >= 2:
                                p.wait("vector", "ST", st_of.get(("fix", j - 2), 0))
                            p.do("vector", lambda e, o_=fsj, d=fixc[j % 2], dt_=dltTc[parj]:
                                 e.tensor_add(d[0:16, :], o_[0:16, 0:128], dt_[0:16, :]), inc="DV")
                            dv_fx = p.val("DV")
                            p.wait("scalar", "DV", dv_fx)
                            p.do("scalar", lambda e, d=fixc[j % 2], dcc=j: e.dma_start(
                                out=o_fix[0:16, 128 * dcc:128 * dcc + 128], in_=d[0:16, :]),
                                inc="ST", inc_n=16)
                            st_of[("fix", j)] = p.val("ST")
                        if tt == 3:
                            p.wait("scalar", "DV", dv_o)
                            p.do("scalar", lambda e, o_=fsj, dcc=j: e.dma_start(
                                out=o_main[0:512, 128 * dcc:128 * dcc + 128]
                                .rearrange("(t p) c -> p t c", t=4),
                                in_=o_[:, 0:512].rearrange("p (t c) -> p t c", t=4)),
                                inc="ST", inc_n=16)
                            st_of[(j, "lo")] = p.val("ST")
                        if tt == 7:
                            p.wait("sync", "DV", dv_o)
                            p.do("sync", lambda e, o_=fsj, dcc=j: e.dma_start(
                                out=o_main[512:1024, 128 * dcc:128 * dcc + 128]
                                .rearrange("(t p) c -> p t c", t=4),
                                in_=o_[:, 512:1024].rearrange("p (t c) -> p t c", t=4)),
                                inc="S2", inc_n=16)
                            st_of[(j, "hi")] = p.val("S2")
                    ep_dv[j] = p.val("DV")
                    ep_sc[j] = p.val("SC")

            st_total = p.val("ST")
            s2_total = p.val("S2")
            p.wait("scalar", "ST", st_total)
            p.wait("scalar", "S2", s2_total)
            p.wait("gpsimd", "ST", st_total)
            p.wait("sync", "S2", s2_total)
            p.wait("sync", "ST", st_total)

        for role, reg in (("sync", block.sync), ("gpsimd", block.gpsimd),
                          ("tensor", block.tensor), ("scalar", block.scalar),
                          ("vector", block.vector)):
            reg(lambda eng, role=role: build(P(role, eng, sems)))

    return nc


_NC_CACHE = None


def kernel(L0_attn_hidden_states, L0_attn_Wq, L0_attn_Wk, L0_attn_Wv,
           L0_attn_Wo, L0_ffn_W1, L0_ffn_W2, L0_ffn_W3):
    global _NC_CACHE
    in_maps = _prep_inputs(L0_attn_hidden_states, L0_attn_Wq, L0_attn_Wk,
                           L0_attn_Wv, L0_attn_Wo, L0_ffn_W1, L0_ffn_W2,
                           L0_ffn_W3)
    if _NC_CACHE is None:
        _NC_CACHE = build_program()
    res = run_bass_kernel_spmd(_NC_CACHE, in_maps, core_ids=list(range(NCORES)))
    out = np.empty((S, D), np.float32)
    for c in range(NCORES):
        out[c * TOK:(c + 1) * TOK] = res.results[c]["o_main"]
    out[0:16] = res.results[0]["o_fix"]
    return out.reshape(1, S, D)
